# version 06
# baseline (speedup 1.0000x reference)
import numpy as np
import concourse.bass as bass
import concourse.mybir as mybir
from concourse.bass_utils import run_bass_kernel_spmd
from contextlib import ExitStack

F32 = mybir.dt.float32
AX = mybir.AxisListType
ALU = mybir.AluOpType
ACTF = mybir.ActivationFunctionType

ENGINES = ("sync", "tensor", "vector", "scalar", "gpsimd")

D = 1024
NCH = 8
TT = 512
HALO = 128
HB = 64
MEM = 40
NPTOT = 2048
EPS = 1e-5
MLN32 = -float(np.log(32.0))
ARN = 48256


class Prog:
    def __init__(self):
        self.ops = {e: [] for e in ENGINES}
        self.count = {}
        self.writer = {}
        self.psum_free = {}
        self.last_inc = {}
        self.dirty = {e: False for e in ENGINES}

    def sem(self, name):
        self.count.setdefault(name, 0)
        return name

    def at(self, name):
        return self.count.get(name, 0)

    def op(self, eng, fn, wait=(), inc=None, inc_amt=1):
        w = [(self.sem(s), v) for (s, v) in wait if v and v > 0]
        post = None
        incs = []
        if inc is not None:
            self.sem(inc)
            assert self.writer.setdefault(inc, eng) == eng, (inc, eng)
            self.count[inc] += inc_amt
            post = self.count[inc]
            incs.append((inc, inc_amt))
            self.last_inc[eng] = (inc, post)
        self.ops[eng].append([fn, w, incs])
        self.dirty[eng] = True
        return post

    def dma(self, eng, out, in_, sem, wait=()):
        return self.op(eng, lambda e, o=out, i=in_: e.dma_start(out=o, in_=i),
                       wait=wait, inc=sem, inc_amt=16)

    def barrier(self):
        vals = {}
        for e in ENGINES:
            eos = self.sem(f"eos_{e}")
            assert self.writer.setdefault(eos, e) == e
            if self.dirty[e] and self.ops[e]:
                last = self.ops[e][-1]
                if not last[2]:
                    self.count[eos] += 1
                    last[2].append((eos, 1))
                    self.last_inc[e] = (eos, self.count[eos])
                waits = [(s, self.count[s]) for s, w in self.writer.items()
                         if w == e and s != eos and self.count.get(s, 0) > 0]
                vals[e] = self.op(e, lambda en: en.nop(), wait=waits, inc=eos)
            else:
                vals[e] = self.op(e, lambda en: en.nop(), inc=eos)
        for e in ENGINES:
            self.op(e, lambda en: en.nop(),
                    wait=[(f"eos_{o}", vals[o]) for o in ENGINES if o != e])
        for e in ENGINES:
            self.dirty[e] = False

    def guard(self, bank):
        w = self.psum_free.get(bank)
        return [w] if w else []

    def release(self, bank, sem, val):
        self.psum_free[bank] = (sem, val)

    def mm_group(self, bank, bank_ap, mms, wait=()):
        n = len(mms)
        te = None
        for i, (lh, rh) in enumerate(mms):
            w = (list(wait) + self.guard(bank)) if i == 0 else []
            te = self.op("tensor",
                         lambda t, lh=lh, rh=rh, i=i, n=n, b=bank_ap: t.matmul(
                             b, lh, rh, start=(i == 0), stop=(i == n - 1)),
                         wait=w, inc="te" if i == n - 1 else None)
        return te

    def emit(self, block, sems):
        def run(eng_obj, eng_name):
            for fn, waits, incs in self.ops[eng_name]:
                for s, v in waits:
                    eng_obj.wait_ge(sems[s], v)
                r = fn(eng_obj)
                for s, amt in incs:
                    r.then_inc(sems[s], amt)
        block.sync(lambda e: run(e, "sync"))
        block.tensor(lambda e: run(e, "tensor"))
        block.vector(lambda e: run(e, "vector"))
        block.scalar(lambda e: run(e, "scalar"))
        block.gpsimd(lambda e: run(e, "gpsimd"))


def build_nc(ncores=8, tloc=4096, dbg=False, stop_stage=99):
    NT = tloc // TT
    NPC = NPTOT // 128
    NPL = NPTOT // ncores
    NPLC = NPL // 128
    NEL = 64 // ncores

    nc = bass.Bass("TRN2", target_bir_lowering=False,
                   detect_race_conditions=False)
    dt = nc.dram_tensor

    x_in = dt("x", [HALO + tloc, D], F32, kind="ExternalInput").ap()
    w1 = dt("w1", [D, D], F32, kind="ExternalInput").ap()
    b1 = dt("b1", [D], F32, kind="ExternalInput").ap()
    w2 = dt("w2", [D, D], F32, kind="ExternalInput").ap()
    b2 = dt("b2", [1, D], F32, kind="ExternalInput").ap()
    mw = dt("mw", [MEM + 1, D], F32, kind="ExternalInput").ap()
    wrT = dt("wrT", [D, D], F32, kind="ExternalInput").ap()
    wkT = dt("wkT", [D, D], F32, kind="ExternalInput").ap()
    wvT = dt("wvT", [D, D], F32, kind="ExternalInput").ap()
    mu_r = dt("mu_r", [D], F32, kind="ExternalInput").ap()
    mu_k = dt("mu_k", [D], F32, kind="ExternalInput").ap()
    wqTa = dt("wqTa", [D, D], F32, kind="ExternalInput").ap()
    wkTa = dt("wkTa", [D, D], F32, kind="ExternalInput").ap()
    wvTa = dt("wvTa", [D, D], F32, kind="ExternalInput").ap()
    woTa = dt("woTa", [D, D], F32, kind="ExternalInput").ap()
    dlogit = dt("dlogit", [1], F32, kind="ExternalInput").ap()
    oscale = dt("oscale", [1], F32, kind="ExternalInput").ap()
    phiT = dt("phiT", [D, NPTOT], F32, kind="ExternalInput").ap()
    phib = dt("phib", [1, NPTOT], F32, kind="ExternalInput").ap()
    ew = dt("ew", [NEL, D, D], F32, kind="ExternalInput").ap()
    eb = dt("eb", [1, NEL * D], F32, kind="ExternalInput").ap()
    ident = dt("ident", [128, 128], F32, kind="ExternalInput").ap()
    expo512 = dt("expo512", [512, 512], F32, kind="ExternalInput").ap()
    arange_rep = dt("arange_rep", [128, tloc], F32, kind="ExternalInput").ap()
    ecol = dt("ecol", [128, 4], F32, kind="ExternalInput").ap()
    hmask = dt("hmask", [1, HALO], F32, kind="ExternalInput").ap()
    hflag = dt("hflag", [1], F32, kind="ExternalInput").ap()
    selpair = dt("selpair", [1, ncores], F32, kind="ExternalInput").ap()
    selown = dt("selown", [1, ncores], F32, kind="ExternalInput").ap()

    ik = "ExternalOutput" if dbg else "Internal"
    if dbg:
        dbg_ln = dt("dbg_ln", [128, 4 * D], F32, kind="ExternalOutput").ap()
        dbg_fm = dt("dbg_fm", [128, 4096], F32, kind="ExternalOutput").ap()
        dbg_sc = dt("dbg_sc", [128, 2048], F32, kind="ExternalOutput").ap()
        dbg_qt = dt("dbg_qt", [128, 4096], F32, kind="ExternalOutput").ap()
        dbg_wi = dt("dbg_wi", [128, 2048], F32, kind="ExternalOutput").ap()
        dbg_ss = dt("dbg_ss", [128, 8192], F32, kind="ExternalOutput").ap()
        dbg_ao = dt("dbg_ao", [128, 4096], F32, kind="ExternalOutput").ap()
        dbg_wo = dt("dbg_wo", [128, 8192], F32, kind="ExternalOutput").ap()
    out = dt("out", [tloc, D], F32, kind="ExternalOutput").ap()
    p_scr = dt("p_scr", [D, HALO + tloc], F32, kind=ik).ap()
    r1_scr = dt("r1_scr", [HALO + tloc, D], F32, kind=ik).ap()
    r2_scr = dt("r2_scr", [tloc, D], F32, kind=ik).ap()
    q_scr = dt("q_scr", [D, tloc], F32, kind=ik).ap()
    k_scr = dt("k_scr", [D, tloc], F32, kind=ik).ap()
    v_scr = dt("v_scr", [tloc, D], F32, kind=ik).ap()
    ret_scr = dt("ret_scr", [tloc, D], F32, kind=ik).ap()
    r3_scr = dt("r3_scr", [tloc, D], F32, kind=ik).ap()
    d_scr = dt("d_scr", [tloc, NPTOT], F32, kind=ik).ap()
    c_scr = dt("c_scr", [NPTOT, tloc], F32, kind=ik).ap()
    ar_in = dt("ar_in", [128, 8 * ncores], F32, kind="Internal").ap()
    ar_out = dt("ar_out", [128, 8 * ncores], F32, kind="Internal").ap()
    sag_in = dt("sag_in", [D, D], F32, kind="Internal").ap()
    sag_out = dt("sag_out", [2 * D, D], F32, kind="Internal").ap()
    xt_in = dt("xt_in", [NPTOT, D], F32, kind="Internal").ap()
    xt_out = dt("xt_out", [NPL, D], F32, kind="Internal").ap()
    yt_in = dt("yt_in", [NPL, D], F32, kind="Internal").ap()
    yt_out = dt("yt_out", [NPTOT, D], F32, kind="Internal").ap()

    rg_all = [list(range(ncores))]
    rg_pair = [[2 * i, 2 * i + 1] for i in range(ncores // 2)]

    es = ExitStack()
    AR = es.enter_context(nc.sbuf_tensor("AR", [128, ARN], F32))
    SM = es.enter_context(nc.sbuf_tensor("SMALL", [128, 4096], F32))
    PS = [es.enter_context(nc.psum_tensor(f"ps{i}", [128, 512], F32))
          for i in range(8)]
    sem_names = ["ld", "ldw", "ldw2", "ldxa", "ldxb", "ldya", "ldyb", "ldz",
                 "te", "ve", "se", "sevac", "vevac", "gp",
                 "sta", "stb", "stc", "sto", "cc", "ldg"] + [f"eos_{e}" for e in ENGINES]
    sems = {n: es.enter_context(nc.semaphore(n)) for n in sem_names}
    block = es.enter_context(nc.Block())

    def A(off, n, p0=0, p1=128):
        assert 0 <= off and off + n <= ARN, (off, n)
        return AR[p0:p1, off:off + n]

    P = Prog()

    ones_c = SM[:, 0:1]
    lng_c = SM[:, 1:2]
    g512_c = SM[:, 2:3]
    hflag_c = SM[:, 3:4]
    osc_c = SM[:, 4:5]
    hfi_c = SM[:, 12:13]
    kscale = SM[:, 8:12]
    b1_c = SM[:, 16:24]
    mw0_c = SM[:, 24:32]
    sump = SM[:, 32:40]
    corr = SM[:, 48:56]
    mur_c = SM[:, 56:64]
    murI_c = SM[:, 64:72]
    muk_c = SM[:, 72:80]
    mukI_c = SM[:, 80:88]
    selpair_r = SM[:, 88:88 + ncores]
    selown_r = SM[:, 96:96 + ncores]
    prevcol = SM[:, 104:112]
    KT = SM[:, 128:128 + NCH * MEM]
    STT = SM[:, 448:448 + 64]
    SCR1 = SM[:, 512:513]
    SCR2 = SM[:, 513:514]
    SCR3 = SM[:, 514:515]
    SD64 = SM[:, 516:580]
    SD32 = SM[:, 580:612]
    onesrow = SM[0:1, 640:768]
    cells = SM[0:1, 768:784]
    selown_cl = SM[0:1, 784:784 + ncores]
    selpair_cl = SM[0:1, 784 + ncores:784 + 2 * ncores]
    hmrow = SM[0:1, 884:1012]
    b2row = SM[0:1, 1024:2048]
    phibrow = SM[0:1, 2048:4096]

    IDT = A(47104, 128)
    LNDUMP = A(47232, 1024)
    CEPS = SM[:, 5:6]
    CLN32 = SM[:, 6:7]
    nc.const_aps.register(F32, EPS, CEPS)
    nc.const_aps.register(F32, MLN32, CLN32)

    P.dma("sync", IDT, ident[:, :], "ld")
    STG1 = A(8192, 1024)
    STG2 = A(9216, 1024)
    P.dma("sync", STG1[0:MEM, :], mw[1:, :], "ld")
    P.dma("sync", STG2[0:1, :], b1.unsqueeze(0), "ld")
    P.dma("sync", STG2[1:2, :], mw[0:1, :], "ld")
    P.dma("sync", STG2[2:3, :], mu_r.unsqueeze(0), "ld")
    P.dma("sync", STG2[3:4, :], mu_k.unsqueeze(0), "ld")
    P.dma("sync", kscale, ecol[:, :], "ld")
    P.dma("sync", cells[:, 0:1], dlogit.unsqueeze(0), "ld")
    P.dma("sync", cells[:, 1:2], oscale.unsqueeze(0), "ld")
    P.dma("sync", cells[:, 2:3], hflag.unsqueeze(0), "ld")
    P.dma("sync", selown_cl, selown[:, :], "ld")
    P.dma("sync", selpair_cl, selpair[:, :], "ld")
    P.dma("sync", b2row, b2[:, :], "ld")
    P.dma("sync", phibrow, phib[:, :], "ld")
    ldv = P.at("ld")

    P.op("vector", lambda v: v.memset(ones_c, 1.0), inc="ve")
    P.op("vector", lambda v: v.memset(CEPS, EPS), inc="ve")
    P.op("vector", lambda v: v.memset(CLN32, MLN32), inc="ve")
    vones = P.op("vector", lambda v: v.memset(onesrow, 1.0), inc="ve")
    sev = P.op("scalar", lambda s: s.activation(cells[:, 3:4], cells[:, 0:1],
                                                ACTF.Sigmoid),
               wait=[("ld", ldv)], inc="se")
    sev = P.op("scalar", lambda s: s.activation(cells[:, 4:5], cells[:, 3:4],
                                                ACTF.Ln),
               wait=[("se", sev)], inc="se")
    te = P.mm_group(0, PS[0][:, 0:5], [(onesrow, cells[0:1, 0:5])],
                    wait=[("se", sev), ("ve", vones)])
    P.op("vector", lambda v: v.tensor_copy(osc_c, PS[0][:, 1:2]),
         wait=[("te", te)], inc="vevac")
    P.op("vector", lambda v: v.tensor_copy(hflag_c, PS[0][:, 2:3]), inc="vevac")
    ev = P.op("vector", lambda v: v.tensor_copy(lng_c, PS[0][:, 4:5]), inc="vevac")
    P.release(0, "vevac", ev)
    te = P.mm_group(1, PS[1][:, 0:2 * ncores],
                    [(onesrow, SM[0:1, 784:784 + 2 * ncores])],
                    wait=[("ve", vones), ("ld", ldv)])
    ev = P.op("vector", lambda v: v.tensor_copy(
        SM[:, 96:96 + 2 * ncores], PS[1][:, 0:2 * ncores]),
        wait=[("te", te)], inc="vevac")
    P.release(1, "vevac", ev)
    ev = P.op("vector", lambda v: v.tensor_copy(
        selpair_r, SM[:, 96 + ncores:96 + 2 * ncores]), inc="vevac")
    sev = P.op("scalar", lambda s: s.activation(g512_c, lng_c, ACTF.Exp,
                                                scale=512.0),
               wait=[("vevac", ev)], inc="se")
    sev = P.op("scalar", lambda s: s.activation(kscale, kscale, ACTF.Exp,
                                                scale=lng_c), inc="se")
    for c in range(NCH):
        b = 2 + (c % 2)
        te = P.op("tensor", lambda t, c=c, b=b: t.transpose(
            PS[b][:, 0:MEM], STG1[0:MEM, c * 128:(c + 1) * 128], IDT[0:MEM, 0:MEM]),
            wait=([("ld", ldv)] if c == 0 else []) + P.guard(b), inc="te")
        ev = P.op("vector", lambda v, c=c, b=b: v.tensor_copy(
            KT[:, c * MEM:(c + 1) * MEM], PS[b][:, 0:MEM]),
            wait=[("te", te)], inc="vevac")
        P.release(b, "vevac", ev)
        b = 4 + (c % 2)
        te = P.op("tensor", lambda t, c=c, b=b: t.transpose(
            PS[b][:, 0:4], STG2[0:4, c * 128:(c + 1) * 128], IDT[0:4, 0:4]),
            wait=P.guard(b), inc="te")
        for dst_i, dstc in enumerate((b1_c, mw0_c, mur_c, muk_c)):
            ev = P.op("vector", lambda v, c=c, b=b, i=dst_i, d=dstc:
                      v.tensor_copy(d[:, c:c + 1], PS[b][:, i:i + 1]),
                      wait=[("te", te)] if dst_i == 0 else (), inc="vevac")
        P.release(b, "vevac", ev)
    for c in range(NCH):
        P.op("vector", lambda v, c=c: v.tensor_scalar(
            KT[:, c * MEM:(c + 1) * MEM], KT[:, c * MEM:(c + 1) * MEM],
            mw0_c[:, c:c + 1], None, ALU.subtract),
            wait=[("vevac", P.at("vevac"))] if c == 0 else (), inc="ve")
    P.op("vector", lambda v: v.tensor_scalar(murI_c, mur_c, -1.0, 1.0,
                                             ALU.mult, ALU.add),
         wait=[("vevac", P.at("vevac"))], inc="ve")
    P.op("vector", lambda v: v.tensor_scalar(mukI_c, muk_c, -1.0, 1.0,
                                             ALU.mult, ALU.add), inc="ve")
    P.op("vector", lambda v: v.tensor_scalar(hfi_c, hflag_c, -1.0, 1.0,
                                             ALU.mult, ALU.add), inc="ve")
    P.op("vector", lambda v: v.memset(sump, 0.0), inc="ve")
    P.barrier()

    def ln_chunk(src, dst, slot, ldwait=(), pr=128, wait2=()):
        st = STT[0:pr, slot * 4: slot * 4 + 4]
        mean, var, rstd, nmr = (st[:, 0:1], st[:, 1:2], st[:, 2:3], st[:, 3:4])
        tA = SCR1[0:pr]
        tB = SCR2[0:pr]
        def vw(fn):
            return P.op("vector", fn, wait=[("ve", P.at("ve"))], inc="ve")
        P.op("vector", lambda v: v.reduce_sum(mean, src, axis=AX.X),
             wait=ldwait, inc="ve")
        sev = P.op("scalar", lambda s: s.activation(LNDUMP[0:pr, :], src,
                                                    ACTF.Square),
                   wait=ldwait, inc="se")
        P.op("vector", lambda v: v.reduce_sum(var, LNDUMP[0:pr, :], axis=AX.X),
             wait=[("se", sev)], inc="ve")
        vw(lambda v: v.tensor_scalar(var, var, 1.0 / D, None, ALU.mult))
        vw(lambda v: v.tensor_scalar(mean, mean, 1.0 / D, None, ALU.mult))
        vw(lambda v: v.tensor_mul(tA, mean, mean))
        vev = vw(lambda v: v.tensor_sub(var, var, tA))
        sev = P.op("scalar", lambda s: s.activation(rstd, var, ACTF.Sqrt,
                                                    bias=EPS),
                   wait=[("ve", vev)], inc="se")
        P.op("vector", lambda v: v.reciprocal(rstd, rstd),
             wait=[("se", sev)], inc="ve")
        vw(lambda v: v.tensor_mul(nmr, mean, rstd))
        vev = vw(lambda v: v.tensor_scalar(nmr, nmr, -1.0, None, ALU.mult))
        sev = P.op("scalar", lambda s: s.activation(dst, src, ACTF.Identity,
                                                    bias=nmr, scale=rstd),
                   wait=[("ve", vev)] + list(wait2), inc="se")
        return sev

    def tr_tile(src_off, dst_off, rows, colw, wait=(), banks=(2, 3),
                scale=None):
        last = None
        for c in range(NCH):
            for r in range(rows):
                b = banks[(c * rows + r) % len(banks)]
                w = list(wait) if (c == 0 and r == 0) else []
                te = P.op("tensor", lambda t, c=c, r=r, b=b: t.transpose(
                    PS[b][:, 0:128], A(src_off + r * D + c * 128, 128), IDT),
                    wait=w + P.guard(b), inc="te")
                if scale is None:
                    last = P.op("vector",
                                lambda v, c=c, r=r, b=b: v.tensor_copy(
                                    A(dst_off + c * colw + r * 128, 128),
                                    PS[b][:, 0:128]),
                                wait=[("te", te)], inc="vevac")
                    P.release(b, "vevac", last)
                else:
                    last = P.op("scalar",
                                lambda s, c=c, r=r, b=b: s.activation(
                                    A(dst_off + c * colw + r * 128, 128),
                                    PS[b][:, 0:128], ACTF.Identity,
                                    scale=scale(r)),
                                wait=[("te", te)], inc="sevac")
                    P.release(b, "sevac", last)
        return last

    W1S, XT0, XT1, LNT0, LNT1, LNFM, PFM, HMR = (
        0, 8192, 12288, 16384, 20480, 24576, 28672, 32768)
    XTb, LNTb = [XT0, XT1], [LNT0, LNT1]
    ldwv = P.dma("sync", A(W1S, 8 * D).rearrange("p (c e) -> p c e", c=NCH),
                 w1.rearrange("(c p) e -> p c e", p=128), "ldw")
    P.dma("sync", hmrow, hmask[:, :], "ld")
    te = P.mm_group(2, PS[2][:, 0:HALO], [(onesrow, hmrow)],
                    wait=[("ve", vones), ("ld", P.at("ld"))])
    ev = P.op("vector", lambda v: v.tensor_copy(A(HMR, HALO), PS[2][:, 0:HALO]),
              wait=[("te", te)], inc="vevac")
    P.release(2, "vevac", ev)

    xfree = [0, 0]
    s1a_stgate = [0]
    for it in range(NT + 1):
        par = it % 2
        rows = 1 if it == 0 else 4
        roff = 0 if it == 0 else HALO + (it - 1) * TT
        xoff, lnoff = XTb[par], LNTb[par]
        ldx = "ldxa" if par == 0 else "ldxb"
        P.dma("sync", A(xoff, rows * D).rearrange("p (r d) -> p r d", r=rows),
              x_in.rearrange("(r p) d -> p r d", p=128)[
                  :, roff // 128:roff // 128 + rows, :],
              ldx, wait=[("se", xfree[par])])
        lw = [(ldx, P.at(ldx))]
        se_ln = 0
        for r in range(rows):
            se_ln = ln_chunk(A(xoff + r * D, D), A(lnoff + r * D, D), r,
                             ldwait=lw if r == 0 else ())
        xfree[par] = se_ln
        colw = rows * 128
        tr_tile(lnoff, LNFM, rows, colw, wait=[("se", se_ln)])
        vtr = P.at("vevac")
        for m in range(NCH):
            bank = 4 + (m % 2)
            te = P.mm_group(bank, PS[bank][:, 0:colw],
                            [(A(W1S + k * D + m * 128, 128),
                              A(LNFM + k * colw, colw)) for k in range(NCH)],
                            wait=[("vevac", vtr), ("ldw", ldwv)])
            ev = P.op("scalar", lambda s, m=m, bank=bank, colw=colw:
                      s.activation(A(PFM + m * colw, colw),
                                   PS[bank][:, 0:colw], ACTF.Identity,
                                   bias=b1_c[:, m:m + 1]),
                      wait=[("te", te)] +
                           ([("sta", s1a_stgate[0])] if m == 0 else []),
                      inc="sevac")
            P.release(bank, "sevac", ev)
        sev_evac = P.at("sevac")
        if it == 0:
            vv = 0
            for m in range(NCH):
                vv = P.op("vector", lambda v, m=m: v.tensor_mul(
                    A(PFM + m * 128, 128), A(PFM + m * 128, 128),
                    A(HMR, HALO)),
                    wait=[("sevac", sev_evac)] if m == 0 else (), inc="ve")
            st = P.dma("gpsimd",
                       p_scr.rearrange("(c p) t -> p c t", p=128)[:, :, 0:HALO],
                       A(PFM, NCH * 128).rearrange("p (c t) -> p c t", c=NCH),
                       "sta", wait=[("ve", vv)])
        else:
            for m in range(NCH):
                P.op("vector", lambda v, m=m, colw=colw: v.reduce_sum(
                    SCR3, A(PFM + m * colw, colw), axis=AX.X),
                    wait=[("sevac", sev_evac)] if m == 0 else (), inc="ve")
                P.op("vector", lambda v, m=m: v.tensor_add(
                    sump[:, m:m + 1], sump[:, m:m + 1], SCR3),
                    wait=[("ve", P.at("ve"))], inc="ve")
            st = P.dma("gpsimd",
                       p_scr.rearrange("(c p) t -> p c t", p=128)[
                           :, :, roff:roff + colw],
                       A(PFM, NCH * colw).rearrange("p (c t) -> p c t", c=NCH),
                       "sta", wait=[("ve", P.at("ve"))])
        s1a_stgate[0] = st

    if dbg:
        P.dma("gpsimd", dbg_ln[:, :], A(LNTb[NT % 2], 4 * D), "stb",
              wait=[("se", P.at("se"))])
        P.dma("gpsimd", dbg_fm[:, :], A(LNFM, 4096), "stb",
              wait=[("vevac", P.at("vevac"))])
    Z = A(33024, 8 * ncores)
    vv = P.op("vector", lambda v: v.memset(Z, 0.0), inc="ve")
    for c8 in range(ncores):
        vv = P.op("vector", lambda v, c8=c8: v.scalar_tensor_tensor(
            Z[:, 8 * c8:8 * c8 + 8], sump, selown_r[:, c8:c8 + 1],
            Z[:, 8 * c8:8 * c8 + 8], ALU.mult, ALU.add),
            wait=[("ve", P.at("ve"))] if c8 == 0 else (), inc="ve")
    st = P.dma("gpsimd", ar_in[:, :], Z, "stb", wait=[("ve", vv)])
    ccv = P.op("gpsimd", lambda g: g.collective_compute(
        "AllReduce", ALU.add, replica_groups=rg_all,
        ins=[ar_in[:, :]], outs=[ar_out[:, :]]),
        wait=[("stb", st)], inc="cc")
    lz = P.dma("gpsimd", Z, ar_out[:, :], "ldg", wait=[("cc", ccv)])
    vv = P.op("vector", lambda v: v.memset(corr, 0.0),
              wait=[("ldg", lz)], inc="ve")
    for c8 in range(ncores):
        vv = P.op("vector", lambda v, c8=c8: v.scalar_tensor_tensor(
            corr, Z[:, 8 * c8:8 * c8 + 8], selpair_r[:, c8:c8 + 1], corr,
            ALU.mult, ALU.add), wait=[("ve", P.at("ve"))], inc="ve")
    vv = P.op("vector", lambda v: v.tensor_mul(corr, corr, mw0_c),
              wait=[("ve", P.at("ve"))], inc="ve")
    P.barrier()

    if stop_stage < 1:
        P.emit(block, sems)
        es.close()
        return nc

    W2S = 0
    PT = 8192
    QFM = 13312
    XTq0, XTq1 = 17408, 21504
    AOT = 25600
    R1T = 29696
    ldwv = P.dma("sync", A(W2S, 8 * D).rearrange("p (c e) -> p c e", c=NCH),
                 w2.rearrange("(c p) e -> p c e", p=128), "ldw")

    def s1b_tile(it):
        if it < 0:
            cw, roff = HB, HALO - HB
        else:
            cw, roff = TT, HALO + it * TT
        hw = cw + MEM + 1
        par = it % 2
        ldx = "ldxa" if par == 0 else "ldxb"
        xoff = XTq0 if par == 0 else XTq1
        rows = cw // 128 if cw >= 128 else 1
        te_prev = P.at("te")
        P.dma("sync", A(PT, NCH * 560).rearrange("p (c w) -> p c w", w=560)[
            :, :, 0:hw],
            p_scr.rearrange("(c p) t -> p c t", p=128)[
                :, :, roff - MEM - 1:roff + cw],
            "ldya", wait=[("te", s1b_tfree[0]), ("ve", s1b_vfree[0])])
        lpy = P.at("ldya")
        xw = [("se", s1b_xfree[par]), ("ve", s1b_xvfree[par])]
        if cw < 128:
            P.dma("sync", A(xoff, D, 0, cw), x_in[roff:roff + cw, :],
                  ldx, wait=xw)
        else:
            P.dma("sync", A(xoff, rows * D).rearrange("p (r d) -> p r d", r=rows),
                  x_in.rearrange("(r p) d -> p r d", p=128)[
                      :, roff // 128:roff // 128 + rows, :],
                  ldx, wait=xw)
        lpx = P.at(ldx)
        vv = 0
        for c in range(NCH):
            base = PT + c * 560
            P.op("vector", lambda v, c=c, base=base: v.tensor_scalar(
                A(QFM + c * cw, cw), A(base + MEM + 1, cw),
                KT[:, c * MEM:c * MEM + 1], None, ALU.mult),
                wait=[("ldya", lpy)] if c == 0 else (), inc="ve")
            for j in range(1, MEM):
                P.op("vector", lambda v, c=c, j=j, base=base:
                     v.scalar_tensor_tensor(
                         A(QFM + c * cw, cw), A(base + MEM + 1 - j, cw),
                         KT[:, c * MEM + j:c * MEM + j + 1],
                         A(QFM + c * cw, cw), ALU.mult, ALU.add), inc="ve")
            vv = P.op("vector", lambda v, c=c: v.tensor_scalar(
                A(QFM + c * cw, cw), A(QFM + c * cw, cw),
                corr[:, c:c + 1], None, ALU.add), inc="ve")
        s1b_vfree[0] = vv
        nts = max(1, cw // 128)
        sev_all = 0
        for ts in range(nts):
            pr = min(cw, 128)
            for b2i in range(2):
                bank = 4 + ((ts * 2 + b2i) % 2)
                mms = [(A(QFM + k * cw + ts * 128, pr),
                        A(W2S + k * D + b2i * 512, 512)) for k in range(NCH)]
                mms.append((onesrow[0:1, 0:pr],
                            b2row[0:1, b2i * 512:b2i * 512 + 512]))
                te = P.mm_group(bank, PS[bank][0:pr, :], mms,
                                wait=[("ve", vv), ("ldw", ldwv),
                                      (ldx, lpx)])
                ev = P.op("scalar", lambda s, ts=ts, b2i=b2i, bank=bank, pr=pr:
                          s.copy(A(AOT + ts * D + b2i * 512, 512, 0, pr),
                                 PS[bank][0:pr, :]),
                          wait=[("te", te)], inc="sevac")
                P.release(bank, "sevac", ev)
        sv = P.at("sevac")
        s1b_tfree[0] = P.at("te")
        for ts in range(nts):
            pr = min(cw, 128)
            se_ln = ln_chunk(A(AOT + ts * D, D, 0, pr),
                             A(R1T + ts * D, D, 0, pr), ts,
                             ldwait=[("sevac", sv)] if ts == 0 else (), pr=pr,
                             wait2=[("stc", s1b_stgate[0])] if ts == 0 else ())
            vv2 = P.op("vector", lambda v, ts=ts, pr=pr: v.tensor_add(
                A(R1T + ts * D, D, 0, pr), A(R1T + ts * D, D, 0, pr),
                A(xoff + (ts * D if cw >= 128 else 0), D, 0, pr)),
                wait=[("se", se_ln)], inc="ve")
        s1b_xfree[par] = P.at("se")
        s1b_xvfree[par] = P.at("ve")
        if cw >= 128:
            st = P.dma("gpsimd", r1_scr.rearrange("(r p) d -> p r d", p=128)[
                :, roff // 128:roff // 128 + rows, :],
                A(R1T, rows * D).rearrange("p (r d) -> p r d", r=rows),
                "stc", wait=[("ve", P.at("ve"))])
        else:
            st = P.dma("gpsimd", r1_scr[roff:roff + cw, :],
                       A(R1T, D, 0, cw), "stc", wait=[("ve", P.at("ve"))])
        s1b_stgate[0] = st

    s1b_tfree = [0]
    s1b_vfree = [0]
    s1b_xfree = [0, 0]
    s1b_xvfree = [0, 0]
    s1b_stgate = [0]
    s1b_tile(-1)
    for it in range(NT):
        s1b_tile(it)
    P.barrier()

    if stop_stage < 2:
        P.emit(block, sems)
        es.close()
        return nc

    WRS, WKS, WVS = 0, 8192, 16384
    R1I0, R1I1 = 24576, 26624
    LNT20, LNT21 = 28672, 30720
    LNF2 = 32768
    XRB, XKB = 34816, 36864
    RSB, KAB = 38912, 40960
    R2T = 43008
    T2 = 256
    NT2 = tloc // T2
    ldwv = P.dma("sync", A(WRS, 8 * D).rearrange("p (c e) -> p c e", c=NCH),
                 wrT.rearrange("(c p) e -> p c e", p=128), "ldw")
    ldwv = P.dma("sync", A(WKS, 8 * D).rearrange("p (c e) -> p c e", c=NCH),
                 wkT.rearrange("(c p) e -> p c e", p=128), "ldw")
    ldwv = P.dma("sync", A(WVS, 8 * D).rearrange("p (c e) -> p c e", c=NCH),
                 wvT.rearrange("(c p) e -> p c e", p=128), "ldw")
    P.dma("sync", A(XRB, D, 0, 1), r1_scr[HALO - 1:HALO, :], "ldz")
    se_ln = ln_chunk(A(XRB, D, 0, 1), A(XKB, D, 0, 1), 0,
                     ldwait=[("ldz", P.at("ldz"))], pr=1)
    for c in range(NCH):
        te = P.op("tensor", lambda t, c=c: t.transpose(
            PS[2][:, 0:1], A(XKB + c * 128, 128, 0, 1), IDT[0:1, 0:1]),
            wait=([("se", se_ln)] if c == 0 else []) + P.guard(2), inc="te")
        ev = P.op("vector", lambda v, c=c: v.tensor_copy(
            prevcol[:, c:c + 1], PS[2][:, 0:1]), wait=[("te", te)], inc="vevac")
        P.release(2, "vevac", ev)
    vv = P.op("vector", lambda v: v.tensor_scalar(
        prevcol, prevcol, hflag_c[:, 0:1], None, ALU.mult),
        wait=[("vevac", P.at("vevac"))], inc="ve")

    s2_xfree = [0, 0]
    s2_vfree = [0, 0]
    s2_stgate = [0]
    for it in range(NT2):
        par = it % 2
        roff = HALO + it * T2
        r1o = R1I0 if par == 0 else R1I1
        lno = LNT20 if par == 0 else LNT21
        ldx = "ldxa" if par == 0 else "ldxb"
        P.dma("sync", A(r1o, 2 * D).rearrange("p (r d) -> p r d", r=2),
              r1_scr.rearrange("(r p) d -> p r d", p=128)[
                  :, roff // 128:roff // 128 + 2, :],
              ldx, wait=[("se", s2_xfree[par]), ("ve", s2_vfree[par])])
        lw = [(ldx, P.at(ldx))]
        se_ln = 0
        for r in range(2):
            se_ln = ln_chunk(A(r1o + r * D, D), A(lno + r * D, D), r,
                             ldwait=lw if r == 0 else ())
        s2_xfree[par] = se_ln
        tr_tile(lno, LNF2, 2, T2, wait=[("se", se_ln)])
        vtr = P.at("vevac")
        vv = 0
        for c in range(NCH):
            P.op("vector", lambda v, c=c: v.tensor_scalar(
                A(XRB + c * T2, T2), A(LNF2 + c * T2, T2),
                mur_c[:, c:c + 1], None, ALU.mult),
                wait=([("vevac", vtr), ("stc", s2_stgate[0])]
                      if c == 0 else ()), inc="ve")
            P.op("vector", lambda v, c=c: v.scalar_tensor_tensor(
                A(XRB + c * T2 + 1, T2 - 1), A(LNF2 + c * T2, T2 - 1),
                murI_c[:, c:c + 1], A(XRB + c * T2 + 1, T2 - 1),
                ALU.mult, ALU.add), inc="ve")
            P.op("vector", lambda v, c=c: v.scalar_tensor_tensor(
                A(XRB + c * T2, 1), prevcol[:, c:c + 1],
                murI_c[:, c:c + 1], A(XRB + c * T2, 1),
                ALU.mult, ALU.add), inc="ve")
            P.op("vector", lambda v, c=c: v.tensor_scalar(
                A(XKB + c * T2, T2), A(LNF2 + c * T2, T2),
                muk_c[:, c:c + 1], None, ALU.mult), inc="ve")
            P.op("vector", lambda v, c=c: v.scalar_tensor_tensor(
                A(XKB + c * T2 + 1, T2 - 1), A(LNF2 + c * T2, T2 - 1),
                mukI_c[:, c:c + 1], A(XKB + c * T2 + 1, T2 - 1),
                ALU.mult, ALU.add), inc="ve")
            vv = P.op("vector", lambda v, c=c: v.scalar_tensor_tensor(
                A(XKB + c * T2, 1), prevcol[:, c:c + 1],
                mukI_c[:, c:c + 1], A(XKB + c * T2, 1),
                ALU.mult, ALU.add), inc="ve")
        for c in range(NCH):
            vv = P.op("vector", lambda v, c=c: v.tensor_copy(
                prevcol[:, c:c + 1], A(LNF2 + c * T2 + T2 - 1, 1)), inc="ve")
        for ts in range(2):
            bank = 0 + (ts % 2)
            for b2i in range(2):
                bank = (ts * 2 + b2i) % 2
                mms = [(A(XRB + k * T2 + ts * 128, 128),
                        A(WRS + k * D + b2i * 512, 512)) for k in range(NCH)]
                te = P.mm_group(bank, PS[bank][:, :], mms,
                                wait=[("ve", vv), ("ldw", ldwv)])
                ev = P.op("scalar", lambda s, ts=ts, b2i=b2i, bank=bank:
                          s.activation(A(RSB + ts * D + b2i * 512, 512),
                                       PS[bank][:, :], ACTF.Sigmoid),
                          wait=[("te", te)], inc="sevac")
                P.release(bank, "sevac", ev)
        for m in range(NCH):
            bank = 2 + (m % 2)
            mms = [(A(WKS + k * D + m * 128, 128),
                    A(XKB + k * T2, T2)) for k in range(NCH)]
            te = P.mm_group(bank, PS[bank][:, 0:T2], mms,
                            wait=[("ve", vv), ("ldw", ldwv)])
            ev = P.op("scalar", lambda s, m=m, bank=bank: s.activation(
                A(KAB + m * T2, T2), PS[bank][:, 0:T2], ACTF.Relu),
                wait=[("te", te)], inc="sevac")
            P.release(bank, "sevac", ev)
            P.op("scalar", lambda s, m=m: s.activation(
                A(KAB + m * T2, T2), A(KAB + m * T2, T2), ACTF.Square),
                wait=[("sevac", P.at("sevac"))], inc="sevac")
        sv = P.at("sevac")
        for ts in range(2):
            for b2i in range(2):
                bank = 4 + ((ts * 2 + b2i) % 2)
                mms = [(A(KAB + k * T2 + ts * 128, 128),
                        A(WVS + k * D + b2i * 512, 512)) for k in range(NCH)]
                te = P.mm_group(bank, PS[bank][:, :], mms, wait=[("sevac", sv)])
                ev = P.op("vector", lambda v, ts=ts, b2i=b2i, bank=bank:
                          v.tensor_mul(A(R2T + ts * D + b2i * 512, 512),
                                       A(RSB + ts * D + b2i * 512, 512),
                                       PS[bank][:, :]),
                          wait=[("te", te)], inc="vevac")
                P.release(bank, "vevac", ev)
        vv2 = P.at("vevac")
        for ts in range(2):
            se_ln = ln_chunk(A(R2T + ts * D, D), A(XRB + ts * D, D), ts + 2,
                             ldwait=[("vevac", vv2)] if ts == 0 else ())
            P.op("vector", lambda v, ts=ts, r1o=r1o: v.tensor_add(
                A(XRB + ts * D, D), A(XRB + ts * D, D), A(r1o + ts * D, D)),
                wait=[("se", se_ln)], inc="ve")
        s2_vfree[par] = P.at("ve")
        st = P.dma("gpsimd", r2_scr.rearrange("(r p) d -> p r d", p=128)[
            :, it * 2:it * 2 + 2, :],
            A(XRB, 2 * D).rearrange("p (r d) -> p r d", r=2),
            "stc", wait=[("ve", P.at("ve"))])
        s2_stgate[0] = st
    P.barrier()

    if stop_stage < 3:
        P.emit(block, sems)
        es.close()
        return nc

    WQS, WKS3, WVS3 = 0, 8192, 16384
    R2I0, R2I1 = 24576, 26624
    LNT30, LNT31 = 28672, 30720
    LNF3 = 32768
    QFO, KFO, VTO = 34816, 36864, 38912
    ldwv = P.dma("sync", A(WQS, 8 * D).rearrange("p (c e) -> p c e", c=NCH),
                 wqTa.rearrange("(c p) e -> p c e", p=128), "ldw")
    ldwv = P.dma("sync", A(WKS3, 8 * D).rearrange("p (c e) -> p c e", c=NCH),
                 wkTa.rearrange("(c p) e -> p c e", p=128), "ldw")
    ldwv = P.dma("sync", A(WVS3, 8 * D).rearrange("p (c e) -> p c e", c=NCH),
                 wvTa.rearrange("(c p) e -> p c e", p=128), "ldw")
    s3_xfree = [0, 0]
    s3_sfree = [0]
    for it in range(NT2):
        par = it % 2
        r2o = R2I0 if par == 0 else R2I1
        lno = LNT30 if par == 0 else LNT31
        ldx = "ldxa" if par == 0 else "ldxb"
        P.dma("sync", A(r2o, 2 * D).rearrange("p (r d) -> p r d", r=2),
              r2_scr.rearrange("(r p) d -> p r d", p=128)[
                  :, it * 2:it * 2 + 2, :],
              ldx, wait=[("se", s3_xfree[par])])
        lw = [(ldx, P.at(ldx))]
        se_ln = 0
        for r in range(2):
            se_ln = ln_chunk(A(r2o + r * D, D), A(lno + r * D, D), r,
                             ldwait=lw if r == 0 else ())
        s3_xfree[par] = se_ln
        tr_tile(lno, LNF3, 2, T2, wait=[("se", se_ln), ("sta", s3_sfree[0])])
        vtr = P.at("vevac")
        for dst, ws in ((QFO, WQS), (KFO, WKS3)):
            for m in range(NCH):
                bank = (m % 2) + (0 if ws == WQS else 2)
                mms = [(A(ws + k * D + m * 128, 128),
                        A(LNF3 + k * T2, T2)) for k in range(NCH)]
                te = P.mm_group(bank, PS[bank][:, 0:T2], mms,
                                wait=[("vevac", vtr), ("ldw", ldwv)])
                ev = P.op("scalar", lambda s, dst=dst, m=m, bank=bank:
                          s.copy(A(dst + m * T2, T2), PS[bank][:, 0:T2]),
                          wait=[("te", te)], inc="sevac")
                P.release(bank, "sevac", ev)
        for ts in range(2):
            for b2i in range(2):
                bank = 4 + ((ts * 2 + b2i) % 2)
                mms = [(A(LNF3 + k * T2 + ts * 128, 128),
                        A(WVS3 + k * D + b2i * 512, 512)) for k in range(NCH)]
                te = P.mm_group(bank, PS[bank][:, :], mms,
                                wait=[("vevac", vtr), ("ldw", ldwv)])
                ev = P.op("scalar", lambda s, ts=ts, b2i=b2i, bank=bank:
                          s.copy(A(VTO + ts * D + b2i * 512, 512),
                                 PS[bank][:, :]),
                          wait=[("te", te)], inc="sevac")
                P.release(bank, "sevac", ev)
        sv = P.at("sevac")
        st = P.dma("gpsimd", q_scr.rearrange("(c p) t -> p c t", p=128)[
            :, :, it * T2:(it + 1) * T2],
            A(QFO, NCH * T2).rearrange("p (c t) -> p c t", c=NCH),
            "sta", wait=[("sevac", sv)])
        st = P.dma("gpsimd", k_scr.rearrange("(c p) t -> p c t", p=128)[
            :, :, it * T2:(it + 1) * T2],
            A(KFO, NCH * T2).rearrange("p (c t) -> p c t", c=NCH),
            "sta", wait=[])
        st = P.dma("gpsimd", v_scr.rearrange("(r p) d -> p r d", p=128)[
            :, it * 2:it * 2 + 2, :],
            A(VTO, 2 * D).rearrange("p (r d) -> p r d", r=2),
            "sta", wait=[])
        s3_sfree[0] = st
    P.barrier()

    if stop_stage < 4:
        P.emit(block, sems)
        es.close()
        return nc

    SS = 0
    QF3, KF3, KTM, VT3, QT3 = 8192, 12288, 16384, 20480, 24576
    SC3, WI3, G1B = 28672, 30720, 32768
    RT3 = 33280
    NC3 = tloc // 512
    P.dma("sync", A(WI3, 4 * 512).rearrange("p (s t) -> p s t", s=4),
          expo512.rearrange("(s p) t -> p s t", p=128), "ldz")
    P.dma("sync", A(G1B, 512), arange_rep[:, tloc - 512:tloc], "ldz")
    lz = P.at("ldz")
    sv = P.op("scalar", lambda s: s.activation(
        A(WI3, 4 * 512), A(WI3, 4 * 512), ACTF.Exp, bias=MLN32,
        scale=lng_c), wait=[("ldz", lz)], inc="se")
    sv = P.op("scalar", lambda s: s.activation(
        A(G1B, 512), A(G1B, 512), ACTF.Exp, bias=MLN32,
        scale=lng_c), inc="se")
    vv = P.op("vector", lambda v: v.memset(A(SS, 8 * D), 0.0), inc="ve")
    s3b_stgate = [0]
    if dbg:
        P.dma("gpsimd", dbg_ss[:, :], A(SS, 8 * D), "stb",
              wait=[("ve", vv)])
    s3b_free = [0, 0, 0]
    for ci in reversed(range(NC3)):
        tprev, vprev, sprev = s3b_free
        P.dma("sync", A(QF3, 8 * 512).rearrange("p (c t) -> p c t", c=NCH),
              q_scr.rearrange("(c p) t -> p c t", p=128)[
                  :, :, ci * 512:(ci + 1) * 512],
              "ldxa", wait=[("te", tprev), ("ve", vprev)])
        P.dma("sync", A(KF3, 8 * 512).rearrange("p (c t) -> p c t", c=NCH),
              k_scr.rearrange("(c p) t -> p c t", p=128)[
                  :, :, ci * 512:(ci + 1) * 512],
              "ldxb", wait=[("te", tprev), ("sevac", sprev)])
        P.dma("sync", A(VT3, 4 * D).rearrange("p (r d) -> p r d", r=4),
              v_scr.rearrange("(r p) d -> p r d", p=128)[
                  :, ci * 4:ci * 4 + 4, :],
              "ldya", wait=[("te", tprev)])
        lq, lk, lv = P.at("ldxa"), P.at("ldxb"), P.at("ldya")
        for tk in range(4):
            bank = tk % 2
            mms = [(A(KF3 + c * 512 + tk * 128, 128), A(QF3 + c * 512, 512))
                   for c in range(NCH)]
            te = P.mm_group(bank, PS[bank][:, :], mms,
                            wait=[("ldxa", lq), ("ldxb", lk), ("se", sv)])
            ev = P.op("vector", lambda v, tk=tk, bank=bank: v.tensor_mul(
                A(SC3 + tk * 512, 512), PS[bank][:, :], A(WI3 + tk * 512, 512)),
                wait=[("te", te)], inc="vevac")
            P.release(bank, "vevac", ev)
        for c in range(NCH):
            P.op("vector", lambda v, c=c: v.tensor_mul(
                A(QT3 + c * 512, 512), A(QF3 + c * 512, 512), A(G1B, 512)),
                wait=[("ldxa", lq)] if c == 0 else (), inc="ve")
        vq = P.at("ve")
        tr_last = None
        for c in range(NCH):
            for r in range(4):
                b = 2 + ((c * 4 + r) % 2)
                te = P.op("tensor", lambda t, c=c, r=r, b=b: t.transpose(
                    PS[b][:, 0:128], A(KF3 + c * 512 + r * 128, 128), IDT),
                    wait=([("ldxb", lk)] if (c == 0 and r == 0) else []) +
                    P.guard(b), inc="te")
                tr_last = P.op("scalar", lambda s, c=c, r=r, b=b: s.activation(
                    A(KTM + r * D + c * 128, 128), PS[b][:, 0:128],
                    ACTF.Identity, scale=kscale[:, r:r + 1]),
                    wait=[("te", te)], inc="sevac")
                P.release(b, "sevac", tr_last)
        ktm_done = P.at("sevac")
        scev = P.at("vevac")
        for tq in range(4):
            for b2i in range(2):
                bank = 4 + ((tq * 2 + b2i) % 2)
                mms = [(A(SC3 + tk * 512 + tq * 128, 128),
                        A(VT3 + tk * D + b2i * 512, 512)) for tk in range(4)]
                mms += [(A(QT3 + c * 512 + tq * 128, 128),
                         A(SS + c * D + b2i * 512, 512)) for c in range(NCH)]
                te = P.mm_group(bank, PS[bank][:, :], mms,
                                wait=[("vevac", scev), ("ve", vq),
                                      ("ldya", lv),
                                      ("sevac", P.at("sevac"))])
                ev = P.op("scalar", lambda s, tq=tq, b2i=b2i, bank=bank:
                          s.copy(A(RT3 + tq * D + b2i * 512, 512),
                                 PS[bank][:, :]),
                          wait=[("te", te)] +
                               ([("sta", s3b_stgate[0])]
                                if (tq == 0 and b2i == 0) else []),
                          inc="sevac")
                P.release(bank, "sevac", ev)
        st = P.dma("gpsimd", ret_scr.rearrange("(r p) d -> p r d", p=128)[
            :, ci * 4:ci * 4 + 4, :],
            A(RT3, 4 * D).rearrange("p (r d) -> p r d", r=4),
            "sta", wait=[("sevac", P.at("sevac"))])
        for m in range(NCH):
            for b2i in range(2):
                bank = 6 + ((m * 2 + b2i) % 2)
                mms = [(A(KTM + r * D + m * 128, 128),
                        A(VT3 + r * D + b2i * 512, 512)) for r in range(4)]
                te = P.mm_group(bank, PS[bank][:, :], mms,
                                wait=[("sevac", ktm_done)])
                ev = P.op("vector", lambda v, m=m, b2i=b2i, bank=bank:
                          v.scalar_tensor_tensor(
                              A(SS + m * D + b2i * 512, 512),
                              A(SS + m * D + b2i * 512, 512),
                              g512_c[:, 0:1], PS[bank][:, :],
                              ALU.mult, ALU.add),
                          wait=[("te", te)], inc="vevac")
                P.release(bank, "vevac", ev)
        s3b_free = [P.at("te"), P.at("ve"), P.at("sevac")]
        s3b_stgate[0] = st
    if dbg:
        P.dma("gpsimd", dbg_sc[:, :], A(SC3, 2048), "stb",
              wait=[("vevac", P.at("vevac")), ("te", P.at("te"))])
        P.dma("gpsimd", dbg_qt[:, :], A(QT3, 4096), "stb",
              wait=[("ve", P.at("ve"))])
        P.dma("gpsimd", dbg_wi[:, :], A(WI3, 2048), "stb",
              wait=[("se", P.at("se"))])
    st = P.dma("gpsimd", sag_in.rearrange("(c p) e -> p c e", p=128),
               A(SS, 8 * D).rearrange("p (c e) -> p c e", c=NCH),
               "stb", wait=[("vevac", P.at("vevac"))])
    ccv = P.op("gpsimd", lambda g: g.collective_compute(
        "AllGather", ALU.bypass, replica_groups=rg_pair,
        ins=[sag_in[:, :]], outs=[sag_out[:, :]]),
        wait=[("stb", st)], inc="cc")
    P.barrier()

    if stop_stage < 5:
        P.emit(block, sems)
        es.close()
        return nc

    SFO, QF4, QT4, GB, RT4 = 0, 8192, 12288, 16384, 16384 + tloc
    lz = P.dma("sync", A(SFO, 8 * D).rearrange("p (c e) -> p c e", c=NCH),
               sag_out[D:2 * D, :].rearrange("(c p) e -> p c e", p=128),
               "ldz", wait=[("cc", P.at("cc"))])
    vv = P.op("vector", lambda v: v.tensor_scalar(
        A(SFO, 8 * D), A(SFO, 8 * D), hfi_c[:, 0:1], None, ALU.mult),
        wait=[("ldz", lz)], inc="ve")
    P.dma("sync", A(GB, tloc), arange_rep[:, :], "ldz")
    sv = P.op("scalar", lambda s: s.activation(
        A(GB, tloc), A(GB, tloc), ACTF.Exp, bias=MLN32, scale=lng_c),
        wait=[("ldz", P.at("ldz"))], inc="se")
    p2free = [0, 0, 0]
    p2_stgate = [0]
    for ci in range(NC3):
        P.dma("sync", A(QF4, 8 * 512).rearrange("p (c t) -> p c t", c=NCH),
              q_scr.rearrange("(c p) t -> p c t", p=128)[
                  :, :, ci * 512:(ci + 1) * 512],
              "ldxa", wait=[("ve", p2free[0]), ("te", p2free[2])])
        lq = P.at("ldxa")
        P.dma("sync", A(RT4, 4 * D).rearrange("p (r d) -> p r d", r=4),
              ret_scr.rearrange("(r p) d -> p r d", p=128)[
                  :, ci * 4:ci * 4 + 4, :],
              "ldya", wait=[("vevac", p2free[1]), ("sta", p2_stgate[0])])
        lr = P.at("ldya")
        for c in range(NCH):
            P.op("vector", lambda v, c=c, ci=ci: v.tensor_mul(
                A(QT4 + c * 512, 512), A(QF4 + c * 512, 512),
                A(GB + ci * 512, 512)),
                wait=[("ldxa", lq), ("se", sv), ("ve", vv)] if c == 0 else (),
                inc="ve")
        vq = P.at("ve")
        p2free[0] = vq
        vlast = 0
        for tq in range(4):
            for b2i in range(2):
                bank = (tq * 2 + b2i) % 4
                mms = [(A(QT4 + c * 512 + tq * 128, 128),
                        A(SFO + c * D + b2i * 512, 512)) for c in range(NCH)]
                te = P.mm_group(bank, PS[bank][:, :], mms,
                                wait=[("ve", vq)])
                vlast = P.op("vector", lambda v, tq=tq, b2i=b2i, bank=bank:
                             v.tensor_add(A(RT4 + tq * D + b2i * 512, 512),
                                          A(RT4 + tq * D + b2i * 512, 512),
                                          PS[bank][:, :]),
                             wait=[("te", te), ("ldya", lr)] +
                                  ([("sta", p2_stgate[0])]
                                   if (tq == 0 and b2i == 0) else []),
                             inc="vevac")
                P.release(bank, "vevac", vlast)
        st = P.dma("gpsimd", ret_scr.rearrange("(r p) d -> p r d", p=128)[
            :, ci * 4:ci * 4 + 4, :],
            A(RT4, 4 * D).rearrange("p (r d) -> p r d", r=4),
            "sta", wait=[("vevac", vlast)])
        p2_stgate[0] = st
        p2free[1] = P.at("vevac")
        p2free[2] = P.at("te")
    P.barrier()

    if stop_stage < 6:
        P.emit(block, sems)
        es.close()
        return nc

    WOS = 0
    REI0, REI1 = 8192, 12288
    R2I0b, R2I1b = 16384, 20480
    AOT3, R3T = 24576, 28672
    ldwv = P.dma("sync", A(WOS, 8 * D).rearrange("p (c e) -> p c e", c=NCH),
                 woTa.rearrange("(c p) e -> p c e", p=128), "ldw")
    s3c_tfree = [0, 0]
    s3c_vfree = [0, 0]
    s3c_stgate = [0]
    for it in range(NT):
        par = it % 2
        reo = REI0 if par == 0 else REI1
        r2o = R2I0b if par == 0 else R2I1b
        ldx = "ldxa" if par == 0 else "ldxb"
        ldy = "ldya" if par == 0 else "ldyb"
        P.dma("sync", A(reo, 4 * D).rearrange("p (r d) -> p r d", r=4),
              ret_scr.rearrange("(r p) d -> p r d", p=128)[
                  :, it * 4:it * 4 + 4, :],
              ldx, wait=[("se", s3c_tfree[par])])
        P.dma("sync", A(r2o, 4 * D).rearrange("p (r d) -> p r d", r=4),
              r2_scr.rearrange("(r p) d -> p r d", p=128)[
                  :, it * 4:it * 4 + 4, :],
              ldy, wait=[("ve", s3c_vfree[par])])
        lr, l2 = P.at(ldx), P.at(ldy)
        tr_tile(reo, AOT3, 4, 512, wait=[(ldx, lr)])
        vtr = P.at("vevac")
        for ts in range(4):
            for b2i in range(2):
                bank = 4 + ((ts * 2 + b2i) % 2)
                mms = [(A(AOT3 + k * 512 + ts * 128, 128),
                        A(WOS + k * D + b2i * 512, 512)) for k in range(NCH)]
                te = P.mm_group(bank, PS[bank][:, :], mms,
                                wait=[("vevac", vtr), ("ldw", ldwv)])
                ev = P.op("scalar", lambda s, ts=ts, b2i=b2i, bank=bank,
                          reo=reo:
                          s.activation(A(reo + ts * D + b2i * 512, 512),
                                       PS[bank][:, :], ACTF.Identity,
                                       scale=osc_c[:, 0:1]),
                          wait=[("te", te)], inc="sevac")
                P.release(bank, "sevac", ev)
        sv = P.at("sevac")
        for ts in range(4):
            se_ln = ln_chunk(A(reo + ts * D, D), A(R3T + ts * D, D), ts,
                             ldwait=[("sevac", sv)] if ts == 0 else (),
                             wait2=[("stc", s3c_stgate[0])] if ts == 0 else ())
            P.op("vector", lambda v, ts=ts, r2o=r2o: v.tensor_add(
                A(R3T + ts * D, D), A(R3T + ts * D, D), A(r2o + ts * D, D)),
                wait=[("se", se_ln), (ldy, l2)], inc="ve")
        st = P.dma("gpsimd", r3_scr.rearrange("(r p) d -> p r d", p=128)[
            :, it * 4:it * 4 + 4, :],
            A(R3T, 4 * D).rearrange("p (r d) -> p r d", r=4),
            "stc", wait=[("ve", P.at("ve"))])
        s3c_stgate[0] = st
        s3c_tfree[par] = P.at("se")
        s3c_vfree[par] = P.at("ve")
    P.barrier()

    if stop_stage < 7:
        P.emit(block, sems)
        es.close()
        return nc

    PHS = 0
    R3I0, R3I1 = 16384, 18432
    LNT40, LNT41 = 20480, 22528
    LNF4 = 24576
    LGB, ELG, DBF, CBF, CTB = 26624, 30720, 34816, 38912, 43008
    ldwv = P.dma("sync", A(PHS, NCH * NPTOT).rearrange(
        "p (c e) -> p c e", c=NCH),
        phiT.rearrange("(c p) e -> p c e", p=128), "ldw")
    s4_xfree = [0, 0]
    s4_pfree = [0]
    for it in range(NT2):
        par = it % 2
        r3o = R3I0 if par == 0 else R3I1
        lno = LNT40 if par == 0 else LNT41
        ldx = "ldxa" if par == 0 else "ldxb"
        P.dma("sync", A(r3o, 2 * D).rearrange("p (r d) -> p r d", r=2),
              r3_scr.rearrange("(r p) d -> p r d", p=128)[
                  :, it * 2:it * 2 + 2, :],
              ldx, wait=[("se", s4_xfree[par])])
        lw = [(ldx, P.at(ldx))]
        se_ln = 0
        for r in range(2):
            se_ln = ln_chunk(A(r3o + r * D, D), A(lno + r * D, D), r,
                             ldwait=lw if r == 0 else ())
        s4_xfree[par] = se_ln
        tr_tile(lno, LNF4, 2, T2, wait=[("se", se_ln)])
        vtr = P.at("vevac")
        for ts in range(2):
            for bq in range(4):
                bank = (ts * 4 + bq) % 4
                mms = [(A(LNF4 + k * T2 + ts * 128, 128),
                        A(PHS + k * NPTOT + bq * 512, 512)) for k in range(NCH)]
                mms.append((onesrow[0:1, 0:128],
                            phibrow[0:1, bq * 512:bq * 512 + 512]))
                te = P.mm_group(bank, PS[bank][:, :], mms,
                                wait=[("vevac", vtr), ("ldw", ldwv),
                                      ("sta", s4_pfree[0])])
                ev = P.op("scalar", lambda s, ts=ts, bq=bq, bank=bank:
                          s.copy(A(LGB + ts * NPTOT + bq * 512, 512),
                                 PS[bank][:, :]),
                          wait=[("te", te)], inc="sevac")
                P.release(bank, "sevac", ev)
        sv = P.at("sevac")
        for r in range(2):
            lg = A(LGB + r * NPTOT, NPTOT)
            el = A(ELG + r * NPTOT, NPTOT)
            db = A(DBF + r * NPTOT, NPTOT)
            cb = A(CBF + r * NPTOT, NPTOT)
            sv = P.op("scalar", lambda s, lg=lg, el=el: s.activation(
                el, lg, ACTF.Exp), wait=[("sevac", sv)] if r == 0 else (),
                inc="se")
            vv = P.op("vector", lambda v, el=el: v.reduce_sum(
                SD64, el.rearrange("p (g s) -> p g s", s=32), axis=AX.X),
                wait=[("se", sv)], inc="ve")
            vv = P.op("vector", lambda v: v.reciprocal(SD64, SD64),
                      wait=[("ve", P.at("ve"))], inc="ve")
            vv = P.op("vector", lambda v, el=el, db=db: v.tensor_tensor(
                db.rearrange("p (g s) -> p g s", s=32),
                el.rearrange("p (g s) -> p g s", s=32),
                SD64.unsqueeze(2).broadcast_to([128, 64, 32]), ALU.mult),
                wait=[("ve", P.at("ve"))] +
                     ([("sta", s4_pfree[0])] if r == 0 else []), inc="ve")
            vv = P.op("vector", lambda v, el=el: v.reduce_sum(
                SD32, el.rearrange("p (g s) -> p s g", s=32), axis=AX.X),
                inc="ve")
            vv = P.op("vector", lambda v: v.reciprocal(SD32, SD32),
                      wait=[("ve", P.at("ve"))], inc="ve")
            vv = P.op("vector", lambda v, el=el, cb=cb: v.tensor_tensor(
                cb.rearrange("p (g s) -> p s g", s=32),
                el.rearrange("p (g s) -> p s g", s=32),
                SD32.unsqueeze(2).broadcast_to([128, 32, 64]), ALU.mult),
                wait=[("ve", P.at("ve"))], inc="ve")
        vdc = P.at("ve")
        st = P.dma("gpsimd", d_scr.rearrange("(r p) e -> p r e", p=128)[
            :, it * 2:it * 2 + 2, :],
            A(DBF, 2 * NPTOT).rearrange("p (r e) -> p r e", r=2),
            "sta", wait=[("ve", vdc)])
        for npc in range(NPC):
            for r in range(2):
                b = 4 + ((npc * 2 + r) % 2)
                te = P.op("tensor", lambda t, npc=npc, r=r, b=b: t.transpose(
                    PS[b][:, 0:128], A(CBF + r * NPTOT + npc * 128, 128), IDT),
                    wait=([("ve", vdc)] if (npc == 0 and r == 0) else []) +
                    P.guard(b), inc="te")
                ev = P.op("vector", lambda v, npc=npc, r=r, b=b: v.tensor_copy(
                    A(CTB + npc * T2 + r * 128, 128), PS[b][:, 0:128]),
                    wait=[("te", te)], inc="vevac")
                P.release(b, "vevac", ev)
        st = P.dma("gpsimd", c_scr.rearrange("(c p) t -> p c t", p=128)[
            :, :, it * T2:(it + 1) * T2],
            A(CTB, NPC * T2).rearrange("p (c t) -> p c t", c=NPC),
            "sta", wait=[("vevac", P.at("vevac"))])
        s4_pfree[0] = st
    P.barrier()

    if stop_stage < 8:
        P.emit(block, sems)
        es.close()
        return nc

    XP = 0
    DI0, DI1 = 16384, 20480
    R4I0, R4I1 = 24576, 26624
    LNT50, LNT51 = 28672, 30720
    vxp = P.op("vector", lambda v: v.memset(A(XP, 16384), 0.0), inc="ve")
    s4b_tfree = [0, 0]
    s4b_sfree = [0, 0]
    for it in range(NT2):
        par = it % 2
        dio = DI0 if par == 0 else DI1
        r3o = R4I0 if par == 0 else R4I1
        lno = LNT50 if par == 0 else LNT51
        ldx = "ldxa" if par == 0 else "ldxb"
        ldy = "ldya" if par == 0 else "ldyb"
        P.dma("sync", A(dio, 2 * NPTOT).rearrange("p (r e) -> p r e", r=2),
              d_scr.rearrange("(r p) e -> p r e", p=128)[
                  :, it * 2:it * 2 + 2, :],
              ldx, wait=[("te", s4b_tfree[par])])
        P.dma("sync", A(r3o, 2 * D).rearrange("p (r d) -> p r d", r=2),
              r3_scr.rearrange("(r p) d -> p r d", p=128)[
                  :, it * 2:it * 2 + 2, :],
              ldy, wait=[("se", s4b_sfree[par])])
        ld_, lr_ = P.at(ldx), P.at(ldy)
        se_ln = 0
        for r in range(2):
            se_ln = ln_chunk(A(r3o + r * D, D), A(lno + r * D, D), r,
                             ldwait=[(ldy, lr_)] if r == 0 else ())
        s4b_sfree[par] = se_ln
        for npc in range(NPC):
            for b2i in range(2):
                bank = (npc * 2 + b2i) % 6
                mms = [(A(dio + r * NPTOT + npc * 128, 128),
                        A(lno + r * D + b2i * 512, 512)) for r in range(2)]
                te = P.mm_group(bank, PS[bank][:, :], mms,
                                wait=[(ldx, ld_), ("se", se_ln),
                                      ("ve", vxp)])
                ev = P.op("vector", lambda v, npc=npc, b2i=b2i, bank=bank:
                          v.tensor_add(A(XP + npc * D + b2i * 512, 512),
                                       A(XP + npc * D + b2i * 512, 512),
                                       PS[bank][:, :]),
                          wait=[("te", te)], inc="vevac")
                P.release(bank, "vevac", ev)
        s4b_tfree[par] = P.at("te")
    st = P.dma("gpsimd", xt_in.rearrange("(c p) e -> p c e", p=128),
               A(XP, NPC * D).rearrange("p (c e) -> p c e", c=NPC),
               "stb", wait=[("vevac", P.at("vevac"))])
    ccv = P.op("gpsimd", lambda g: g.collective_compute(
        "ReduceScatter", ALU.add, replica_groups=rg_all,
        ins=[xt_in[:, :]], outs=[xt_out[:, :]]),
        wait=[("stb", st)], inc="cc")
    P.barrier()

    if stop_stage < 9:
        P.emit(block, sems)
        es.close()
        return nc

    XS = 0
    LNB = XS + NPLC * D
    XTF = LNB + NPLC * D
    EW0 = XTF + NCH * NPL
    EW1 = EW0 + 8192
    YTL = EW1 + 8192
    EBR = SM[0:1, 1024:2048]
    EBA = YTL + NPLC * D
    lz = P.dma("sync", A(XS, NPLC * D).rearrange("p (c d) -> p c d", c=NPLC),
               xt_out.rearrange("(c p) d -> p c d", p=128),
               "ldz", wait=[("cc", P.at("cc"))])
    P.dma("sync", A(EBA, NEL * D, 0, 1), eb[:, :], "ldz")
    lz = P.at("ldz")
    vv = P.op("vector", lambda v: v.tensor_scalar(
        A(XS, NPLC * D), A(XS, NPLC * D), 65000.0, -33000.0, ALU.min, ALU.max),
        wait=[("ldz", lz)], inc="ve")
    sv = 0
    for r in range(NPLC):
        sv = ln_chunk(A(XS + r * D, D), A(LNB + r * D, D), r,
                      ldwait=[("ve", vv)] if r == 0 else ())
    for r in range(NPLC):
        sv = P.op("scalar", lambda s, r=r: s.activation(
            A(XS + r * D, D), A(LNB + r * D, D), ACTF.Tanh), inc="se")
    tr_tile(XS, XTF, NPLC, NPL, wait=[("se", sv)])
    vtr = P.at("vevac")
    ew_sem = ["ldxa", "ldxb"]
    ew_free = [0, 0]
    for e in range(NEL):
        par = e % 2
        ewo = EW0 if par == 0 else EW1
        P.dma("sync", A(ewo, 8 * D).rearrange("p (c h) -> p c h", c=NCH),
              ew[e, :, :].rearrange("(c p) h -> p c h", p=128),
              ew_sem[par], wait=[("te", ew_free[par])])
        le = P.at(ew_sem[par])
        for b2i in range(2):
            bank = (e * 2 + b2i) % 4
            mms = [(A(XTF + k * NPL + e * 32, 32, 0, 128),
                    A(ewo + k * D + b2i * 512, 512)) for k in range(NCH)]
            mms.append((onesrow[0:1, 0:32],
                        A(EBA + e * D + b2i * 512, 512, 0, 1)))
            te = P.mm_group(bank, PS[bank][0:32, :], mms,
                            wait=[("vevac", vtr), (ew_sem[par], le),
                                  ("ldz", lz)])
            ev = P.op("vector", lambda v, e=e, b2i=b2i, bank=bank:
                      v.tensor_copy(A(YTL + e * D + b2i * 512, 512, 0, 32),
                                    PS[bank][0:32, :]),
                      wait=[("te", te)], inc="vevac")
            P.release(bank, "vevac", ev)
        ew_free[par] = P.at("te")
    st = P.dma("gpsimd", yt_in.rearrange("(n p) d -> p n d", p=32),
               A(YTL, NEL * D, 0, 32).rearrange("p (n d) -> p n d", n=NEL),
               "stb", wait=[("vevac", P.at("vevac"))])
    ccv = P.op("gpsimd", lambda g: g.collective_compute(
        "AllGather", ALU.bypass, replica_groups=rg_all,
        ins=[yt_in[:, :]], outs=[yt_out[:, :]]),
        wait=[("stb", st)], inc="cc")
    P.barrier()

    if stop_stage < 10:
        P.emit(block, sems)
        es.close()
        return nc

    YTS = 0
    CFB = 16384
    R5I0, R5I1 = 24576, 28672
    YOB, OTB = 32768, 36864
    lz = P.dma("sync", A(YTS, NPC * D).rearrange("p (c d) -> p c d", c=NPC),
               yt_out.rearrange("(c p) d -> p c d", p=128),
               "ldz", wait=[("cc", P.at("cc"))])
    lyt = P.at("ldz")
    s5_free = [0, 0]
    s5_cfree = [0]
    s5_stgate = [0]
    for it in range(NT):
        par = it % 2
        r3o = R5I0 if par == 0 else R5I1
        ldx = "ldxa" if par == 0 else "ldxb"
        P.dma("sync", A(CFB, NPC * 512).rearrange("p (c t) -> p c t", c=NPC),
              c_scr.rearrange("(c p) t -> p c t", p=128)[
                  :, :, it * 512:(it + 1) * 512],
              "ldya", wait=[("te", s5_cfree[0])])
        lc = P.at("ldya")
        P.dma("sync", A(r3o, 4 * D).rearrange("p (r d) -> p r d", r=4),
              r3_scr.rearrange("(r p) d -> p r d", p=128)[
                  :, it * 4:it * 4 + 4, :],
              ldx, wait=[("ve", s5_free[par])])
        l3 = P.at(ldx)
        for ts in range(4):
            for b2i in range(2):
                bank = (ts * 2 + b2i) % 4
                mms = [(A(CFB + npc * 512 + ts * 128, 128),
                        A(YTS + npc * D + b2i * 512, 512))
                       for npc in range(NPC)]
                te = P.mm_group(bank, PS[bank][:, :], mms,
                                wait=[("ldya", lc), ("ldz", lyt)])
                ev = P.op("scalar", lambda s, ts=ts, b2i=b2i, bank=bank:
                          s.copy(A(YOB + ts * D + b2i * 512, 512),
                                 PS[bank][:, :]),
                          wait=[("te", te)], inc="sevac")
                P.release(bank, "sevac", ev)
        s5_cfree[0] = P.at("te")
        sv = P.at("sevac")
        for ts in range(4):
            se_ln = ln_chunk(A(YOB + ts * D, D), A(OTB + ts * D, D), ts,
                             ldwait=[("sevac", sv)] if ts == 0 else (),
                             wait2=[("sto", s5_stgate[0])] if ts == 0 else ())
            P.op("vector", lambda v, ts=ts, r3o=r3o: v.tensor_add(
                A(OTB + ts * D, D), A(OTB + ts * D, D), A(r3o + ts * D, D)),
                wait=[("se", se_ln), (ldx, l3)], inc="ve")
        st = P.dma("gpsimd", out.rearrange("(r p) d -> p r d", p=128)[
            :, it * 4:it * 4 + 4, :],
            A(OTB, 4 * D).rearrange("p (r d) -> p r d", r=4),
            "sto", wait=[("ve", P.at("ve"))])
        s5_free[par] = P.at("ve")
        s5_stgate[0] = st
    P.op("gpsimd", lambda g: g.nop(), wait=[("sto", P.at("sto"))])

    P.emit(block, sems)
    es.close()
    return nc


def _mk_inputs(core, ncores, tloc, ins):
    b, h = core // 2, core % 2
    nbat = 4 if ncores == 8 else 1
    x = np.asarray(ins["L0_attn_input_data"])
    T = x.shape[1]
    xs = x[b, h * tloc:(h + 1) * tloc]
    halo = (np.zeros((HALO, D), np.float32) if h == 0
            else x[b, tloc - HALO:tloc])
    NEL = 64 // ncores
    expo = np.arange(512)[:, None] - np.arange(512)[None, :] - 1.0
    expo = np.where(expo >= 0, expo, 1e38).astype(np.float32)
    ecol = (128.0 * np.arange(4)[None, :]
            + np.arange(128)[:, None]).astype(np.float32)
    sel_own = np.zeros((1, ncores), np.float32)
    sel_own[0, core] = 1.0
    sel_pair = np.zeros((1, ncores), np.float32)
    sel_pair[0, 2 * b] = 1.0
    sel_pair[0, 2 * b + 1] = 1.0
    return {
        "x": np.ascontiguousarray(
            np.concatenate([halo, xs], 0), np.float32),
        "w1": np.ascontiguousarray(ins["L0_attn_W1"], np.float32),
        "b1": np.ascontiguousarray(ins["L0_attn_bias1"], np.float32),
        "w2": np.ascontiguousarray(ins["L0_attn_W2"], np.float32),
        "b2": np.ascontiguousarray(
            ins["L0_attn_bias2"], np.float32).reshape(1, D),
        "mw": np.ascontiguousarray(ins["L0_attn_memory_weights"], np.float32),
        "wrT": np.ascontiguousarray(
            np.asarray(ins["L0_ffn_W_r"], np.float32).T),
        "wkT": np.ascontiguousarray(
            np.asarray(ins["L0_ffn_W_k"], np.float32).T),
        "wvT": np.ascontiguousarray(
            np.asarray(ins["L0_ffn_W_v"], np.float32).T),
        "mu_r": np.ascontiguousarray(ins["L0_ffn_mu_r"], np.float32),
        "mu_k": np.ascontiguousarray(ins["L0_ffn_mu_k"], np.float32),
        "wqTa": np.ascontiguousarray(
            np.asarray(ins["L1_attn_Wq"], np.float32).T),
        "wkTa": np.ascontiguousarray(
            np.asarray(ins["L1_attn_Wk"], np.float32).T),
        "wvTa": np.ascontiguousarray(
            np.asarray(ins["L1_attn_Wv"], np.float32).T),
        "woTa": np.ascontiguousarray(
            np.asarray(ins["L1_attn_Wo"], np.float32).T),
        "dlogit": np.asarray(ins["L1_attn_decay_logit"],
                             np.float32).reshape(1),
        "oscale": np.asarray(ins["L1_attn_out_scale"],
                             np.float32).reshape(1),
        "phiT": np.ascontiguousarray(
            np.asarray(ins["L1_ffn_phi_w"], np.float32).T),
        "phib": np.ascontiguousarray(
            ins["L1_ffn_phi_b"], np.float32).reshape(1, NPTOT),
        "ew": np.ascontiguousarray(
            np.asarray(ins["L1_ffn_expert_w"],
                       np.float32)[core * NEL:(core + 1) * NEL]),
        "eb": np.ascontiguousarray(
            np.asarray(ins["L1_ffn_expert_b"],
                       np.float32)[core * NEL:(core + 1) * NEL]
        ).reshape(1, NEL * D),
        "ident": np.eye(128, dtype=np.float32),
        "expo512": expo,
        "arange_rep": np.ascontiguousarray(
            np.broadcast_to(np.arange(tloc - 1, -1, -1, dtype=np.float32),
                            (128, tloc))),
        "ecol": ecol,
        "hmask": (np.ones((1, HALO), np.float32) if h == 1
                  else np.zeros((1, HALO), np.float32)),
        "hflag": np.asarray([1.0 if h == 1 else 0.0], np.float32),
        "selpair": sel_pair,
        "selown": sel_own,
    }


def kernel(**inputs):
    ncores = 8
    tloc = 4096
    nc = build_nc(ncores=ncores, tloc=tloc)
    in_maps = [_mk_inputs(c, ncores, tloc, inputs) for c in range(ncores)]
    res = run_bass_kernel_spmd(nc, in_maps, core_ids=list(range(ncores)))
    shards = [res.results[c]["out"] for c in range(ncores)]
    full = np.zeros((4, 2 * tloc, D), np.float32)
    for c in range(ncores):
        b, h = c // 2, c % 2
        full[b, h * tloc:(h + 1) * tloc] = shards[c]
    return full


# version 07
# speedup vs baseline: 1.0151x; 1.0151x over previous
import numpy as np
import concourse.bass as bass
import concourse.mybir as mybir
from concourse.bass_utils import run_bass_kernel_spmd
from contextlib import ExitStack

F32 = mybir.dt.float32
BF16 = mybir.dt.bfloat16
AX = mybir.AxisListType
ALU = mybir.AluOpType
ACTF = mybir.ActivationFunctionType

ENGINES = ("sync", "tensor", "vector", "scalar", "gpsimd")

D = 1024
NCH = 8
TT = 512
HALO = 128
HB = 64
MEM = 40
NPTOT = 2048
EPS = 1e-5
MLN32 = -float(np.log(32.0))
ARN = 48256


class Prog:
    def __init__(self):
        self.ops = {e: [] for e in ENGINES}
        self.count = {}
        self.writer = {}
        self.psum_free = {}
        self.last_inc = {}
        self.dirty = {e: False for e in ENGINES}

    def sem(self, name):
        self.count.setdefault(name, 0)
        return name

    def at(self, name):
        return self.count.get(name, 0)

    def op(self, eng, fn, wait=(), inc=None, inc_amt=1):
        w = [(self.sem(s), v) for (s, v) in wait if v and v > 0]
        post = None
        incs = []
        if inc is not None:
            self.sem(inc)
            assert self.writer.setdefault(inc, eng) == eng, (inc, eng)
            self.count[inc] += inc_amt
            post = self.count[inc]
            incs.append((inc, inc_amt))
            self.last_inc[eng] = (inc, post)
        self.ops[eng].append([fn, w, incs])
        self.dirty[eng] = True
        return post

    def dma(self, eng, out, in_, sem, wait=()):
        return self.op(eng, lambda e, o=out, i=in_: e.dma_start(out=o, in_=i),
                       wait=wait, inc=sem, inc_amt=16)

    def barrier(self):
        vals = {}
        for e in ENGINES:
            eos = self.sem(f"eos_{e}")
            assert self.writer.setdefault(eos, e) == e
            if self.dirty[e] and self.ops[e]:
                last = self.ops[e][-1]
                if not last[2]:
                    self.count[eos] += 1
                    last[2].append((eos, 1))
                    self.last_inc[e] = (eos, self.count[eos])
                waits = [(s, self.count[s]) for s, w in self.writer.items()
                         if w == e and s != eos and self.count.get(s, 0) > 0]
                vals[e] = self.op(e, lambda en: en.nop(), wait=waits, inc=eos)
            else:
                vals[e] = self.op(e, lambda en: en.nop(), inc=eos)
        for e in ENGINES:
            self.op(e, lambda en: en.nop(),
                    wait=[(f"eos_{o}", vals[o]) for o in ENGINES if o != e])
        for e in ENGINES:
            self.dirty[e] = False

    def guard(self, bank):
        w = self.psum_free.get(bank)
        return [w] if w else []

    def release(self, bank, sem, val):
        self.psum_free[bank] = (sem, val)

    def mm_group(self, bank, bank_ap, mms, wait=()):
        n = len(mms)
        te = None
        for i, (lh, rh) in enumerate(mms):
            w = (list(wait) + self.guard(bank)) if i == 0 else []
            te = self.op("tensor",
                         lambda t, lh=lh, rh=rh, i=i, n=n, b=bank_ap: t.matmul(
                             b, lh, rh, start=(i == 0), stop=(i == n - 1)),
                         wait=w, inc="te" if i == n - 1 else None)
        return te

    def emit(self, block, sems):
        def run(eng_obj, eng_name):
            for fn, waits, incs in self.ops[eng_name]:
                for s, v in waits:
                    eng_obj.wait_ge(sems[s], v)
                r = fn(eng_obj)
                for s, amt in incs:
                    r.then_inc(sems[s], amt)
        block.sync(lambda e: run(e, "sync"))
        block.tensor(lambda e: run(e, "tensor"))
        block.vector(lambda e: run(e, "vector"))
        block.scalar(lambda e: run(e, "scalar"))
        block.gpsimd(lambda e: run(e, "gpsimd"))


def build_nc(ncores=8, tloc=4096, dbg=False, stop_stage=99):
    NT = tloc // TT
    NPC = NPTOT // 128
    NPL = NPTOT // ncores
    NPLC = NPL // 128
    NEL = 64 // ncores

    nc = bass.Bass("TRN2", target_bir_lowering=False,
                   detect_race_conditions=False)
    dt = nc.dram_tensor

    x_in = dt("x", [HALO + tloc, D], F32, kind="ExternalInput").ap()
    w1 = dt("w1", [D, D], F32, kind="ExternalInput").ap()
    b1 = dt("b1", [D], F32, kind="ExternalInput").ap()
    w2 = dt("w2", [D, D], F32, kind="ExternalInput").ap()
    b2 = dt("b2", [1, D], F32, kind="ExternalInput").ap()
    mw = dt("mw", [MEM + 1, D], F32, kind="ExternalInput").ap()
    wrT = dt("wrT", [D, D], F32, kind="ExternalInput").ap()
    wkT = dt("wkT", [D, D], F32, kind="ExternalInput").ap()
    wvT = dt("wvT", [D, D], F32, kind="ExternalInput").ap()
    mu_r = dt("mu_r", [D], F32, kind="ExternalInput").ap()
    mu_k = dt("mu_k", [D], F32, kind="ExternalInput").ap()
    wqTa = dt("wqTa", [D, D], F32, kind="ExternalInput").ap()
    wkTa = dt("wkTa", [D, D], F32, kind="ExternalInput").ap()
    wvTa = dt("wvTa", [D, D], F32, kind="ExternalInput").ap()
    woTa = dt("woTa", [D, D], F32, kind="ExternalInput").ap()
    dlogit = dt("dlogit", [1], F32, kind="ExternalInput").ap()
    oscale = dt("oscale", [1], F32, kind="ExternalInput").ap()
    phiT = dt("phiT", [D, NPTOT], F32, kind="ExternalInput").ap()
    phib = dt("phib", [1, NPTOT], F32, kind="ExternalInput").ap()
    ew = dt("ew", [NEL, D, D], F32, kind="ExternalInput").ap()
    eb = dt("eb", [1, NEL * D], F32, kind="ExternalInput").ap()
    ident = dt("ident", [128, 128], F32, kind="ExternalInput").ap()
    expo512 = dt("expo512", [512, 512], F32, kind="ExternalInput").ap()
    arange_rep = dt("arange_rep", [128, tloc], F32, kind="ExternalInput").ap()
    ecol = dt("ecol", [128, 4], F32, kind="ExternalInput").ap()
    hmask = dt("hmask", [1, HALO], F32, kind="ExternalInput").ap()
    hflag = dt("hflag", [1], F32, kind="ExternalInput").ap()
    selpair = dt("selpair", [1, ncores], F32, kind="ExternalInput").ap()
    selown = dt("selown", [1, ncores], F32, kind="ExternalInput").ap()

    ik = "ExternalOutput" if dbg else "Internal"
    if dbg:
        dbg_ln = dt("dbg_ln", [128, 4 * D], F32, kind="ExternalOutput").ap()
        dbg_fm = dt("dbg_fm", [128, 4096], F32, kind="ExternalOutput").ap()
        dbg_sc = dt("dbg_sc", [128, 2048], F32, kind="ExternalOutput").ap()
        dbg_qt = dt("dbg_qt", [128, 4096], F32, kind="ExternalOutput").ap()
        dbg_wi = dt("dbg_wi", [128, 2048], F32, kind="ExternalOutput").ap()
        dbg_ss = dt("dbg_ss", [128, 8192], F32, kind="ExternalOutput").ap()
        dbg_ao = dt("dbg_ao", [128, 4096], F32, kind="ExternalOutput").ap()
        dbg_wo = dt("dbg_wo", [128, 8192], F32, kind="ExternalOutput").ap()
    out = dt("out", [tloc, D], F32, kind="ExternalOutput").ap()
    p_scr = dt("p_scr", [D, HALO + tloc], F32, kind=ik).ap()
    r1_scr = dt("r1_scr", [HALO + tloc, D], F32, kind=ik).ap()
    r2_scr = dt("r2_scr", [tloc, D], F32, kind=ik).ap()
    q_scr = dt("q_scr", [D, tloc], F32, kind=ik).ap()
    k_scr = dt("k_scr", [D, tloc], F32, kind=ik).ap()
    v_scr = dt("v_scr", [tloc, D], F32, kind=ik).ap()
    ret_scr = dt("ret_scr", [tloc, D], F32, kind=ik).ap()
    r3_scr = dt("r3_scr", [tloc, D], F32, kind=ik).ap()
    d_scr = dt("d_scr", [tloc, NPTOT], F32, kind=ik).ap()
    c_scr = dt("c_scr", [NPTOT, tloc], F32, kind=ik).ap()
    ar_in = dt("ar_in", [128, 8 * ncores], F32, kind="Internal").ap()
    ar_out = dt("ar_out", [128, 8 * ncores], F32, kind="Internal").ap()
    sag_in = dt("sag_in", [D, D], F32, kind="Internal").ap()
    sag_out = dt("sag_out", [2 * D, D], F32, kind="Internal").ap()
    xt_in = dt("xt_in", [NPTOT, D], F32, kind="Internal").ap()
    xt_out = dt("xt_out", [NPL, D], F32, kind="Internal").ap()
    yt_in = dt("yt_in", [NPL, D], F32, kind="Internal").ap()
    yt_out = dt("yt_out", [NPTOT, D], F32, kind="Internal").ap()

    rg_all = [list(range(ncores))]
    rg_pair = [[2 * i, 2 * i + 1] for i in range(ncores // 2)]

    es = ExitStack()
    AR = es.enter_context(nc.sbuf_tensor("AR", [128, ARN], F32))
    SM = es.enter_context(nc.sbuf_tensor("SMALL", [128, 4096], F32))
    PS = [es.enter_context(nc.psum_tensor(f"ps{i}", [128, 512], F32))
          for i in range(8)]
    sem_names = ["ld", "ldw", "ldw2", "ldxa", "ldxb", "ldya", "ldyb", "ldz",
                 "te", "ve", "se", "sevac", "vevac", "gp",
                 "sta", "stb", "stc", "sto", "cc", "ldg"] + [f"eos_{e}" for e in ENGINES]
    sems = {n: es.enter_context(nc.semaphore(n)) for n in sem_names}
    block = es.enter_context(nc.Block())

    def A(off, n, p0=0, p1=128):
        assert 0 <= off and off + n <= ARN, (off, n)
        return AR[p0:p1, off:off + n]

    P = Prog()

    ones_c = SM[:, 0:1]
    lng_c = SM[:, 1:2]
    g512_c = SM[:, 2:3]
    hflag_c = SM[:, 3:4]
    osc_c = SM[:, 4:5]
    hfi_c = SM[:, 12:13]
    kscale = SM[:, 8:12]
    b1_c = SM[:, 16:24]
    mw0_c = SM[:, 24:32]
    sump = SM[:, 32:40]
    corr = SM[:, 48:56]
    mur_c = SM[:, 56:64]
    murI_c = SM[:, 64:72]
    muk_c = SM[:, 72:80]
    mukI_c = SM[:, 80:88]
    selpair_r = SM[:, 88:88 + ncores]
    selown_r = SM[:, 96:96 + ncores]
    prevcol = SM[:, 104:112]
    KT = SM[:, 128:128 + NCH * MEM]
    STT = SM[:, 448:448 + 64]
    SCR1 = SM[:, 512:513]
    SCR2 = SM[:, 513:514]
    SCR3 = SM[:, 514:515]
    SD64 = SM[:, 516:580]
    SD32 = SM[:, 580:612]
    onesrow = SM[0:1, 640:768]
    cells = SM[0:1, 768:784]
    selown_cl = SM[0:1, 784:784 + ncores]
    selpair_cl = SM[0:1, 784 + ncores:784 + 2 * ncores]
    hmrow = SM[0:1, 884:1012]
    b2row = SM[0:1, 1024:2048]
    phibrow = SM[0:1, 2048:4096]

    IDT = A(47104, 128)
    LNDUMP = A(47232, 1024)
    CEPS = SM[:, 5:6]
    CLN32 = SM[:, 6:7]
    nc.const_aps.register(F32, EPS, CEPS)
    nc.const_aps.register(F32, MLN32, CLN32)

    P.dma("sync", IDT, ident[:, :], "ld")
    STG1 = A(8192, 1024)
    STG2 = A(9216, 1024)
    P.dma("sync", STG1[0:MEM, :], mw[1:, :], "ld")
    P.dma("sync", STG2[0:1, :], b1.unsqueeze(0), "ld")
    P.dma("sync", STG2[1:2, :], mw[0:1, :], "ld")
    P.dma("sync", STG2[2:3, :], mu_r.unsqueeze(0), "ld")
    P.dma("sync", STG2[3:4, :], mu_k.unsqueeze(0), "ld")
    P.dma("sync", kscale, ecol[:, :], "ld")
    P.dma("sync", cells[:, 0:1], dlogit.unsqueeze(0), "ld")
    P.dma("sync", cells[:, 1:2], oscale.unsqueeze(0), "ld")
    P.dma("sync", cells[:, 2:3], hflag.unsqueeze(0), "ld")
    P.dma("sync", selown_cl, selown[:, :], "ld")
    P.dma("sync", selpair_cl, selpair[:, :], "ld")
    P.dma("sync", b2row, b2[:, :], "ld")
    P.dma("sync", phibrow, phib[:, :], "ld")
    ldv = P.at("ld")

    P.op("vector", lambda v: v.memset(ones_c, 1.0), inc="ve")
    P.op("vector", lambda v: v.memset(CEPS, EPS), inc="ve")
    P.op("vector", lambda v: v.memset(CLN32, MLN32), inc="ve")
    vones = P.op("vector", lambda v: v.memset(onesrow, 1.0), inc="ve")
    sev = P.op("scalar", lambda s: s.activation(cells[:, 3:4], cells[:, 0:1],
                                                ACTF.Sigmoid),
               wait=[("ld", ldv)], inc="se")
    sev = P.op("scalar", lambda s: s.activation(cells[:, 4:5], cells[:, 3:4],
                                                ACTF.Ln),
               wait=[("se", sev)], inc="se")
    te = P.mm_group(0, PS[0][:, 0:5], [(onesrow, cells[0:1, 0:5])],
                    wait=[("se", sev), ("ve", vones)])
    P.op("vector", lambda v: v.tensor_copy(osc_c, PS[0][:, 1:2]),
         wait=[("te", te)], inc="vevac")
    P.op("vector", lambda v: v.tensor_copy(hflag_c, PS[0][:, 2:3]), inc="vevac")
    ev = P.op("vector", lambda v: v.tensor_copy(lng_c, PS[0][:, 4:5]), inc="vevac")
    P.release(0, "vevac", ev)
    te = P.mm_group(1, PS[1][:, 0:2 * ncores],
                    [(onesrow, SM[0:1, 784:784 + 2 * ncores])],
                    wait=[("ve", vones), ("ld", ldv)])
    ev = P.op("vector", lambda v: v.tensor_copy(
        SM[:, 96:96 + 2 * ncores], PS[1][:, 0:2 * ncores]),
        wait=[("te", te)], inc="vevac")
    P.release(1, "vevac", ev)
    ev = P.op("vector", lambda v: v.tensor_copy(
        selpair_r, SM[:, 96 + ncores:96 + 2 * ncores]), inc="vevac")
    sev = P.op("scalar", lambda s: s.activation(g512_c, lng_c, ACTF.Exp,
                                                scale=512.0),
               wait=[("vevac", ev)], inc="se")
    sev = P.op("scalar", lambda s: s.activation(kscale, kscale, ACTF.Exp,
                                                scale=lng_c), inc="se")
    for c in range(NCH):
        b = 2 + (c % 2)
        te = P.op("tensor", lambda t, c=c, b=b: t.transpose(
            PS[b][:, 0:MEM], STG1[0:MEM, c * 128:(c + 1) * 128], IDT[0:MEM, 0:MEM]),
            wait=([("ld", ldv)] if c == 0 else []) + P.guard(b), inc="te")
        ev = P.op("vector", lambda v, c=c, b=b: v.tensor_copy(
            KT[:, c * MEM:(c + 1) * MEM], PS[b][:, 0:MEM]),
            wait=[("te", te)], inc="vevac")
        P.release(b, "vevac", ev)
        b = 4 + (c % 2)
        te = P.op("tensor", lambda t, c=c, b=b: t.transpose(
            PS[b][:, 0:4], STG2[0:4, c * 128:(c + 1) * 128], IDT[0:4, 0:4]),
            wait=P.guard(b), inc="te")
        for dst_i, dstc in enumerate((b1_c, mw0_c, mur_c, muk_c)):
            ev = P.op("vector", lambda v, c=c, b=b, i=dst_i, d=dstc:
                      v.tensor_copy(d[:, c:c + 1], PS[b][:, i:i + 1]),
                      wait=[("te", te)] if dst_i == 0 else (), inc="vevac")
        P.release(b, "vevac", ev)
    for c in range(NCH):
        P.op("vector", lambda v, c=c: v.tensor_scalar(
            KT[:, c * MEM:(c + 1) * MEM], KT[:, c * MEM:(c + 1) * MEM],
            mw0_c[:, c:c + 1], None, ALU.subtract),
            wait=[("vevac", P.at("vevac"))] if c == 0 else (), inc="ve")
    P.op("vector", lambda v: v.tensor_scalar(murI_c, mur_c, -1.0, 1.0,
                                             ALU.mult, ALU.add),
         wait=[("vevac", P.at("vevac"))], inc="ve")
    P.op("vector", lambda v: v.tensor_scalar(mukI_c, muk_c, -1.0, 1.0,
                                             ALU.mult, ALU.add), inc="ve")
    P.op("vector", lambda v: v.tensor_scalar(hfi_c, hflag_c, -1.0, 1.0,
                                             ALU.mult, ALU.add), inc="ve")
    P.op("vector", lambda v: v.memset(sump, 0.0), inc="ve")
    P.barrier()

    def ln_chunk(src, dst, slot, ldwait=(), pr=128, wait2=()):
        st = STT[0:pr, slot * 4: slot * 4 + 4]
        mean, var, rstd, nmr = (st[:, 0:1], st[:, 1:2], st[:, 2:3], st[:, 3:4])
        tA = SCR1[0:pr]
        tB = SCR2[0:pr]
        def vw(fn):
            return P.op("vector", fn, wait=[("ve", P.at("ve"))], inc="ve")
        P.op("vector", lambda v: v.reduce_sum(mean, src, axis=AX.X),
             wait=ldwait, inc="ve")
        sev = P.op("scalar", lambda s: s.activation(LNDUMP[0:pr, :], src,
                                                    ACTF.Square),
                   wait=ldwait, inc="se")
        P.op("vector", lambda v: v.reduce_sum(var, LNDUMP[0:pr, :], axis=AX.X),
             wait=[("se", sev)], inc="ve")
        vw(lambda v: v.tensor_scalar(var, var, 1.0 / D, None, ALU.mult))
        vw(lambda v: v.tensor_scalar(mean, mean, 1.0 / D, None, ALU.mult))
        vw(lambda v: v.tensor_mul(tA, mean, mean))
        vev = vw(lambda v: v.tensor_sub(var, var, tA))
        sev = P.op("scalar", lambda s: s.activation(rstd, var, ACTF.Sqrt,
                                                    bias=EPS),
                   wait=[("ve", vev)], inc="se")
        P.op("vector", lambda v: v.reciprocal(rstd, rstd),
             wait=[("se", sev)], inc="ve")
        vw(lambda v: v.tensor_mul(nmr, mean, rstd))
        vev = vw(lambda v: v.tensor_scalar(nmr, nmr, -1.0, None, ALU.mult))
        sev = P.op("scalar", lambda s: s.activation(dst, src, ACTF.Identity,
                                                    bias=nmr, scale=rstd),
                   wait=[("ve", vev)] + list(wait2), inc="se")
        return sev

    def tr_tile(src_off, dst_off, rows, colw, wait=(), banks=(2, 3),
                scale=None):
        last = None
        for c in range(NCH):
            for r in range(rows):
                b = banks[(c * rows + r) % len(banks)]
                w = list(wait) if (c == 0 and r == 0) else []
                te = P.op("tensor", lambda t, c=c, r=r, b=b: t.transpose(
                    PS[b][:, 0:128], A(src_off + r * D + c * 128, 128), IDT),
                    wait=w + P.guard(b), inc="te")
                if scale is None:
                    last = P.op("vector",
                                lambda v, c=c, r=r, b=b: v.tensor_copy(
                                    A(dst_off + c * colw + r * 128, 128),
                                    PS[b][:, 0:128]),
                                wait=[("te", te)], inc="vevac")
                    P.release(b, "vevac", last)
                else:
                    last = P.op("scalar",
                                lambda s, c=c, r=r, b=b: s.activation(
                                    A(dst_off + c * colw + r * 128, 128),
                                    PS[b][:, 0:128], ACTF.Identity,
                                    scale=scale(r)),
                                wait=[("te", te)], inc="sevac")
                    P.release(b, "sevac", last)
        return last

    def bfsplit(src_off, n, hi_off, lo_off, wait=(), pr=128):
        hi = A(hi_off, n // 2, 0, pr).bitcast(BF16)
        lo = A(lo_off, n // 2, 0, pr).bitcast(BF16)
        sf = A(src_off, n, 0, pr)
        P.op("vector", lambda v: v.tensor_copy(hi, sf), wait=wait, inc="ve")
        return P.op("vector", lambda v: v.tensor_sub(lo, sf, hi),
                    wait=[("ve", P.at("ve"))], inc="ve")

    def bfap(off, n, pr=128):
        return A(off, (n + 1) // 2, 0, pr).bitcast(BF16)[:, 0:n]

    W1S, XT0, XT1, LNT0, LNT1, LNFM, PFM, HMR = (
        0, 8192, 12288, 16384, 20480, 24576, 28672, 32768)
    XTb, LNTb = [XT0, XT1], [LNT0, LNT1]
    ldwv = P.dma("sync", A(W1S, 8 * D).rearrange("p (c e) -> p c e", c=NCH),
                 w1.rearrange("(c p) e -> p c e", p=128), "ldw")
    P.dma("sync", hmrow, hmask[:, :], "ld")
    te = P.mm_group(2, PS[2][:, 0:HALO], [(onesrow, hmrow)],
                    wait=[("ve", vones), ("ld", P.at("ld"))])
    ev = P.op("vector", lambda v: v.tensor_copy(A(HMR, HALO), PS[2][:, 0:HALO]),
              wait=[("te", te)], inc="vevac")
    P.release(2, "vevac", ev)

    xfree = [0, 0]
    s1a_stgate = [0]
    for it in range(NT + 1):
        par = it % 2
        rows = 1 if it == 0 else 4
        roff = 0 if it == 0 else HALO + (it - 1) * TT
        xoff, lnoff = XTb[par], LNTb[par]
        ldx = "ldxa" if par == 0 else "ldxb"
        P.dma("sync", A(xoff, rows * D).rearrange("p (r d) -> p r d", r=rows),
              x_in.rearrange("(r p) d -> p r d", p=128)[
                  :, roff // 128:roff // 128 + rows, :],
              ldx, wait=[("se", xfree[par])])
        lw = [(ldx, P.at(ldx))]
        se_ln = 0
        for r in range(rows):
            se_ln = ln_chunk(A(xoff + r * D, D), A(lnoff + r * D, D), r,
                             ldwait=lw if r == 0 else ())
        xfree[par] = se_ln
        colw = rows * 128
        tr_tile(lnoff, LNFM, rows, colw, wait=[("se", se_ln)])
        vtr = P.at("vevac")
        for m in range(NCH):
            bank = 4 + (m % 2)
            te = P.mm_group(bank, PS[bank][:, 0:colw],
                            [(A(W1S + k * D + m * 128, 128),
                              A(LNFM + k * colw, colw)) for k in range(NCH)],
                            wait=[("vevac", vtr), ("ldw", ldwv)])
            ev = P.op("scalar", lambda s, m=m, bank=bank, colw=colw:
                      s.activation(A(PFM + m * colw, colw),
                                   PS[bank][:, 0:colw], ACTF.Identity,
                                   bias=b1_c[:, m:m + 1]),
                      wait=[("te", te)] +
                           ([("sta", s1a_stgate[0])] if m == 0 else []),
                      inc="sevac")
            P.release(bank, "sevac", ev)
        sev_evac = P.at("sevac")
        if it == 0:
            vv = 0
            for m in range(NCH):
                vv = P.op("vector", lambda v, m=m: v.tensor_mul(
                    A(PFM + m * 128, 128), A(PFM + m * 128, 128),
                    A(HMR, HALO)),
                    wait=[("sevac", sev_evac)] if m == 0 else (), inc="ve")
            st = P.dma("gpsimd",
                       p_scr.rearrange("(c p) t -> p c t", p=128)[:, :, 0:HALO],
                       A(PFM, NCH * 128).rearrange("p (c t) -> p c t", c=NCH),
                       "sta", wait=[("ve", vv)])
        else:
            for m in range(NCH):
                P.op("vector", lambda v, m=m, colw=colw: v.reduce_sum(
                    SCR3, A(PFM + m * colw, colw), axis=AX.X),
                    wait=[("sevac", sev_evac)] if m == 0 else (), inc="ve")
                P.op("vector", lambda v, m=m: v.tensor_add(
                    sump[:, m:m + 1], sump[:, m:m + 1], SCR3),
                    wait=[("ve", P.at("ve"))], inc="ve")
            st = P.dma("gpsimd",
                       p_scr.rearrange("(c p) t -> p c t", p=128)[
                           :, :, roff:roff + colw],
                       A(PFM, NCH * colw).rearrange("p (c t) -> p c t", c=NCH),
                       "sta", wait=[("ve", P.at("ve"))])
        s1a_stgate[0] = st

    if dbg:
        P.dma("gpsimd", dbg_ln[:, :], A(LNTb[NT % 2], 4 * D), "stb",
              wait=[("se", P.at("se"))])
        P.dma("gpsimd", dbg_fm[:, :], A(LNFM, 4096), "stb",
              wait=[("vevac", P.at("vevac"))])
    Z = A(33024, 8 * ncores)
    vv = P.op("vector", lambda v: v.memset(Z, 0.0), inc="ve")
    for c8 in range(ncores):
        vv = P.op("vector", lambda v, c8=c8: v.scalar_tensor_tensor(
            Z[:, 8 * c8:8 * c8 + 8], sump, selown_r[:, c8:c8 + 1],
            Z[:, 8 * c8:8 * c8 + 8], ALU.mult, ALU.add),
            wait=[("ve", P.at("ve"))] if c8 == 0 else (), inc="ve")
    st = P.dma("gpsimd", ar_in[:, :], Z, "stb", wait=[("ve", vv)])
    ccv = P.op("gpsimd", lambda g: g.collective_compute(
        "AllReduce", ALU.add, replica_groups=rg_all,
        ins=[ar_in[:, :]], outs=[ar_out[:, :]]),
        wait=[("stb", st)], inc="cc")
    lz = P.dma("gpsimd", Z, ar_out[:, :], "ldg", wait=[("cc", ccv)])
    vv = P.op("vector", lambda v: v.memset(corr, 0.0),
              wait=[("ldg", lz)], inc="ve")
    for c8 in range(ncores):
        vv = P.op("vector", lambda v, c8=c8: v.scalar_tensor_tensor(
            corr, Z[:, 8 * c8:8 * c8 + 8], selpair_r[:, c8:c8 + 1], corr,
            ALU.mult, ALU.add), wait=[("ve", P.at("ve"))], inc="ve")
    vv = P.op("vector", lambda v: v.tensor_mul(corr, corr, mw0_c),
              wait=[("ve", P.at("ve"))], inc="ve")
    P.barrier()

    if stop_stage < 1:
        P.emit(block, sems)
        es.close()
        return nc

    W2S = 0
    PT = 8192
    QFM = 13312
    XTq0, XTq1 = 17408, 21504
    AOT = 25600
    R1T = 29696
    ldwv = P.dma("sync", A(W2S, 8 * D).rearrange("p (c e) -> p c e", c=NCH),
                 w2.rearrange("(c p) e -> p c e", p=128), "ldw")

    def s1b_tile(it):
        if it < 0:
            cw, roff = HB, HALO - HB
        else:
            cw, roff = TT, HALO + it * TT
        hw = cw + MEM + 1
        par = it % 2
        ldx = "ldxa" if par == 0 else "ldxb"
        xoff = XTq0 if par == 0 else XTq1
        rows = cw // 128 if cw >= 128 else 1
        te_prev = P.at("te")
        P.dma("sync", A(PT, NCH * 560).rearrange("p (c w) -> p c w", w=560)[
            :, :, 0:hw],
            p_scr.rearrange("(c p) t -> p c t", p=128)[
                :, :, roff - MEM - 1:roff + cw],
            "ldya", wait=[("te", s1b_tfree[0]), ("ve", s1b_vfree[0])])
        lpy = P.at("ldya")
        xw = [("se", s1b_xfree[par]), ("ve", s1b_xvfree[par])]
        if cw < 128:
            P.dma("sync", A(xoff, D, 0, cw), x_in[roff:roff + cw, :],
                  ldx, wait=xw)
        else:
            P.dma("sync", A(xoff, rows * D).rearrange("p (r d) -> p r d", r=rows),
                  x_in.rearrange("(r p) d -> p r d", p=128)[
                      :, roff // 128:roff // 128 + rows, :],
                  ldx, wait=xw)
        lpx = P.at(ldx)
        vv = 0
        for c in range(NCH):
            base = PT + c * 560
            P.op("vector", lambda v, c=c, base=base: v.tensor_scalar(
                A(QFM + c * cw, cw), A(base + MEM + 1, cw),
                KT[:, c * MEM:c * MEM + 1], None, ALU.mult),
                wait=[("ldya", lpy)] if c == 0 else (), inc="ve")
            for j in range(1, MEM):
                P.op("vector", lambda v, c=c, j=j, base=base:
                     v.scalar_tensor_tensor(
                         A(QFM + c * cw, cw), A(base + MEM + 1 - j, cw),
                         KT[:, c * MEM + j:c * MEM + j + 1],
                         A(QFM + c * cw, cw), ALU.mult, ALU.add), inc="ve")
            vv = P.op("vector", lambda v, c=c: v.tensor_scalar(
                A(QFM + c * cw, cw), A(QFM + c * cw, cw),
                corr[:, c:c + 1], None, ALU.add), inc="ve")
        s1b_vfree[0] = vv
        nts = max(1, cw // 128)
        sev_all = 0
        for ts in range(nts):
            pr = min(cw, 128)
            for b2i in range(2):
                bank = 4 + ((ts * 2 + b2i) % 2)
                mms = [(A(QFM + k * cw + ts * 128, pr),
                        A(W2S + k * D + b2i * 512, 512)) for k in range(NCH)]
                mms.append((onesrow[0:1, 0:pr],
                            b2row[0:1, b2i * 512:b2i * 512 + 512]))
                te = P.mm_group(bank, PS[bank][0:pr, :], mms,
                                wait=[("ve", vv), ("ldw", ldwv),
                                      (ldx, lpx)])
                ev = P.op("scalar", lambda s, ts=ts, b2i=b2i, bank=bank, pr=pr:
                          s.copy(A(AOT + ts * D + b2i * 512, 512, 0, pr),
                                 PS[bank][0:pr, :]),
                          wait=[("te", te)], inc="sevac")
                P.release(bank, "sevac", ev)
        sv = P.at("sevac")
        s1b_tfree[0] = P.at("te")
        for ts in range(nts):
            pr = min(cw, 128)
            se_ln = ln_chunk(A(AOT + ts * D, D, 0, pr),
                             A(R1T + ts * D, D, 0, pr), ts,
                             ldwait=[("sevac", sv)] if ts == 0 else (), pr=pr,
                             wait2=[("stc", s1b_stgate[0])] if ts == 0 else ())
            vv2 = P.op("vector", lambda v, ts=ts, pr=pr: v.tensor_add(
                A(R1T + ts * D, D, 0, pr), A(R1T + ts * D, D, 0, pr),
                A(xoff + (ts * D if cw >= 128 else 0), D, 0, pr)),
                wait=[("se", se_ln)], inc="ve")
        s1b_xfree[par] = P.at("se")
        s1b_xvfree[par] = P.at("ve")
        if cw >= 128:
            st = P.dma("gpsimd", r1_scr.rearrange("(r p) d -> p r d", p=128)[
                :, roff // 128:roff // 128 + rows, :],
                A(R1T, rows * D).rearrange("p (r d) -> p r d", r=rows),
                "stc", wait=[("ve", P.at("ve"))])
        else:
            st = P.dma("gpsimd", r1_scr[roff:roff + cw, :],
                       A(R1T, D, 0, cw), "stc", wait=[("ve", P.at("ve"))])
        s1b_stgate[0] = st

    s1b_tfree = [0]
    s1b_vfree = [0]
    s1b_xfree = [0, 0]
    s1b_xvfree = [0, 0]
    s1b_stgate = [0]
    s1b_tile(-1)
    for it in range(NT):
        s1b_tile(it)
    P.barrier()

    if stop_stage < 2:
        P.emit(block, sems)
        es.close()
        return nc

    WRS, WKS, WVS = 0, 8192, 16384
    R1I0, R1I1 = 24576, 26624
    LNT20, LNT21 = 28672, 30720
    LNF2 = 32768
    XRB, XKB = 34816, 36864
    RSB, KAB = 38912, 40960
    R2T = 43008
    T2 = 256
    NT2 = tloc // T2
    ldwv = P.dma("sync", A(WRS, 8 * D).rearrange("p (c e) -> p c e", c=NCH),
                 wrT.rearrange("(c p) e -> p c e", p=128), "ldw")
    ldwv = P.dma("sync", A(WKS, 8 * D).rearrange("p (c e) -> p c e", c=NCH),
                 wkT.rearrange("(c p) e -> p c e", p=128), "ldw")
    ldwv = P.dma("sync", A(WVS, 8 * D).rearrange("p (c e) -> p c e", c=NCH),
                 wvT.rearrange("(c p) e -> p c e", p=128), "ldw")
    P.dma("sync", A(XRB, D, 0, 1), r1_scr[HALO - 1:HALO, :], "ldz")
    se_ln = ln_chunk(A(XRB, D, 0, 1), A(XKB, D, 0, 1), 0,
                     ldwait=[("ldz", P.at("ldz"))], pr=1)
    for c in range(NCH):
        te = P.op("tensor", lambda t, c=c: t.transpose(
            PS[2][:, 0:1], A(XKB + c * 128, 128, 0, 1), IDT[0:1, 0:1]),
            wait=([("se", se_ln)] if c == 0 else []) + P.guard(2), inc="te")
        ev = P.op("vector", lambda v, c=c: v.tensor_copy(
            prevcol[:, c:c + 1], PS[2][:, 0:1]), wait=[("te", te)], inc="vevac")
        P.release(2, "vevac", ev)
    vv = P.op("vector", lambda v: v.tensor_scalar(
        prevcol, prevcol, hflag_c[:, 0:1], None, ALU.mult),
        wait=[("vevac", P.at("vevac"))], inc="ve")

    s2_xfree = [0, 0]
    s2_vfree = [0, 0]
    s2_stgate = [0]
    for it in range(NT2):
        par = it % 2
        roff = HALO + it * T2
        r1o = R1I0 if par == 0 else R1I1
        lno = LNT20 if par == 0 else LNT21
        ldx = "ldxa" if par == 0 else "ldxb"
        P.dma("sync", A(r1o, 2 * D).rearrange("p (r d) -> p r d", r=2),
              r1_scr.rearrange("(r p) d -> p r d", p=128)[
                  :, roff // 128:roff // 128 + 2, :],
              ldx, wait=[("se", s2_xfree[par]), ("ve", s2_vfree[par])])
        lw = [(ldx, P.at(ldx))]
        se_ln = 0
        for r in range(2):
            se_ln = ln_chunk(A(r1o + r * D, D), A(lno + r * D, D), r,
                             ldwait=lw if r == 0 else ())
        s2_xfree[par] = se_ln
        tr_tile(lno, LNF2, 2, T2, wait=[("se", se_ln)])
        vtr = P.at("vevac")
        vv = 0
        for c in range(NCH):
            P.op("vector", lambda v, c=c: v.tensor_scalar(
                A(XRB + c * T2, T2), A(LNF2 + c * T2, T2),
                mur_c[:, c:c + 1], None, ALU.mult),
                wait=([("vevac", vtr), ("stc", s2_stgate[0])]
                      if c == 0 else ()), inc="ve")
            P.op("vector", lambda v, c=c: v.scalar_tensor_tensor(
                A(XRB + c * T2 + 1, T2 - 1), A(LNF2 + c * T2, T2 - 1),
                murI_c[:, c:c + 1], A(XRB + c * T2 + 1, T2 - 1),
                ALU.mult, ALU.add), inc="ve")
            P.op("vector", lambda v, c=c: v.scalar_tensor_tensor(
                A(XRB + c * T2, 1), prevcol[:, c:c + 1],
                murI_c[:, c:c + 1], A(XRB + c * T2, 1),
                ALU.mult, ALU.add), inc="ve")
            P.op("vector", lambda v, c=c: v.tensor_scalar(
                A(XKB + c * T2, T2), A(LNF2 + c * T2, T2),
                muk_c[:, c:c + 1], None, ALU.mult), inc="ve")
            P.op("vector", lambda v, c=c: v.scalar_tensor_tensor(
                A(XKB + c * T2 + 1, T2 - 1), A(LNF2 + c * T2, T2 - 1),
                mukI_c[:, c:c + 1], A(XKB + c * T2 + 1, T2 - 1),
                ALU.mult, ALU.add), inc="ve")
            vv = P.op("vector", lambda v, c=c: v.scalar_tensor_tensor(
                A(XKB + c * T2, 1), prevcol[:, c:c + 1],
                mukI_c[:, c:c + 1], A(XKB + c * T2, 1),
                ALU.mult, ALU.add), inc="ve")
        for c in range(NCH):
            vv = P.op("vector", lambda v, c=c: v.tensor_copy(
                prevcol[:, c:c + 1], A(LNF2 + c * T2 + T2 - 1, 1)), inc="ve")
        for ts in range(2):
            bank = 0 + (ts % 2)
            for b2i in range(2):
                bank = (ts * 2 + b2i) % 2
                mms = [(A(XRB + k * T2 + ts * 128, 128),
                        A(WRS + k * D + b2i * 512, 512)) for k in range(NCH)]
                te = P.mm_group(bank, PS[bank][:, :], mms,
                                wait=[("ve", vv), ("ldw", ldwv)])
                ev = P.op("scalar", lambda s, ts=ts, b2i=b2i, bank=bank:
                          s.activation(A(RSB + ts * D + b2i * 512, 512),
                                       PS[bank][:, :], ACTF.Sigmoid),
                          wait=[("te", te)], inc="sevac")
                P.release(bank, "sevac", ev)
        for m in range(NCH):
            bank = 2 + (m % 2)
            mms = [(A(WKS + k * D + m * 128, 128),
                    A(XKB + k * T2, T2)) for k in range(NCH)]
            te = P.mm_group(bank, PS[bank][:, 0:T2], mms,
                            wait=[("ve", vv), ("ldw", ldwv)])
            ev = P.op("scalar", lambda s, m=m, bank=bank: s.activation(
                A(KAB + m * T2, T2), PS[bank][:, 0:T2], ACTF.Relu),
                wait=[("te", te)], inc="sevac")
            P.release(bank, "sevac", ev)
            P.op("scalar", lambda s, m=m: s.activation(
                A(KAB + m * T2, T2), A(KAB + m * T2, T2), ACTF.Square),
                wait=[("sevac", P.at("sevac"))], inc="sevac")
        sv = P.at("sevac")
        for ts in range(2):
            for b2i in range(2):
                bank = 4 + ((ts * 2 + b2i) % 2)
                mms = [(A(KAB + k * T2 + ts * 128, 128),
                        A(WVS + k * D + b2i * 512, 512)) for k in range(NCH)]
                te = P.mm_group(bank, PS[bank][:, :], mms, wait=[("sevac", sv)])
                ev = P.op("vector", lambda v, ts=ts, b2i=b2i, bank=bank:
                          v.tensor_mul(A(R2T + ts * D + b2i * 512, 512),
                                       A(RSB + ts * D + b2i * 512, 512),
                                       PS[bank][:, :]),
                          wait=[("te", te)], inc="vevac")
                P.release(bank, "vevac", ev)
        vv2 = P.at("vevac")
        for ts in range(2):
            se_ln = ln_chunk(A(R2T + ts * D, D), A(XRB + ts * D, D), ts + 2,
                             ldwait=[("vevac", vv2)] if ts == 0 else ())
            P.op("vector", lambda v, ts=ts, r1o=r1o: v.tensor_add(
                A(XRB + ts * D, D), A(XRB + ts * D, D), A(r1o + ts * D, D)),
                wait=[("se", se_ln)], inc="ve")
        s2_vfree[par] = P.at("ve")
        st = P.dma("gpsimd", r2_scr.rearrange("(r p) d -> p r d", p=128)[
            :, it * 2:it * 2 + 2, :],
            A(XRB, 2 * D).rearrange("p (r d) -> p r d", r=2),
            "stc", wait=[("ve", P.at("ve"))])
        s2_stgate[0] = st
    P.barrier()

    if stop_stage < 3:
        P.emit(block, sems)
        es.close()
        return nc

    WQS, WKS3, WVS3 = 0, 8192, 16384
    R2I0, R2I1 = 24576, 26624
    LNT30, LNT31 = 28672, 30720
    LNF3 = 32768
    QFO, KFO, VTO = 34816, 36864, 38912
    ldwv = P.dma("sync", A(WQS, 8 * D).rearrange("p (c e) -> p c e", c=NCH),
                 wqTa.rearrange("(c p) e -> p c e", p=128), "ldw")
    ldwv = P.dma("sync", A(WKS3, 8 * D).rearrange("p (c e) -> p c e", c=NCH),
                 wkTa.rearrange("(c p) e -> p c e", p=128), "ldw")
    ldwv = P.dma("sync", A(WVS3, 8 * D).rearrange("p (c e) -> p c e", c=NCH),
                 wvTa.rearrange("(c p) e -> p c e", p=128), "ldw")
    s3_xfree = [0, 0]
    s3_sfree = [0]
    for it in range(NT2):
        par = it % 2
        r2o = R2I0 if par == 0 else R2I1
        lno = LNT30 if par == 0 else LNT31
        ldx = "ldxa" if par == 0 else "ldxb"
        P.dma("sync", A(r2o, 2 * D).rearrange("p (r d) -> p r d", r=2),
              r2_scr.rearrange("(r p) d -> p r d", p=128)[
                  :, it * 2:it * 2 + 2, :],
              ldx, wait=[("se", s3_xfree[par])])
        lw = [(ldx, P.at(ldx))]
        se_ln = 0
        for r in range(2):
            se_ln = ln_chunk(A(r2o + r * D, D), A(lno + r * D, D), r,
                             ldwait=lw if r == 0 else ())
        s3_xfree[par] = se_ln
        tr_tile(lno, LNF3, 2, T2, wait=[("se", se_ln), ("sta", s3_sfree[0])])
        vtr = P.at("vevac")
        for dst, ws in ((QFO, WQS), (KFO, WKS3)):
            for m in range(NCH):
                bank = (m % 2) + (0 if ws == WQS else 2)
                mms = [(A(ws + k * D + m * 128, 128),
                        A(LNF3 + k * T2, T2)) for k in range(NCH)]
                te = P.mm_group(bank, PS[bank][:, 0:T2], mms,
                                wait=[("vevac", vtr), ("ldw", ldwv)])
                ev = P.op("scalar", lambda s, dst=dst, m=m, bank=bank:
                          s.copy(A(dst + m * T2, T2), PS[bank][:, 0:T2]),
                          wait=[("te", te)], inc="sevac")
                P.release(bank, "sevac", ev)
        for ts in range(2):
            for b2i in range(2):
                bank = 4 + ((ts * 2 + b2i) % 2)
                mms = [(A(LNF3 + k * T2 + ts * 128, 128),
                        A(WVS3 + k * D + b2i * 512, 512)) for k in range(NCH)]
                te = P.mm_group(bank, PS[bank][:, :], mms,
                                wait=[("vevac", vtr), ("ldw", ldwv)])
                ev = P.op("scalar", lambda s, ts=ts, b2i=b2i, bank=bank:
                          s.copy(A(VTO + ts * D + b2i * 512, 512),
                                 PS[bank][:, :]),
                          wait=[("te", te)], inc="sevac")
                P.release(bank, "sevac", ev)
        sv = P.at("sevac")
        st = P.dma("gpsimd", q_scr.rearrange("(c p) t -> p c t", p=128)[
            :, :, it * T2:(it + 1) * T2],
            A(QFO, NCH * T2).rearrange("p (c t) -> p c t", c=NCH),
            "sta", wait=[("sevac", sv)])
        st = P.dma("gpsimd", k_scr.rearrange("(c p) t -> p c t", p=128)[
            :, :, it * T2:(it + 1) * T2],
            A(KFO, NCH * T2).rearrange("p (c t) -> p c t", c=NCH),
            "sta", wait=[])
        st = P.dma("gpsimd", v_scr.rearrange("(r p) d -> p r d", p=128)[
            :, it * 2:it * 2 + 2, :],
            A(VTO, 2 * D).rearrange("p (r d) -> p r d", r=2),
            "sta", wait=[])
        s3_sfree[0] = st
    P.barrier()

    if stop_stage < 4:
        P.emit(block, sems)
        es.close()
        return nc

    SS = 0
    QF3, KF3, KTM, VT3, QT3 = 8192, 12288, 16384, 20480, 24576
    SC3, WI3, G1B = 28672, 30720, 32768
    RT3 = 33280
    NC3 = tloc // 512
    P.dma("sync", A(WI3, 4 * 512).rearrange("p (s t) -> p s t", s=4),
          expo512.rearrange("(s p) t -> p s t", p=128), "ldz")
    P.dma("sync", A(G1B, 512), arange_rep[:, tloc - 512:tloc], "ldz")
    lz = P.at("ldz")
    sv = P.op("scalar", lambda s: s.activation(
        A(WI3, 4 * 512), A(WI3, 4 * 512), ACTF.Exp, bias=MLN32,
        scale=lng_c), wait=[("ldz", lz)], inc="se")
    sv = P.op("scalar", lambda s: s.activation(
        A(G1B, 512), A(G1B, 512), ACTF.Exp, bias=MLN32,
        scale=lng_c), inc="se")
    vv = P.op("vector", lambda v: v.memset(A(SS, 8 * D), 0.0), inc="ve")
    s3b_stgate = [0]
    if dbg:
        P.dma("gpsimd", dbg_ss[:, :], A(SS, 8 * D), "stb",
              wait=[("ve", vv)])
    s3b_free = [0, 0, 0]
    for ci in reversed(range(NC3)):
        tprev, vprev, sprev = s3b_free
        P.dma("sync", A(QF3, 8 * 512).rearrange("p (c t) -> p c t", c=NCH),
              q_scr.rearrange("(c p) t -> p c t", p=128)[
                  :, :, ci * 512:(ci + 1) * 512],
              "ldxa", wait=[("te", tprev), ("ve", vprev)])
        P.dma("sync", A(KF3, 8 * 512).rearrange("p (c t) -> p c t", c=NCH),
              k_scr.rearrange("(c p) t -> p c t", p=128)[
                  :, :, ci * 512:(ci + 1) * 512],
              "ldxb", wait=[("te", tprev), ("sevac", sprev)])
        P.dma("sync", A(VT3, 4 * D).rearrange("p (r d) -> p r d", r=4),
              v_scr.rearrange("(r p) d -> p r d", p=128)[
                  :, ci * 4:ci * 4 + 4, :],
              "ldya", wait=[("te", tprev)])
        lq, lk, lv = P.at("ldxa"), P.at("ldxb"), P.at("ldya")
        for tk in range(4):
            bank = tk % 2
            mms = [(A(KF3 + c * 512 + tk * 128, 128), A(QF3 + c * 512, 512))
                   for c in range(NCH)]
            te = P.mm_group(bank, PS[bank][:, :], mms,
                            wait=[("ldxa", lq), ("ldxb", lk), ("se", sv)])
            ev = P.op("vector", lambda v, tk=tk, bank=bank: v.tensor_mul(
                A(SC3 + tk * 512, 512), PS[bank][:, :], A(WI3 + tk * 512, 512)),
                wait=[("te", te)], inc="vevac")
            P.release(bank, "vevac", ev)
        for c in range(NCH):
            P.op("vector", lambda v, c=c: v.tensor_mul(
                A(QT3 + c * 512, 512), A(QF3 + c * 512, 512), A(G1B, 512)),
                wait=[("ldxa", lq)] if c == 0 else (), inc="ve")
        vq = P.at("ve")
        tr_last = None
        for c in range(NCH):
            for r in range(4):
                b = 2 + ((c * 4 + r) % 2)
                te = P.op("tensor", lambda t, c=c, r=r, b=b: t.transpose(
                    PS[b][:, 0:128], A(KF3 + c * 512 + r * 128, 128), IDT),
                    wait=([("ldxb", lk)] if (c == 0 and r == 0) else []) +
                    P.guard(b), inc="te")
                tr_last = P.op("scalar", lambda s, c=c, r=r, b=b: s.activation(
                    A(KTM + r * D + c * 128, 128), PS[b][:, 0:128],
                    ACTF.Identity, scale=kscale[:, r:r + 1]),
                    wait=[("te", te)], inc="sevac")
                P.release(b, "sevac", tr_last)
        ktm_done = P.at("sevac")
        scev = P.at("vevac")
        for tq in range(4):
            for b2i in range(2):
                bank = 4 + ((tq * 2 + b2i) % 2)
                mms = [(A(SC3 + tk * 512 + tq * 128, 128),
                        A(VT3 + tk * D + b2i * 512, 512)) for tk in range(4)]
                mms += [(A(QT3 + c * 512 + tq * 128, 128),
                         A(SS + c * D + b2i * 512, 512)) for c in range(NCH)]
                te = P.mm_group(bank, PS[bank][:, :], mms,
                                wait=[("vevac", scev), ("ve", vq),
                                      ("ldya", lv),
                                      ("sevac", P.at("sevac"))])
                ev = P.op("scalar", lambda s, tq=tq, b2i=b2i, bank=bank:
                          s.copy(A(RT3 + tq * D + b2i * 512, 512),
                                 PS[bank][:, :]),
                          wait=[("te", te)] +
                               ([("sta", s3b_stgate[0])]
                                if (tq == 0 and b2i == 0) else []),
                          inc="sevac")
                P.release(bank, "sevac", ev)
        st = P.dma("gpsimd", ret_scr.rearrange("(r p) d -> p r d", p=128)[
            :, ci * 4:ci * 4 + 4, :],
            A(RT3, 4 * D).rearrange("p (r d) -> p r d", r=4),
            "sta", wait=[("sevac", P.at("sevac"))])
        for m in range(NCH):
            for b2i in range(2):
                bank = 6 + ((m * 2 + b2i) % 2)
                mms = [(A(KTM + r * D + m * 128, 128),
                        A(VT3 + r * D + b2i * 512, 512)) for r in range(4)]
                te = P.mm_group(bank, PS[bank][:, :], mms,
                                wait=[("sevac", ktm_done)])
                ev = P.op("vector", lambda v, m=m, b2i=b2i, bank=bank:
                          v.scalar_tensor_tensor(
                              A(SS + m * D + b2i * 512, 512),
                              A(SS + m * D + b2i * 512, 512),
                              g512_c[:, 0:1], PS[bank][:, :],
                              ALU.mult, ALU.add),
                          wait=[("te", te)], inc="vevac")
                P.release(bank, "vevac", ev)
        s3b_free = [P.at("te"), P.at("ve"), P.at("sevac")]
        s3b_stgate[0] = st
    if dbg:
        P.dma("gpsimd", dbg_sc[:, :], A(SC3, 2048), "stb",
              wait=[("vevac", P.at("vevac")), ("te", P.at("te"))])
        P.dma("gpsimd", dbg_qt[:, :], A(QT3, 4096), "stb",
              wait=[("ve", P.at("ve"))])
        P.dma("gpsimd", dbg_wi[:, :], A(WI3, 2048), "stb",
              wait=[("se", P.at("se"))])
    st = P.dma("gpsimd", sag_in.rearrange("(c p) e -> p c e", p=128),
               A(SS, 8 * D).rearrange("p (c e) -> p c e", c=NCH),
               "stb", wait=[("vevac", P.at("vevac"))])
    ccv = P.op("gpsimd", lambda g: g.collective_compute(
        "AllGather", ALU.bypass, replica_groups=rg_pair,
        ins=[sag_in[:, :]], outs=[sag_out[:, :]]),
        wait=[("stb", st)], inc="cc")
    P.barrier()

    if stop_stage < 5:
        P.emit(block, sems)
        es.close()
        return nc

    SFO, QF4, QT4, GB, RT4 = 0, 8192, 12288, 16384, 16384 + tloc
    lz = P.dma("sync", A(SFO, 8 * D).rearrange("p (c e) -> p c e", c=NCH),
               sag_out[D:2 * D, :].rearrange("(c p) e -> p c e", p=128),
               "ldz", wait=[("cc", P.at("cc"))])
    vv = P.op("vector", lambda v: v.tensor_scalar(
        A(SFO, 8 * D), A(SFO, 8 * D), hfi_c[:, 0:1], None, ALU.mult),
        wait=[("ldz", lz)], inc="ve")
    P.dma("sync", A(GB, tloc), arange_rep[:, :], "ldz")
    sv = P.op("scalar", lambda s: s.activation(
        A(GB, tloc), A(GB, tloc), ACTF.Exp, bias=MLN32, scale=lng_c),
        wait=[("ldz", P.at("ldz"))], inc="se")
    p2free = [0, 0, 0]
    p2_stgate = [0]
    for ci in range(NC3):
        P.dma("sync", A(QF4, 8 * 512).rearrange("p (c t) -> p c t", c=NCH),
              q_scr.rearrange("(c p) t -> p c t", p=128)[
                  :, :, ci * 512:(ci + 1) * 512],
              "ldxa", wait=[("ve", p2free[0]), ("te", p2free[2])])
        lq = P.at("ldxa")
        P.dma("sync", A(RT4, 4 * D).rearrange("p (r d) -> p r d", r=4),
              ret_scr.rearrange("(r p) d -> p r d", p=128)[
                  :, ci * 4:ci * 4 + 4, :],
              "ldya", wait=[("vevac", p2free[1]), ("sta", p2_stgate[0])])
        lr = P.at("ldya")
        for c in range(NCH):
            P.op("vector", lambda v, c=c, ci=ci: v.tensor_mul(
                A(QT4 + c * 512, 512), A(QF4 + c * 512, 512),
                A(GB + ci * 512, 512)),
                wait=[("ldxa", lq), ("se", sv), ("ve", vv)] if c == 0 else (),
                inc="ve")
        vq = P.at("ve")
        p2free[0] = vq
        vlast = 0
        for tq in range(4):
            for b2i in range(2):
                bank = (tq * 2 + b2i) % 4
                mms = [(A(QT4 + c * 512 + tq * 128, 128),
                        A(SFO + c * D + b2i * 512, 512)) for c in range(NCH)]
                te = P.mm_group(bank, PS[bank][:, :], mms,
                                wait=[("ve", vq)])
                vlast = P.op("vector", lambda v, tq=tq, b2i=b2i, bank=bank:
                             v.tensor_add(A(RT4 + tq * D + b2i * 512, 512),
                                          A(RT4 + tq * D + b2i * 512, 512),
                                          PS[bank][:, :]),
                             wait=[("te", te), ("ldya", lr)] +
                                  ([("sta", p2_stgate[0])]
                                   if (tq == 0 and b2i == 0) else []),
                             inc="vevac")
                P.release(bank, "vevac", vlast)
        st = P.dma("gpsimd", ret_scr.rearrange("(r p) d -> p r d", p=128)[
            :, ci * 4:ci * 4 + 4, :],
            A(RT4, 4 * D).rearrange("p (r d) -> p r d", r=4),
            "sta", wait=[("vevac", vlast)])
        p2_stgate[0] = st
        p2free[1] = P.at("vevac")
        p2free[2] = P.at("te")
    P.barrier()

    if stop_stage < 6:
        P.emit(block, sems)
        es.close()
        return nc

    WOS = 0
    REI0, REI1 = 8192, 12288
    R2I0b, R2I1b = 16384, 20480
    AOT3, R3T = 24576, 28672
    ldwv = P.dma("sync", A(WOS, 8 * D).rearrange("p (c e) -> p c e", c=NCH),
                 woTa.rearrange("(c p) e -> p c e", p=128), "ldw")
    s3c_tfree = [0, 0]
    s3c_vfree = [0, 0]
    s3c_stgate = [0]
    for it in range(NT):
        par = it % 2
        reo = REI0 if par == 0 else REI1
        r2o = R2I0b if par == 0 else R2I1b
        ldx = "ldxa" if par == 0 else "ldxb"
        ldy = "ldya" if par == 0 else "ldyb"
        P.dma("sync", A(reo, 4 * D).rearrange("p (r d) -> p r d", r=4),
              ret_scr.rearrange("(r p) d -> p r d", p=128)[
                  :, it * 4:it * 4 + 4, :],
              ldx, wait=[("se", s3c_tfree[par])])
        P.dma("sync", A(r2o, 4 * D).rearrange("p (r d) -> p r d", r=4),
              r2_scr.rearrange("(r p) d -> p r d", p=128)[
                  :, it * 4:it * 4 + 4, :],
              ldy, wait=[("ve", s3c_vfree[par])])
        lr, l2 = P.at(ldx), P.at(ldy)
        tr_tile(reo, AOT3, 4, 512, wait=[(ldx, lr)])
        vtr = P.at("vevac")
        for ts in range(4):
            for b2i in range(2):
                bank = 4 + ((ts * 2 + b2i) % 2)
                mms = [(A(AOT3 + k * 512 + ts * 128, 128),
                        A(WOS + k * D + b2i * 512, 512)) for k in range(NCH)]
                te = P.mm_group(bank, PS[bank][:, :], mms,
                                wait=[("vevac", vtr), ("ldw", ldwv)])
                ev = P.op("scalar", lambda s, ts=ts, b2i=b2i, bank=bank,
                          reo=reo:
                          s.activation(A(reo + ts * D + b2i * 512, 512),
                                       PS[bank][:, :], ACTF.Identity,
                                       scale=osc_c[:, 0:1]),
                          wait=[("te", te)], inc="sevac")
                P.release(bank, "sevac", ev)
        sv = P.at("sevac")
        for ts in range(4):
            se_ln = ln_chunk(A(reo + ts * D, D), A(R3T + ts * D, D), ts,
                             ldwait=[("sevac", sv)] if ts == 0 else (),
                             wait2=[("stc", s3c_stgate[0])] if ts == 0 else ())
            P.op("vector", lambda v, ts=ts, r2o=r2o: v.tensor_add(
                A(R3T + ts * D, D), A(R3T + ts * D, D), A(r2o + ts * D, D)),
                wait=[("se", se_ln), (ldy, l2)], inc="ve")
        st = P.dma("gpsimd", r3_scr.rearrange("(r p) d -> p r d", p=128)[
            :, it * 4:it * 4 + 4, :],
            A(R3T, 4 * D).rearrange("p (r d) -> p r d", r=4),
            "stc", wait=[("ve", P.at("ve"))])
        s3c_stgate[0] = st
        s3c_tfree[par] = P.at("se")
        s3c_vfree[par] = P.at("ve")
    P.barrier()

    if stop_stage < 7:
        P.emit(block, sems)
        es.close()
        return nc

    PHS = 0
    PHH, PHL = 16384, 24576
    R3I0, R3I1 = 0, 2048
    LNT40, LNT41 = 4096, 6144
    LNF4 = 8192
    LNFH, LNFL = 10240, 11264
    DBF = 12288
    LGB, ELG, CBF = 32768, 36864, 40960
    CTB = 36864
    ldwv = P.dma("sync", A(PHS, NCH * NPTOT).rearrange(
        "p (c e) -> p c e", c=NCH),
        phiT.rearrange("(c p) e -> p c e", p=128), "ldw")
    vphi = 0
    for cc_ in range(NCH):
        vphi = bfsplit(PHS + cc_ * NPTOT, NPTOT, PHH + cc_ * D,
                       PHL + cc_ * D, wait=[("ldw", ldwv)] if cc_ == 0 else ())
    s4_xfree = [0, 0]
    s4_pfree = [0]
    for it in range(NT2):
        par = it % 2
        r3o = R3I0 if par == 0 else R3I1
        lno = LNT40 if par == 0 else LNT41
        ldx = "ldxa" if par == 0 else "ldxb"
        P.dma("sync", A(r3o, 2 * D).rearrange("p (r d) -> p r d", r=2),
              r3_scr.rearrange("(r p) d -> p r d", p=128)[
                  :, it * 2:it * 2 + 2, :],
              ldx, wait=[("se", s4_xfree[par]), ("ve", vphi)])
        lw = [(ldx, P.at(ldx))]
        se_ln = 0
        for r in range(2):
            se_ln = ln_chunk(A(r3o + r * D, D), A(lno + r * D, D), r,
                             ldwait=lw if r == 0 else ())
        s4_xfree[par] = se_ln
        tr_tile(lno, LNF4, 2, T2, wait=[("se", se_ln)])
        vtr = P.at("vevac")
        vls = bfsplit(LNF4, NCH * T2, LNFH, LNFL, wait=[("vevac", vtr)])
        for ts in range(2):
            for bq in range(4):
                bank = (ts * 4 + bq) % 4
                mms = []
                for k in range(NCH):
                    lh = bfap(LNFH, NCH * T2)[:, k * T2 + ts * 128:
                                              k * T2 + ts * 128 + 128]
                    ll = bfap(LNFL, NCH * T2)[:, k * T2 + ts * 128:
                                              k * T2 + ts * 128 + 128]
                    ph = bfap(PHH + k * D, NPTOT)[:, bq * 512:bq * 512 + 512]
                    pl = bfap(PHL + k * D, NPTOT)[:, bq * 512:bq * 512 + 512]
                    mms += [(lh, ph), (lh, pl), (ll, ph)]
                mms.append((onesrow[0:1, 0:128],
                            phibrow[0:1, bq * 512:bq * 512 + 512]))
                te = P.mm_group(bank, PS[bank][:, :], mms,
                                wait=[("ve", vls), ("sta", s4_pfree[0])])
                ev = P.op("scalar", lambda s, ts=ts, bq=bq, bank=bank:
                          s.copy(A(LGB + ts * NPTOT + bq * 512, 512),
                                 PS[bank][:, :]),
                          wait=[("te", te)], inc="sevac")
                P.release(bank, "sevac", ev)
        sv = P.at("sevac")
        for r in range(2):
            lg = A(LGB + r * NPTOT, NPTOT)
            el = A(ELG + r * NPTOT, NPTOT)
            db = A(DBF + r * NPTOT, NPTOT)
            cb = A(CBF + r * NPTOT, NPTOT)
            sv = P.op("scalar", lambda s, lg=lg, el=el: s.activation(
                el, lg, ACTF.Exp),
                wait=([("sevac", sv), ("sta", s4_pfree[0])]
                      if r == 0 else ()), inc="se")
            vv = P.op("vector", lambda v, el=el: v.reduce_sum(
                SD64, el.rearrange("p (g s) -> p g s", s=32), axis=AX.X),
                wait=[("se", sv)], inc="ve")
            vv = P.op("vector", lambda v: v.reciprocal(SD64, SD64),
                      wait=[("ve", P.at("ve"))], inc="ve")
            vv = P.op("vector", lambda v, el=el, db=db: v.tensor_tensor(
                db.rearrange("p (g s) -> p g s", s=32),
                el.rearrange("p (g s) -> p g s", s=32),
                SD64.unsqueeze(2).broadcast_to([128, 64, 32]), ALU.mult),
                wait=[("ve", P.at("ve"))] +
                     ([("sta", s4_pfree[0])] if r == 0 else []), inc="ve")
            vv = P.op("vector", lambda v, el=el: v.reduce_sum(
                SD32, el.rearrange("p (g s) -> p s g", s=32), axis=AX.X),
                inc="ve")
            vv = P.op("vector", lambda v: v.reciprocal(SD32, SD32),
                      wait=[("ve", P.at("ve"))], inc="ve")
            vv = P.op("vector", lambda v, el=el, cb=cb: v.tensor_tensor(
                cb.rearrange("p (g s) -> p s g", s=32),
                el.rearrange("p (g s) -> p s g", s=32),
                SD32.unsqueeze(2).broadcast_to([128, 32, 64]), ALU.mult),
                wait=[("ve", P.at("ve"))], inc="ve")
        vdc = P.at("ve")
        st = P.dma("gpsimd", d_scr.rearrange("(r p) e -> p r e", p=128)[
            :, it * 2:it * 2 + 2, :],
            A(DBF, 2 * NPTOT).rearrange("p (r e) -> p r e", r=2),
            "sta", wait=[("ve", vdc)])
        for npc in range(NPC):
            for r in range(2):
                b = 4 + ((npc * 2 + r) % 2)
                te = P.op("tensor", lambda t, npc=npc, r=r, b=b: t.transpose(
                    PS[b][:, 0:128], A(CBF + r * NPTOT + npc * 128, 128), IDT),
                    wait=([("ve", vdc)] if (npc == 0 and r == 0) else []) +
                    P.guard(b), inc="te")
                ev = P.op("vector", lambda v, npc=npc, r=r, b=b: v.tensor_copy(
                    A(CTB + npc * T2 + r * 128, 128), PS[b][:, 0:128]),
                    wait=[("te", te)], inc="vevac")
                P.release(b, "vevac", ev)
        st = P.dma("gpsimd", c_scr.rearrange("(c p) t -> p c t", p=128)[
            :, :, it * T2:(it + 1) * T2],
            A(CTB, NPC * T2).rearrange("p (c t) -> p c t", c=NPC),
            "sta", wait=[("vevac", P.at("vevac"))])
        s4_pfree[0] = st
    P.barrier()

    if stop_stage < 8:
        P.emit(block, sems)
        es.close()
        return nc

    XP = 0
    DI0, DI1 = 16384, 20480
    DH0, DH1, DL0, DL1 = 24576, 26624, 28672, 30720
    R4I0, R4I1 = 32768, 34816
    LNT50, LNT51 = 36864, 38912
    LNH0, LNH1, LNL0, LNL1 = 40960, 41984, 43008, 44032
    vxp = P.op("vector", lambda v: v.memset(A(XP, 16384), 0.0), inc="ve")
    s4b_tfree = [0, 0]
    s4b_sfree = [0, 0]
    for it in range(NT2):
        par = it % 2
        dio = DI0 if par == 0 else DI1
        r3o = R4I0 if par == 0 else R4I1
        lno = LNT50 if par == 0 else LNT51
        ldx = "ldxa" if par == 0 else "ldxb"
        ldy = "ldya" if par == 0 else "ldyb"
        P.dma("sync", A(dio, 2 * NPTOT).rearrange("p (r e) -> p r e", r=2),
              d_scr.rearrange("(r p) e -> p r e", p=128)[
                  :, it * 2:it * 2 + 2, :],
              ldx, wait=[("te", s4b_tfree[par])])
        P.dma("sync", A(r3o, 2 * D).rearrange("p (r d) -> p r d", r=2),
              r3_scr.rearrange("(r p) d -> p r d", p=128)[
                  :, it * 2:it * 2 + 2, :],
              ldy, wait=[("se", s4b_sfree[par])])
        ld_, lr_ = P.at(ldx), P.at(ldy)
        se_ln = 0
        for r in range(2):
            se_ln = ln_chunk(A(r3o + r * D, D), A(lno + r * D, D), r,
                             ldwait=[(ldy, lr_)] if r == 0 else ())
        s4b_sfree[par] = se_ln
        dh = DH0 if par == 0 else DH1
        dl = DL0 if par == 0 else DL1
        lnh = LNH0 if par == 0 else LNH1
        lnl = LNL0 if par == 0 else LNL1
        vds = bfsplit(dio, 2 * NPTOT, dh, dl, wait=[(ldx, ld_)])
        vds = bfsplit(lno, 2 * D, lnh, lnl, wait=[("se", se_ln)])
        for npc in range(NPC):
            for b2i in range(2):
                bank = (npc * 2 + b2i) % 6
                mms = []
                for r in range(2):
                    ah = bfap(dh, 2 * NPTOT)[:, r * NPTOT + npc * 128:
                                             r * NPTOT + npc * 128 + 128]
                    al = bfap(dl, 2 * NPTOT)[:, r * NPTOT + npc * 128:
                                             r * NPTOT + npc * 128 + 128]
                    bh = bfap(lnh, 2 * D)[:, r * D + b2i * 512:
                                          r * D + b2i * 512 + 512]
                    bl = bfap(lnl, 2 * D)[:, r * D + b2i * 512:
                                          r * D + b2i * 512 + 512]
                    mms += [(ah, bh), (ah, bl), (al, bh)]
                te = P.mm_group(bank, PS[bank][:, :], mms,
                                wait=[("ve", vds), ("ve", vxp)])
                ev = P.op("vector", lambda v, npc=npc, b2i=b2i, bank=bank:
                          v.tensor_add(A(XP + npc * D + b2i * 512, 512),
                                       A(XP + npc * D + b2i * 512, 512),
                                       PS[bank][:, :]),
                          wait=[("te", te)], inc="vevac")
                P.release(bank, "vevac", ev)
        s4b_tfree[par] = P.at("te")
    st = P.dma("gpsimd", xt_in.rearrange("(c p) e -> p c e", p=128),
               A(XP, NPC * D).rearrange("p (c e) -> p c e", c=NPC),
               "stb", wait=[("vevac", P.at("vevac"))])
    ccv = P.op("gpsimd", lambda g: g.collective_compute(
        "ReduceScatter", ALU.add, replica_groups=rg_all,
        ins=[xt_in[:, :]], outs=[xt_out[:, :]]),
        wait=[("stb", st)], inc="cc")
    P.barrier()

    if stop_stage < 9:
        P.emit(block, sems)
        es.close()
        return nc

    XS = 0
    LNB = XS + NPLC * D
    XTF = LNB + NPLC * D
    EW0 = XTF + NCH * NPL
    EW1 = EW0 + 8192
    YTL = EW1 + 8192
    EBR = SM[0:1, 1024:2048]
    EBA = YTL + NPLC * D
    lz = P.dma("sync", A(XS, NPLC * D).rearrange("p (c d) -> p c d", c=NPLC),
               xt_out.rearrange("(c p) d -> p c d", p=128),
               "ldz", wait=[("cc", P.at("cc"))])
    P.dma("sync", A(EBA, NEL * D, 0, 1), eb[:, :], "ldz")
    lz = P.at("ldz")
    vv = P.op("vector", lambda v: v.tensor_scalar(
        A(XS, NPLC * D), A(XS, NPLC * D), 65000.0, -33000.0, ALU.min, ALU.max),
        wait=[("ldz", lz)], inc="ve")
    sv = 0
    for r in range(NPLC):
        sv = ln_chunk(A(XS + r * D, D), A(LNB + r * D, D), r,
                      ldwait=[("ve", vv)] if r == 0 else ())
    for r in range(NPLC):
        sv = P.op("scalar", lambda s, r=r: s.activation(
            A(XS + r * D, D), A(LNB + r * D, D), ACTF.Tanh), inc="se")
    tr_tile(XS, XTF, NPLC, NPL, wait=[("se", sv)])
    vtr = P.at("vevac")
    ew_sem = ["ldxa", "ldxb"]
    ew_free = [0, 0]
    for e in range(NEL):
        par = e % 2
        ewo = EW0 if par == 0 else EW1
        P.dma("sync", A(ewo, 8 * D).rearrange("p (c h) -> p c h", c=NCH),
              ew[e, :, :].rearrange("(c p) h -> p c h", p=128),
              ew_sem[par], wait=[("te", ew_free[par])])
        le = P.at(ew_sem[par])
        for b2i in range(2):
            bank = (e * 2 + b2i) % 4
            mms = [(A(XTF + k * NPL + e * 32, 32, 0, 128),
                    A(ewo + k * D + b2i * 512, 512)) for k in range(NCH)]
            mms.append((onesrow[0:1, 0:32],
                        A(EBA + e * D + b2i * 512, 512, 0, 1)))
            te = P.mm_group(bank, PS[bank][0:32, :], mms,
                            wait=[("vevac", vtr), (ew_sem[par], le),
                                  ("ldz", lz)])
            ev = P.op("vector", lambda v, e=e, b2i=b2i, bank=bank:
                      v.tensor_copy(A(YTL + e * D + b2i * 512, 512, 0, 32),
                                    PS[bank][0:32, :]),
                      wait=[("te", te)], inc="vevac")
            P.release(bank, "vevac", ev)
        ew_free[par] = P.at("te")
    st = P.dma("gpsimd", yt_in.rearrange("(n p) d -> p n d", p=32),
               A(YTL, NEL * D, 0, 32).rearrange("p (n d) -> p n d", n=NEL),
               "stb", wait=[("vevac", P.at("vevac"))])
    ccv = P.op("gpsimd", lambda g: g.collective_compute(
        "AllGather", ALU.bypass, replica_groups=rg_all,
        ins=[yt_in[:, :]], outs=[yt_out[:, :]]),
        wait=[("stb", st)], inc="cc")
    P.barrier()

    if stop_stage < 10:
        P.emit(block, sems)
        es.close()
        return nc

    YTF = 0
    YTH, YTL5 = 16384, 24576
    CFB = 32768
    CFH, CFL = 0, 4096
    R5I0, R5I1 = 12288, 12288
    YOB, OTB = 8192, 40960
    lz = P.dma("sync", A(YTF, NPC * D).rearrange("p (c d) -> p c d", c=NPC),
               yt_out.rearrange("(c p) d -> p c d", p=128),
               "ldz", wait=[("cc", P.at("cc"))])
    lyt = P.at("ldz")
    vsp = 0
    for cc_ in range(NPC):
        vsp = bfsplit(YTF + cc_ * D, D, YTH + cc_ * 512, YTL5 + cc_ * 512,
                      wait=[("ldz", lyt)] if cc_ == 0 else ())
    s5_free = [0, 0]
    s5_cfree = [0]
    s5_vfree = [0]
    s5_stgate = [0]
    for it in range(NT):
        par = it % 2
        r3o = R5I0 if par == 0 else R5I1
        ldx = "ldxa" if par == 0 else "ldxb"
        P.dma("sync", A(CFB, NPC * 512).rearrange("p (c t) -> p c t", c=NPC),
              c_scr.rearrange("(c p) t -> p c t", p=128)[
                  :, :, it * 512:(it + 1) * 512],
              "ldya", wait=[("ve", max(s5_vfree[0], vsp)),
                            ("te", s5_cfree[0])])
        lc = P.at("ldya")
        P.dma("sync", A(r3o, 4 * D).rearrange("p (r d) -> p r d", r=4),
              r3_scr.rearrange("(r p) d -> p r d", p=128)[
                  :, it * 4:it * 4 + 4, :],
              ldx, wait=[("ve", max(s5_free[0], vsp))])
        l3 = P.at(ldx)
        vps = 0
        for cc_ in range(NPC):
            vps = bfsplit(CFB + cc_ * 512, 512, CFH + cc_ * 256,
                          CFL + cc_ * 256,
                          wait=[("ldya", lc)] if cc_ == 0 else ())
        for ts in range(4):
            for b2i in range(2):
                bank = (ts * 2 + b2i) % 4
                mms = []
                for npc in range(NPC):
                    ch = bfap(CFH + npc * 256, 512)[:, ts * 128:ts * 128 + 128]
                    cl = bfap(CFL + npc * 256, 512)[:, ts * 128:ts * 128 + 128]
                    yh = bfap(YTH + npc * 512, D)[:, b2i * 512:b2i * 512 + 512]
                    yl = bfap(YTL5 + npc * 512, D)[:, b2i * 512:b2i * 512 + 512]
                    mms += [(ch, yh), (ch, yl), (cl, yh)]
                te = P.mm_group(bank, PS[bank][:, :], mms,
                                wait=[("ve", vps)])
                ev = P.op("scalar", lambda s, ts=ts, b2i=b2i, bank=bank:
                          s.copy(A(YOB + ts * D + b2i * 512, 512),
                                 PS[bank][:, :]),
                          wait=[("te", te)], inc="sevac")
                P.release(bank, "sevac", ev)
        s5_cfree[0] = P.at("te")
        s5_vfree[0] = P.at("ve")
        sv = P.at("sevac")
        for ts in range(4):
            se_ln = ln_chunk(A(YOB + ts * D, D), A(OTB + ts * D, D), ts,
                             ldwait=[("sevac", sv)] if ts == 0 else (),
                             wait2=[("sto", s5_stgate[0])] if ts == 0 else ())
            P.op("vector", lambda v, ts=ts, r3o=r3o: v.tensor_add(
                A(OTB + ts * D, D), A(OTB + ts * D, D), A(r3o + ts * D, D)),
                wait=[("se", se_ln), (ldx, l3)], inc="ve")
        st = P.dma("gpsimd", out.rearrange("(r p) d -> p r d", p=128)[
            :, it * 4:it * 4 + 4, :],
            A(OTB, 4 * D).rearrange("p (r d) -> p r d", r=4),
            "sto", wait=[("ve", P.at("ve"))])
        s5_free[0] = P.at("ve")
        s5_stgate[0] = st
    P.op("gpsimd", lambda g: g.nop(), wait=[("sto", P.at("sto"))])

    P.emit(block, sems)
    es.close()
    return nc


def _mk_inputs(core, ncores, tloc, ins):
    b, h = core // 2, core % 2
    nbat = 4 if ncores == 8 else 1
    x = np.asarray(ins["L0_attn_input_data"])
    T = x.shape[1]
    xs = x[b, h * tloc:(h + 1) * tloc]
    halo = (np.zeros((HALO, D), np.float32) if h == 0
            else x[b, tloc - HALO:tloc])
    NEL = 64 // ncores
    expo = np.arange(512)[:, None] - np.arange(512)[None, :] - 1.0
    expo = np.where(expo >= 0, expo, 1e38).astype(np.float32)
    ecol = (128.0 * np.arange(4)[None, :]
            + np.arange(128)[:, None]).astype(np.float32)
    sel_own = np.zeros((1, ncores), np.float32)
    sel_own[0, core] = 1.0
    sel_pair = np.zeros((1, ncores), np.float32)
    sel_pair[0, 2 * b] = 1.0
    sel_pair[0, 2 * b + 1] = 1.0
    return {
        "x": np.ascontiguousarray(
            np.concatenate([halo, xs], 0), np.float32),
        "w1": np.ascontiguousarray(ins["L0_attn_W1"], np.float32),
        "b1": np.ascontiguousarray(ins["L0_attn_bias1"], np.float32),
        "w2": np.ascontiguousarray(ins["L0_attn_W2"], np.float32),
        "b2": np.ascontiguousarray(
            ins["L0_attn_bias2"], np.float32).reshape(1, D),
        "mw": np.ascontiguousarray(ins["L0_attn_memory_weights"], np.float32),
        "wrT": np.ascontiguousarray(
            np.asarray(ins["L0_ffn_W_r"], np.float32).T),
        "wkT": np.ascontiguousarray(
            np.asarray(ins["L0_ffn_W_k"], np.float32).T),
        "wvT": np.ascontiguousarray(
            np.asarray(ins["L0_ffn_W_v"], np.float32).T),
        "mu_r": np.ascontiguousarray(ins["L0_ffn_mu_r"], np.float32),
        "mu_k": np.ascontiguousarray(ins["L0_ffn_mu_k"], np.float32),
        "wqTa": np.ascontiguousarray(
            np.asarray(ins["L1_attn_Wq"], np.float32).T),
        "wkTa": np.ascontiguousarray(
            np.asarray(ins["L1_attn_Wk"], np.float32).T),
        "wvTa": np.ascontiguousarray(
            np.asarray(ins["L1_attn_Wv"], np.float32).T),
        "woTa": np.ascontiguousarray(
            np.asarray(ins["L1_attn_Wo"], np.float32).T),
        "dlogit": np.asarray(ins["L1_attn_decay_logit"],
                             np.float32).reshape(1),
        "oscale": np.asarray(ins["L1_attn_out_scale"],
                             np.float32).reshape(1),
        "phiT": np.ascontiguousarray(
            np.asarray(ins["L1_ffn_phi_w"], np.float32).T),
        "phib": np.ascontiguousarray(
            ins["L1_ffn_phi_b"], np.float32).reshape(1, NPTOT),
        "ew": np.ascontiguousarray(
            np.asarray(ins["L1_ffn_expert_w"],
                       np.float32)[core * NEL:(core + 1) * NEL]),
        "eb": np.ascontiguousarray(
            np.asarray(ins["L1_ffn_expert_b"],
                       np.float32)[core * NEL:(core + 1) * NEL]
        ).reshape(1, NEL * D),
        "ident": np.eye(128, dtype=np.float32),
        "expo512": expo,
        "arange_rep": np.ascontiguousarray(
            np.broadcast_to(np.arange(tloc - 1, -1, -1, dtype=np.float32),
                            (128, tloc))),
        "ecol": ecol,
        "hmask": (np.ones((1, HALO), np.float32) if h == 1
                  else np.zeros((1, HALO), np.float32)),
        "hflag": np.asarray([1.0 if h == 1 else 0.0], np.float32),
        "selpair": sel_pair,
        "selown": sel_own,
    }


def kernel(**inputs):
    ncores = 8
    tloc = 4096
    nc = build_nc(ncores=ncores, tloc=tloc)
    in_maps = [_mk_inputs(c, ncores, tloc, inputs) for c in range(ncores)]
    res = run_bass_kernel_spmd(nc, in_maps, core_ids=list(range(ncores)))
    shards = [res.results[c]["out"] for c in range(ncores)]
    full = np.zeros((4, 2 * tloc, D), np.float32)
    for c in range(ncores):
        b, h = c // 2, c % 2
        full[b, h * tloc:(h + 1) * tloc] = shards[c]
    return full


# version 08
# speedup vs baseline: 1.0551x; 1.0395x over previous
import numpy as np
import concourse.bass as bass
import concourse.mybir as mybir
from concourse.bass_utils import run_bass_kernel_spmd
from contextlib import ExitStack

F32 = mybir.dt.float32
BF16 = mybir.dt.bfloat16
AX = mybir.AxisListType
ALU = mybir.AluOpType
ACTF = mybir.ActivationFunctionType

ENGINES = ("sync", "tensor", "vector", "scalar", "gpsimd")

D = 1024
NCH = 8
TT = 512
HALO = 128
HB = 64
MEM = 40
NPTOT = 2048
EPS = 1e-5
MLN32 = -float(np.log(32.0))
ARN = 48256


class Prog:
    def __init__(self):
        self.ops = {e: [] for e in ENGINES}
        self.count = {}
        self.writer = {}
        self.psum_free = {}
        self.last_inc = {}
        self.dirty = {e: False for e in ENGINES}

    def sem(self, name):
        self.count.setdefault(name, 0)
        return name

    def at(self, name):
        return self.count.get(name, 0)

    def op(self, eng, fn, wait=(), inc=None, inc_amt=1):
        w = [(self.sem(s), v) for (s, v) in wait if v and v > 0]
        post = None
        incs = []
        if inc is not None:
            self.sem(inc)
            assert self.writer.setdefault(inc, eng) == eng, (inc, eng)
            self.count[inc] += inc_amt
            post = self.count[inc]
            incs.append((inc, inc_amt))
            self.last_inc[eng] = (inc, post)
        self.ops[eng].append([fn, w, incs])
        self.dirty[eng] = True
        return post

    def dma(self, eng, out, in_, sem, wait=()):
        return self.op(eng, lambda e, o=out, i=in_: e.dma_start(out=o, in_=i),
                       wait=wait, inc=sem, inc_amt=16)

    def barrier(self):
        vals = {}
        for e in ENGINES:
            eos = self.sem(f"eos_{e}")
            assert self.writer.setdefault(eos, e) == e
            if self.dirty[e] and self.ops[e]:
                last = self.ops[e][-1]
                if not last[2]:
                    self.count[eos] += 1
                    last[2].append((eos, 1))
                    self.last_inc[e] = (eos, self.count[eos])
                waits = [(s, self.count[s]) for s, w in self.writer.items()
                         if w == e and s != eos and self.count.get(s, 0) > 0]
                vals[e] = self.op(e, lambda en: en.nop(), wait=waits, inc=eos)
            else:
                vals[e] = self.op(e, lambda en: en.nop(), inc=eos)
        for e in ENGINES:
            self.op(e, lambda en: en.nop(),
                    wait=[(f"eos_{o}", vals[o]) for o in ENGINES if o != e])
        for e in ENGINES:
            self.dirty[e] = False

    def guard(self, bank):
        w = self.psum_free.get(bank)
        return [w] if w else []

    def release(self, bank, sem, val):
        self.psum_free[bank] = (sem, val)

    def mm_group(self, bank, bank_ap, mms, wait=()):
        n = len(mms)
        te = None
        for i, (lh, rh) in enumerate(mms):
            w = (list(wait) + self.guard(bank)) if i == 0 else []
            te = self.op("tensor",
                         lambda t, lh=lh, rh=rh, i=i, n=n, b=bank_ap: t.matmul(
                             b, lh, rh, start=(i == 0), stop=(i == n - 1)),
                         wait=w, inc="te" if i == n - 1 else None)
        return te

    def emit(self, block, sems):
        def run(eng_obj, eng_name):
            for fn, waits, incs in self.ops[eng_name]:
                for s, v in waits:
                    eng_obj.wait_ge(sems[s], v)
                r = fn(eng_obj)
                for s, amt in incs:
                    r.then_inc(sems[s], amt)
        block.sync(lambda e: run(e, "sync"))
        block.tensor(lambda e: run(e, "tensor"))
        block.vector(lambda e: run(e, "vector"))
        block.scalar(lambda e: run(e, "scalar"))
        block.gpsimd(lambda e: run(e, "gpsimd"))


def build_nc(ncores=8, tloc=4096, dbg=False, stop_stage=99):
    NT = tloc // TT
    NPC = NPTOT // 128
    NPL = NPTOT // ncores
    NPLC = NPL // 128
    NEL = 64 // ncores

    nc = bass.Bass("TRN2", target_bir_lowering=False,
                   detect_race_conditions=False)
    dt = nc.dram_tensor

    x_in = dt("x", [HALO + tloc, D], F32, kind="ExternalInput").ap()
    w1 = dt("w1", [D, D], F32, kind="ExternalInput").ap()
    b1 = dt("b1", [D], F32, kind="ExternalInput").ap()
    w2 = dt("w2", [D, D], F32, kind="ExternalInput").ap()
    b2 = dt("b2", [1, D], F32, kind="ExternalInput").ap()
    mw = dt("mw", [MEM + 1, D], F32, kind="ExternalInput").ap()
    wrT = dt("wrT", [D, D], F32, kind="ExternalInput").ap()
    wkT = dt("wkT", [D, D], F32, kind="ExternalInput").ap()
    wvT = dt("wvT", [D, D], F32, kind="ExternalInput").ap()
    mu_r = dt("mu_r", [D], F32, kind="ExternalInput").ap()
    mu_k = dt("mu_k", [D], F32, kind="ExternalInput").ap()
    wqTa = dt("wqTa", [D, D], F32, kind="ExternalInput").ap()
    wkTa = dt("wkTa", [D, D], F32, kind="ExternalInput").ap()
    wvTa = dt("wvTa", [D, D], F32, kind="ExternalInput").ap()
    woTa = dt("woTa", [D, D], F32, kind="ExternalInput").ap()
    dlogit = dt("dlogit", [1], F32, kind="ExternalInput").ap()
    oscale = dt("oscale", [1], F32, kind="ExternalInput").ap()
    phiT = dt("phiT", [D, NPTOT], F32, kind="ExternalInput").ap()
    phib = dt("phib", [1, NPTOT], F32, kind="ExternalInput").ap()
    ew = dt("ew", [NEL, D, D], F32, kind="ExternalInput").ap()
    eb = dt("eb", [1, NEL * D], F32, kind="ExternalInput").ap()
    ident = dt("ident", [128, 128], F32, kind="ExternalInput").ap()
    expo512 = dt("expo512", [512, 512], F32, kind="ExternalInput").ap()
    arange_rep = dt("arange_rep", [128, tloc], F32, kind="ExternalInput").ap()
    ecol = dt("ecol", [128, 4], F32, kind="ExternalInput").ap()
    hmask = dt("hmask", [1, HALO], F32, kind="ExternalInput").ap()
    hflag = dt("hflag", [1], F32, kind="ExternalInput").ap()
    selpair = dt("selpair", [1, ncores], F32, kind="ExternalInput").ap()
    selown = dt("selown", [1, ncores], F32, kind="ExternalInput").ap()

    ik = "ExternalOutput" if dbg else "Internal"
    if dbg:
        dbg_ln = dt("dbg_ln", [128, 4 * D], F32, kind="ExternalOutput").ap()
        dbg_fm = dt("dbg_fm", [128, 4096], F32, kind="ExternalOutput").ap()
        dbg_sc = dt("dbg_sc", [128, 2048], F32, kind="ExternalOutput").ap()
        dbg_qt = dt("dbg_qt", [128, 4096], F32, kind="ExternalOutput").ap()
        dbg_wi = dt("dbg_wi", [128, 2048], F32, kind="ExternalOutput").ap()
        dbg_ss = dt("dbg_ss", [128, 8192], F32, kind="ExternalOutput").ap()
        dbg_ao = dt("dbg_ao", [128, 4096], F32, kind="ExternalOutput").ap()
        dbg_wo = dt("dbg_wo", [128, 8192], F32, kind="ExternalOutput").ap()
    out = dt("out", [tloc, D], F32, kind="ExternalOutput").ap()
    p_scr = dt("p_scr", [D, HALO + tloc], F32, kind=ik).ap()
    r1_scr = dt("r1_scr", [HALO + tloc, D], F32, kind=ik).ap()
    r2_scr = dt("r2_scr", [tloc, D], F32, kind=ik).ap()
    q_scr = dt("q_scr", [D, tloc], F32, kind=ik).ap()
    k_scr = dt("k_scr", [D, tloc], F32, kind=ik).ap()
    v_scr = dt("v_scr", [tloc, D], F32, kind=ik).ap()
    ret_scr = dt("ret_scr", [tloc, D], F32, kind=ik).ap()
    r3_scr = dt("r3_scr", [tloc, D], F32, kind=ik).ap()
    d_scr = dt("d_scr", [tloc, NPTOT], F32, kind=ik).ap()
    c_scr = dt("c_scr", [NPTOT, tloc], F32, kind=ik).ap()
    ar_in = dt("ar_in", [128, 8 * ncores], F32, kind="Internal").ap()
    ar_out = dt("ar_out", [128, 8 * ncores], F32, kind="Internal").ap()
    sag_in = dt("sag_in", [D, D], F32, kind="Internal").ap()
    sag_out = dt("sag_out", [2 * D, D], F32, kind="Internal").ap()
    xt_in = dt("xt_in", [NPTOT, D], F32, kind="Internal").ap()
    xt_out = dt("xt_out", [NPL, D], F32, kind="Internal").ap()
    yt_in = dt("yt_in", [NPL, D], F32, kind="Internal").ap()
    yt_out = dt("yt_out", [NPTOT, D], F32, kind="Internal").ap()

    rg_all = [list(range(ncores))]
    rg_pair = [[2 * i, 2 * i + 1] for i in range(ncores // 2)]

    es = ExitStack()
    AR = es.enter_context(nc.sbuf_tensor("AR", [128, ARN], F32))
    SM = es.enter_context(nc.sbuf_tensor("SMALL", [128, 4096], F32))
    PS = [es.enter_context(nc.psum_tensor(f"ps{i}", [128, 512], F32))
          for i in range(8)]
    sem_names = ["ld", "ldw", "ldw2", "ldxa", "ldxb", "ldya", "ldyb", "ldz",
                 "te", "ve", "se", "sevac", "vevac", "gp",
                 "sta", "stb", "stc", "sto", "cc", "ldg"] + [f"eos_{e}" for e in ENGINES]
    sems = {n: es.enter_context(nc.semaphore(n)) for n in sem_names}
    block = es.enter_context(nc.Block())

    def A(off, n, p0=0, p1=128):
        assert 0 <= off and off + n <= ARN, (off, n)
        return AR[p0:p1, off:off + n]

    P = Prog()

    ones_c = SM[:, 0:1]
    lng_c = SM[:, 1:2]
    g512_c = SM[:, 2:3]
    hflag_c = SM[:, 3:4]
    osc_c = SM[:, 4:5]
    hfi_c = SM[:, 12:13]
    kscale = SM[:, 8:12]
    b1_c = SM[:, 16:24]
    mw0_c = SM[:, 24:32]
    sump = SM[:, 32:40]
    corr = SM[:, 48:56]
    mur_c = SM[:, 56:64]
    murI_c = SM[:, 64:72]
    muk_c = SM[:, 72:80]
    mukI_c = SM[:, 80:88]
    selpair_r = SM[:, 88:88 + ncores]
    selown_r = SM[:, 96:96 + ncores]
    prevcol = SM[:, 104:112]
    KT = SM[:, 128:128 + NCH * MEM]
    STT = SM[:, 448:448 + 64]
    SCR1 = SM[:, 512:513]
    SCR2 = SM[:, 513:514]
    SCR3 = SM[:, 514:515]
    SD64 = SM[:, 516:580]
    SD32 = SM[:, 580:612]
    onesrow = SM[0:1, 640:768]
    cells = SM[0:1, 768:784]
    selown_cl = SM[0:1, 784:784 + ncores]
    selpair_cl = SM[0:1, 784 + ncores:784 + 2 * ncores]
    hmrow = SM[0:1, 884:1012]
    b2row = SM[0:1, 1024:2048]
    phibrow = SM[0:1, 2048:4096]

    IDT = A(47104, 128)
    LNDUMP = A(47232, 1024)
    CEPS = SM[:, 5:6]
    CLN32 = SM[:, 6:7]
    nc.const_aps.register(F32, EPS, CEPS)
    nc.const_aps.register(F32, MLN32, CLN32)

    P.dma("sync", IDT, ident[:, :], "ld")
    STG1 = A(8192, 1024)
    STG2 = A(9216, 1024)
    P.dma("sync", STG1[0:MEM, :], mw[1:, :], "ld")
    P.dma("sync", STG2[0:1, :], b1.unsqueeze(0), "ld")
    P.dma("sync", STG2[1:2, :], mw[0:1, :], "ld")
    P.dma("sync", STG2[2:3, :], mu_r.unsqueeze(0), "ld")
    P.dma("sync", STG2[3:4, :], mu_k.unsqueeze(0), "ld")
    P.dma("sync", kscale, ecol[:, :], "ld")
    P.dma("sync", cells[:, 0:1], dlogit.unsqueeze(0), "ld")
    P.dma("sync", cells[:, 1:2], oscale.unsqueeze(0), "ld")
    P.dma("sync", cells[:, 2:3], hflag.unsqueeze(0), "ld")
    P.dma("sync", selown_cl, selown[:, :], "ld")
    P.dma("sync", selpair_cl, selpair[:, :], "ld")
    P.dma("sync", b2row, b2[:, :], "ld")
    P.dma("sync", phibrow, phib[:, :], "ld")
    ldv = P.at("ld")

    P.op("vector", lambda v: v.memset(ones_c, 1.0), inc="ve")
    P.op("vector", lambda v: v.memset(CEPS, EPS), inc="ve")
    P.op("vector", lambda v: v.memset(CLN32, MLN32), inc="ve")
    vones = P.op("vector", lambda v: v.memset(onesrow, 1.0), inc="ve")
    sev = P.op("scalar", lambda s: s.activation(cells[:, 3:4], cells[:, 0:1],
                                                ACTF.Sigmoid),
               wait=[("ld", ldv)], inc="se")
    sev = P.op("scalar", lambda s: s.activation(cells[:, 4:5], cells[:, 3:4],
                                                ACTF.Ln),
               wait=[("se", sev)], inc="se")
    te = P.mm_group(0, PS[0][:, 0:5], [(onesrow, cells[0:1, 0:5])],
                    wait=[("se", sev), ("ve", vones)])
    P.op("vector", lambda v: v.tensor_copy(osc_c, PS[0][:, 1:2]),
         wait=[("te", te)], inc="vevac")
    P.op("vector", lambda v: v.tensor_copy(hflag_c, PS[0][:, 2:3]), inc="vevac")
    ev = P.op("vector", lambda v: v.tensor_copy(lng_c, PS[0][:, 4:5]), inc="vevac")
    P.release(0, "vevac", ev)
    te = P.mm_group(1, PS[1][:, 0:2 * ncores],
                    [(onesrow, SM[0:1, 784:784 + 2 * ncores])],
                    wait=[("ve", vones), ("ld", ldv)])
    ev = P.op("vector", lambda v: v.tensor_copy(
        SM[:, 96:96 + 2 * ncores], PS[1][:, 0:2 * ncores]),
        wait=[("te", te)], inc="vevac")
    P.release(1, "vevac", ev)
    ev = P.op("vector", lambda v: v.tensor_copy(
        selpair_r, SM[:, 96 + ncores:96 + 2 * ncores]), inc="vevac")
    sev = P.op("scalar", lambda s: s.activation(g512_c, lng_c, ACTF.Exp,
                                                scale=512.0),
               wait=[("vevac", ev)], inc="se")
    sev = P.op("scalar", lambda s: s.activation(kscale, kscale, ACTF.Exp,
                                                scale=lng_c), inc="se")
    for c in range(NCH):
        b = 2 + (c % 2)
        te = P.op("tensor", lambda t, c=c, b=b: t.transpose(
            PS[b][:, 0:MEM], STG1[0:MEM, c * 128:(c + 1) * 128], IDT[0:MEM, 0:MEM]),
            wait=([("ld", ldv)] if c == 0 else []) + P.guard(b), inc="te")
        ev = P.op("vector", lambda v, c=c, b=b: v.tensor_copy(
            KT[:, c * MEM:(c + 1) * MEM], PS[b][:, 0:MEM]),
            wait=[("te", te)], inc="vevac")
        P.release(b, "vevac", ev)
        b = 4 + (c % 2)
        te = P.op("tensor", lambda t, c=c, b=b: t.transpose(
            PS[b][:, 0:4], STG2[0:4, c * 128:(c + 1) * 128], IDT[0:4, 0:4]),
            wait=P.guard(b), inc="te")
        for dst_i, dstc in enumerate((b1_c, mw0_c, mur_c, muk_c)):
            ev = P.op("vector", lambda v, c=c, b=b, i=dst_i, d=dstc:
                      v.tensor_copy(d[:, c:c + 1], PS[b][:, i:i + 1]),
                      wait=[("te", te)] if dst_i == 0 else (), inc="vevac")
        P.release(b, "vevac", ev)
    for c in range(NCH):
        P.op("vector", lambda v, c=c: v.tensor_scalar(
            KT[:, c * MEM:(c + 1) * MEM], KT[:, c * MEM:(c + 1) * MEM],
            mw0_c[:, c:c + 1], None, ALU.subtract),
            wait=[("vevac", P.at("vevac"))] if c == 0 else (), inc="ve")
    P.op("vector", lambda v: v.tensor_scalar(murI_c, mur_c, -1.0, 1.0,
                                             ALU.mult, ALU.add),
         wait=[("vevac", P.at("vevac"))], inc="ve")
    P.op("vector", lambda v: v.tensor_scalar(mukI_c, muk_c, -1.0, 1.0,
                                             ALU.mult, ALU.add), inc="ve")
    P.op("vector", lambda v: v.tensor_scalar(hfi_c, hflag_c, -1.0, 1.0,
                                             ALU.mult, ALU.add), inc="ve")
    P.op("vector", lambda v: v.memset(sump, 0.0), inc="ve")
    P.barrier()

    def ln_chunk(src, dst, slot, ldwait=(), pr=128, wait2=()):
        st = STT[0:pr, slot * 4: slot * 4 + 4]
        mean, var, rstd, nmr = (st[:, 0:1], st[:, 1:2], st[:, 2:3], st[:, 3:4])
        tA = SCR1[0:pr]
        tB = SCR2[0:pr]
        def vw(fn):
            return P.op("vector", fn, wait=[("ve", P.at("ve"))], inc="ve")
        P.op("vector", lambda v: v.reduce_sum(mean, src, axis=AX.X),
             wait=ldwait, inc="ve")
        sev = P.op("scalar", lambda s: s.activation(LNDUMP[0:pr, :], src,
                                                    ACTF.Square),
                   wait=ldwait, inc="se")
        P.op("vector", lambda v: v.reduce_sum(var, LNDUMP[0:pr, :], axis=AX.X),
             wait=[("se", sev)], inc="ve")
        vw(lambda v: v.tensor_scalar(var, var, 1.0 / D, None, ALU.mult))
        vw(lambda v: v.tensor_scalar(mean, mean, 1.0 / D, None, ALU.mult))
        vw(lambda v: v.tensor_mul(tA, mean, mean))
        vev = vw(lambda v: v.tensor_sub(var, var, tA))
        sev = P.op("scalar", lambda s: s.activation(rstd, var, ACTF.Sqrt,
                                                    bias=EPS),
                   wait=[("ve", vev)], inc="se")
        P.op("vector", lambda v: v.reciprocal(rstd, rstd),
             wait=[("se", sev)], inc="ve")
        vw(lambda v: v.tensor_mul(nmr, mean, rstd))
        vev = vw(lambda v: v.tensor_scalar(nmr, nmr, -1.0, None, ALU.mult))
        sev = P.op("scalar", lambda s: s.activation(dst, src, ACTF.Identity,
                                                    bias=nmr, scale=rstd),
                   wait=[("ve", vev)] + list(wait2), inc="se")
        return sev

    def tr_tile(src_off, dst_off, rows, colw, wait=(), banks=(2, 3),
                scale=None):
        last = None
        for c in range(NCH):
            for r in range(rows):
                b = banks[(c * rows + r) % len(banks)]
                w = list(wait) if (c == 0 and r == 0) else []
                te = P.op("tensor", lambda t, c=c, r=r, b=b: t.transpose(
                    PS[b][:, 0:128], A(src_off + r * D + c * 128, 128), IDT),
                    wait=w + P.guard(b), inc="te")
                if scale is None:
                    last = P.op("vector",
                                lambda v, c=c, r=r, b=b: v.tensor_copy(
                                    A(dst_off + c * colw + r * 128, 128),
                                    PS[b][:, 0:128]),
                                wait=[("te", te)], inc="vevac")
                    P.release(b, "vevac", last)
                else:
                    last = P.op("scalar",
                                lambda s, c=c, r=r, b=b: s.activation(
                                    A(dst_off + c * colw + r * 128, 128),
                                    PS[b][:, 0:128], ACTF.Identity,
                                    scale=scale(r)),
                                wait=[("te", te)], inc="sevac")
                    P.release(b, "sevac", last)
        return last

    def bfsplit(src_off, n, hi_off, lo_off, wait=(), pr=128):
        hi = A(hi_off, n // 2, 0, pr).bitcast(BF16)
        lo = A(lo_off, n // 2, 0, pr).bitcast(BF16)
        sf = A(src_off, n, 0, pr)
        P.op("vector", lambda v: v.tensor_copy(hi, sf), wait=wait, inc="ve")
        return P.op("vector", lambda v: v.tensor_sub(lo, sf, hi),
                    wait=[("ve", P.at("ve"))], inc="ve")

    def bfap(off, n, pr=128):
        return A(off, (n + 1) // 2, 0, pr).bitcast(BF16)[:, 0:n]

    def wsplit_inplace(base, nch, tmp_off, wait0=()):
        vv = 0
        for c in range(nch):
            off = base + c * D
            th = A(tmp_off, 512).bitcast(BF16)
            tl = A(tmp_off + 512, 512).bitcast(BF16)
            P.op("vector", lambda v, off=off, th=th: v.tensor_copy(
                th, A(off, D)), wait=wait0 if c == 0 else (), inc="ve")
            P.op("vector", lambda v, off=off, th=th, tl=tl: v.tensor_sub(
                tl, A(off, D), th), wait=[("ve", P.at("ve"))], inc="ve")
            P.op("vector", lambda v, off=off, tmp=tmp_off: v.tensor_copy(
                A(off, 512), A(tmp, 512)), wait=[("ve", P.at("ve"))], inc="ve")
            vv = P.op("vector", lambda v, off=off, tmp=tmp_off: v.tensor_copy(
                A(off + 512, 512), A(tmp + 512, 512)),
                wait=[("ve", P.at("ve"))], inc="ve")
        return vv

    W1S, XT0, XT1, LNT0, LNT1, LNFM, PFM, HMR = (
        0, 8192, 12288, 16384, 20480, 24576, 28672, 32768)
    XTb, LNTb = [XT0, XT1], [LNT0, LNT1]
    ldwv = P.dma("sync", A(W1S, 8 * D).rearrange("p (c e) -> p c e", c=NCH),
                 w1.rearrange("(c p) e -> p c e", p=128), "ldw")
    P.dma("sync", hmrow, hmask[:, :], "ld")
    te = P.mm_group(2, PS[2][:, 0:HALO], [(onesrow, hmrow)],
                    wait=[("ve", vones), ("ld", P.at("ld"))])
    ev = P.op("vector", lambda v: v.tensor_copy(A(HMR, HALO), PS[2][:, 0:HALO]),
              wait=[("te", te)], inc="vevac")
    P.release(2, "vevac", ev)

    xfree = [0, 0]
    s1a_stgate = [0]
    for it in range(NT + 1):
        par = it % 2
        rows = 1 if it == 0 else 4
        roff = 0 if it == 0 else HALO + (it - 1) * TT
        xoff, lnoff = XTb[par], LNTb[par]
        ldx = "ldxa" if par == 0 else "ldxb"
        P.dma("sync", A(xoff, rows * D).rearrange("p (r d) -> p r d", r=rows),
              x_in.rearrange("(r p) d -> p r d", p=128)[
                  :, roff // 128:roff // 128 + rows, :],
              ldx, wait=[("se", xfree[par])])
        lw = [(ldx, P.at(ldx))]
        se_ln = 0
        for r in range(rows):
            se_ln = ln_chunk(A(xoff + r * D, D), A(lnoff + r * D, D), r,
                             ldwait=lw if r == 0 else ())
        xfree[par] = se_ln
        colw = rows * 128
        tr_tile(lnoff, LNFM, rows, colw, wait=[("se", se_ln)])
        vtr = P.at("vevac")
        for m in range(NCH):
            bank = 4 + (m % 2)
            te = P.mm_group(bank, PS[bank][:, 0:colw],
                            [(A(W1S + k * D + m * 128, 128),
                              A(LNFM + k * colw, colw)) for k in range(NCH)],
                            wait=[("vevac", vtr), ("ldw", ldwv)])
            ev = P.op("scalar", lambda s, m=m, bank=bank, colw=colw:
                      s.activation(A(PFM + m * colw, colw),
                                   PS[bank][:, 0:colw], ACTF.Identity,
                                   bias=b1_c[:, m:m + 1]),
                      wait=[("te", te)] +
                           ([("sta", s1a_stgate[0])] if m == 0 else []),
                      inc="sevac")
            P.release(bank, "sevac", ev)
        sev_evac = P.at("sevac")
        if it == 0:
            vv = 0
            for m in range(NCH):
                vv = P.op("vector", lambda v, m=m: v.tensor_mul(
                    A(PFM + m * 128, 128), A(PFM + m * 128, 128),
                    A(HMR, HALO)),
                    wait=[("sevac", sev_evac)] if m == 0 else (), inc="ve")
            st = P.dma("gpsimd",
                       p_scr.rearrange("(c p) t -> p c t", p=128)[:, :, 0:HALO],
                       A(PFM, NCH * 128).rearrange("p (c t) -> p c t", c=NCH),
                       "sta", wait=[("ve", vv)])
        else:
            for m in range(NCH):
                P.op("vector", lambda v, m=m, colw=colw: v.reduce_sum(
                    SCR3, A(PFM + m * colw, colw), axis=AX.X),
                    wait=[("sevac", sev_evac)] if m == 0 else (), inc="ve")
                P.op("vector", lambda v, m=m: v.tensor_add(
                    sump[:, m:m + 1], sump[:, m:m + 1], SCR3),
                    wait=[("ve", P.at("ve"))], inc="ve")
            st = P.dma("gpsimd",
                       p_scr.rearrange("(c p) t -> p c t", p=128)[
                           :, :, roff:roff + colw],
                       A(PFM, NCH * colw).rearrange("p (c t) -> p c t", c=NCH),
                       "sta", wait=[("ve", P.at("ve"))])
        s1a_stgate[0] = st

    if dbg:
        P.dma("gpsimd", dbg_ln[:, :], A(LNTb[NT % 2], 4 * D), "stb",
              wait=[("se", P.at("se"))])
        P.dma("gpsimd", dbg_fm[:, :], A(LNFM, 4096), "stb",
              wait=[("vevac", P.at("vevac"))])
    Z = A(33024, 8 * ncores)
    vv = P.op("vector", lambda v: v.memset(Z, 0.0), inc="ve")
    for c8 in range(ncores):
        vv = P.op("vector", lambda v, c8=c8: v.scalar_tensor_tensor(
            Z[:, 8 * c8:8 * c8 + 8], sump, selown_r[:, c8:c8 + 1],
            Z[:, 8 * c8:8 * c8 + 8], ALU.mult, ALU.add),
            wait=[("ve", P.at("ve"))] if c8 == 0 else (), inc="ve")
    st = P.dma("gpsimd", ar_in[:, :], Z, "stb", wait=[("ve", vv)])
    ccv = P.op("gpsimd", lambda g: g.collective_compute(
        "AllReduce", ALU.add, replica_groups=rg_all,
        ins=[ar_in[:, :]], outs=[ar_out[:, :]]),
        wait=[("stb", st)], inc="cc")
    lz = P.dma("gpsimd", Z, ar_out[:, :], "ldg", wait=[("cc", ccv)])
    vv = P.op("vector", lambda v: v.memset(corr, 0.0),
              wait=[("ldg", lz)], inc="ve")
    for c8 in range(ncores):
        vv = P.op("vector", lambda v, c8=c8: v.scalar_tensor_tensor(
            corr, Z[:, 8 * c8:8 * c8 + 8], selpair_r[:, c8:c8 + 1], corr,
            ALU.mult, ALU.add), wait=[("ve", P.at("ve"))], inc="ve")
    vv = P.op("vector", lambda v: v.tensor_mul(corr, corr, mw0_c),
              wait=[("ve", P.at("ve"))], inc="ve")
    P.barrier()

    if stop_stage < 1:
        P.emit(block, sems)
        es.close()
        return nc

    W2S = 0
    PT = 8192
    QFM = 13312
    XTq0, XTq1 = 17408, 21504
    AOT = 25600
    R1T = 29696
    ldwv = P.dma("sync", A(W2S, 8 * D).rearrange("p (c e) -> p c e", c=NCH),
                 w2.rearrange("(c p) e -> p c e", p=128), "ldw")

    def s1b_tile(it):
        if it < 0:
            cw, roff = HB, HALO - HB
        else:
            cw, roff = TT, HALO + it * TT
        hw = cw + MEM + 1
        par = it % 2
        ldx = "ldxa" if par == 0 else "ldxb"
        xoff = XTq0 if par == 0 else XTq1
        rows = cw // 128 if cw >= 128 else 1
        te_prev = P.at("te")
        P.dma("sync", A(PT, NCH * 560).rearrange("p (c w) -> p c w", w=560)[
            :, :, 0:hw],
            p_scr.rearrange("(c p) t -> p c t", p=128)[
                :, :, roff - MEM - 1:roff + cw],
            "ldya", wait=[("te", s1b_tfree[0]), ("ve", s1b_vfree[0])])
        lpy = P.at("ldya")
        xw = [("se", s1b_xfree[par]), ("ve", s1b_xvfree[par])]
        if cw < 128:
            P.dma("sync", A(xoff, D, 0, cw), x_in[roff:roff + cw, :],
                  ldx, wait=xw)
        else:
            P.dma("sync", A(xoff, rows * D).rearrange("p (r d) -> p r d", r=rows),
                  x_in.rearrange("(r p) d -> p r d", p=128)[
                      :, roff // 128:roff // 128 + rows, :],
                  ldx, wait=xw)
        lpx = P.at(ldx)
        vv = 0
        for c in range(NCH):
            base = PT + c * 560
            P.op("vector", lambda v, c=c, base=base: v.tensor_scalar(
                A(QFM + c * cw, cw), A(base + MEM + 1, cw),
                KT[:, c * MEM:c * MEM + 1], None, ALU.mult),
                wait=[("ldya", lpy)] if c == 0 else (), inc="ve")
            for j in range(1, MEM):
                P.op("vector", lambda v, c=c, j=j, base=base:
                     v.scalar_tensor_tensor(
                         A(QFM + c * cw, cw), A(base + MEM + 1 - j, cw),
                         KT[:, c * MEM + j:c * MEM + j + 1],
                         A(QFM + c * cw, cw), ALU.mult, ALU.add), inc="ve")
            vv = P.op("vector", lambda v, c=c: v.tensor_scalar(
                A(QFM + c * cw, cw), A(QFM + c * cw, cw),
                corr[:, c:c + 1], None, ALU.add), inc="ve")
        s1b_vfree[0] = vv
        nts = max(1, cw // 128)
        sev_all = 0
        for ts in range(nts):
            pr = min(cw, 128)
            for b2i in range(2):
                bank = 4 + ((ts * 2 + b2i) % 2)
                mms = [(A(QFM + k * cw + ts * 128, pr),
                        A(W2S + k * D + b2i * 512, 512)) for k in range(NCH)]
                mms.append((onesrow[0:1, 0:pr],
                            b2row[0:1, b2i * 512:b2i * 512 + 512]))
                te = P.mm_group(bank, PS[bank][0:pr, :], mms,
                                wait=[("ve", vv), ("ldw", ldwv),
                                      (ldx, lpx)])
                ev = P.op("scalar", lambda s, ts=ts, b2i=b2i, bank=bank, pr=pr:
                          s.copy(A(AOT + ts * D + b2i * 512, 512, 0, pr),
                                 PS[bank][0:pr, :]),
                          wait=[("te", te)], inc="sevac")
                P.release(bank, "sevac", ev)
        sv = P.at("sevac")
        s1b_tfree[0] = P.at("te")
        for ts in range(nts):
            pr = min(cw, 128)
            se_ln = ln_chunk(A(AOT + ts * D, D, 0, pr),
                             A(R1T + ts * D, D, 0, pr), ts,
                             ldwait=[("sevac", sv)] if ts == 0 else (), pr=pr,
                             wait2=[("stc", s1b_stgate[0])] if ts == 0 else ())
            vv2 = P.op("vector", lambda v, ts=ts, pr=pr: v.tensor_add(
                A(R1T + ts * D, D, 0, pr), A(R1T + ts * D, D, 0, pr),
                A(xoff + (ts * D if cw >= 128 else 0), D, 0, pr)),
                wait=[("se", se_ln)], inc="ve")
        s1b_xfree[par] = P.at("se")
        s1b_xvfree[par] = P.at("ve")
        if cw >= 128:
            st = P.dma("gpsimd", r1_scr.rearrange("(r p) d -> p r d", p=128)[
                :, roff // 128:roff // 128 + rows, :],
                A(R1T, rows * D).rearrange("p (r d) -> p r d", r=rows),
                "stc", wait=[("ve", P.at("ve"))])
        else:
            st = P.dma("gpsimd", r1_scr[roff:roff + cw, :],
                       A(R1T, D, 0, cw), "stc", wait=[("ve", P.at("ve"))])
        s1b_stgate[0] = st

    s1b_tfree = [0]
    s1b_vfree = [0]
    s1b_xfree = [0, 0]
    s1b_xvfree = [0, 0]
    s1b_stgate = [0]
    s1b_tile(-1)
    for it in range(NT):
        s1b_tile(it)
    P.barrier()

    if stop_stage < 2:
        P.emit(block, sems)
        es.close()
        return nc

    WRS, WKS, WVS = 0, 8192, 16384
    R1I0, R1I1 = 24576, 26624
    LNT20, LNT21 = 28672, 30720
    LNF2 = 32768
    XRB, XKB = 34816, 36864
    RSB, KAB = 38912, 40960
    R2T = 43008
    T2 = 256
    NT2 = tloc // T2
    ldwv = P.dma("sync", A(WRS, 8 * D).rearrange("p (c e) -> p c e", c=NCH),
                 wrT.rearrange("(c p) e -> p c e", p=128), "ldw")
    ldwv = P.dma("sync", A(WKS, 8 * D).rearrange("p (c e) -> p c e", c=NCH),
                 wkT.rearrange("(c p) e -> p c e", p=128), "ldw")
    ldwv = P.dma("sync", A(WVS, 8 * D).rearrange("p (c e) -> p c e", c=NCH),
                 wvT.rearrange("(c p) e -> p c e", p=128), "ldw")
    P.dma("sync", A(XRB, D, 0, 1), r1_scr[HALO - 1:HALO, :], "ldz")
    se_ln = ln_chunk(A(XRB, D, 0, 1), A(XKB, D, 0, 1), 0,
                     ldwait=[("ldz", P.at("ldz"))], pr=1)
    for c in range(NCH):
        te = P.op("tensor", lambda t, c=c: t.transpose(
            PS[2][:, 0:1], A(XKB + c * 128, 128, 0, 1), IDT[0:1, 0:1]),
            wait=([("se", se_ln)] if c == 0 else []) + P.guard(2), inc="te")
        ev = P.op("vector", lambda v, c=c: v.tensor_copy(
            prevcol[:, c:c + 1], PS[2][:, 0:1]), wait=[("te", te)], inc="vevac")
        P.release(2, "vevac", ev)
    vv = P.op("vector", lambda v: v.tensor_scalar(
        prevcol, prevcol, hflag_c[:, 0:1], None, ALU.mult),
        wait=[("vevac", P.at("vevac"))], inc="ve")

    s2_xfree = [0, 0]
    s2_vfree = [0, 0]
    s2_stgate = [0]
    for it in range(NT2):
        par = it % 2
        roff = HALO + it * T2
        r1o = R1I0 if par == 0 else R1I1
        lno = LNT20 if par == 0 else LNT21
        ldx = "ldxa" if par == 0 else "ldxb"
        P.dma("sync", A(r1o, 2 * D).rearrange("p (r d) -> p r d", r=2),
              r1_scr.rearrange("(r p) d -> p r d", p=128)[
                  :, roff // 128:roff // 128 + 2, :],
              ldx, wait=[("se", s2_xfree[par]), ("ve", s2_vfree[par])])
        lw = [(ldx, P.at(ldx))]
        se_ln = 0
        for r in range(2):
            se_ln = ln_chunk(A(r1o + r * D, D), A(lno + r * D, D), r,
                             ldwait=lw if r == 0 else ())
        s2_xfree[par] = se_ln
        tr_tile(lno, LNF2, 2, T2, wait=[("se", se_ln)])
        vtr = P.at("vevac")
        vv = 0
        for c in range(NCH):
            P.op("vector", lambda v, c=c: v.tensor_scalar(
                A(XRB + c * T2, T2), A(LNF2 + c * T2, T2),
                mur_c[:, c:c + 1], None, ALU.mult),
                wait=([("vevac", vtr), ("stc", s2_stgate[0])]
                      if c == 0 else ()), inc="ve")
            P.op("vector", lambda v, c=c: v.scalar_tensor_tensor(
                A(XRB + c * T2 + 1, T2 - 1), A(LNF2 + c * T2, T2 - 1),
                murI_c[:, c:c + 1], A(XRB + c * T2 + 1, T2 - 1),
                ALU.mult, ALU.add), inc="ve")
            P.op("vector", lambda v, c=c: v.scalar_tensor_tensor(
                A(XRB + c * T2, 1), prevcol[:, c:c + 1],
                murI_c[:, c:c + 1], A(XRB + c * T2, 1),
                ALU.mult, ALU.add), inc="ve")
            P.op("vector", lambda v, c=c: v.tensor_scalar(
                A(XKB + c * T2, T2), A(LNF2 + c * T2, T2),
                muk_c[:, c:c + 1], None, ALU.mult), inc="ve")
            P.op("vector", lambda v, c=c: v.scalar_tensor_tensor(
                A(XKB + c * T2 + 1, T2 - 1), A(LNF2 + c * T2, T2 - 1),
                mukI_c[:, c:c + 1], A(XKB + c * T2 + 1, T2 - 1),
                ALU.mult, ALU.add), inc="ve")
            vv = P.op("vector", lambda v, c=c: v.scalar_tensor_tensor(
                A(XKB + c * T2, 1), prevcol[:, c:c + 1],
                mukI_c[:, c:c + 1], A(XKB + c * T2, 1),
                ALU.mult, ALU.add), inc="ve")
        for c in range(NCH):
            vv = P.op("vector", lambda v, c=c: v.tensor_copy(
                prevcol[:, c:c + 1], A(LNF2 + c * T2 + T2 - 1, 1)), inc="ve")
        for ts in range(2):
            bank = 0 + (ts % 2)
            for b2i in range(2):
                bank = (ts * 2 + b2i) % 2
                mms = [(A(XRB + k * T2 + ts * 128, 128),
                        A(WRS + k * D + b2i * 512, 512)) for k in range(NCH)]
                te = P.mm_group(bank, PS[bank][:, :], mms,
                                wait=[("ve", vv), ("ldw", ldwv)])
                ev = P.op("scalar", lambda s, ts=ts, b2i=b2i, bank=bank:
                          s.activation(A(RSB + ts * D + b2i * 512, 512),
                                       PS[bank][:, :], ACTF.Sigmoid),
                          wait=[("te", te)], inc="sevac")
                P.release(bank, "sevac", ev)
        for m in range(NCH):
            bank = 2 + (m % 2)
            mms = [(A(WKS + k * D + m * 128, 128),
                    A(XKB + k * T2, T2)) for k in range(NCH)]
            te = P.mm_group(bank, PS[bank][:, 0:T2], mms,
                            wait=[("ve", vv), ("ldw", ldwv)])
            ev = P.op("scalar", lambda s, m=m, bank=bank: s.activation(
                A(KAB + m * T2, T2), PS[bank][:, 0:T2], ACTF.Relu),
                wait=[("te", te)], inc="sevac")
            P.release(bank, "sevac", ev)
            P.op("scalar", lambda s, m=m: s.activation(
                A(KAB + m * T2, T2), A(KAB + m * T2, T2), ACTF.Square),
                wait=[("sevac", P.at("sevac"))], inc="sevac")
        sv = P.at("sevac")
        for ts in range(2):
            for b2i in range(2):
                bank = 4 + ((ts * 2 + b2i) % 2)
                mms = [(A(KAB + k * T2 + ts * 128, 128),
                        A(WVS + k * D + b2i * 512, 512)) for k in range(NCH)]
                te = P.mm_group(bank, PS[bank][:, :], mms, wait=[("sevac", sv)])
                ev = P.op("vector", lambda v, ts=ts, b2i=b2i, bank=bank:
                          v.tensor_mul(A(R2T + ts * D + b2i * 512, 512),
                                       A(RSB + ts * D + b2i * 512, 512),
                                       PS[bank][:, :]),
                          wait=[("te", te)], inc="vevac")
                P.release(bank, "vevac", ev)
        vv2 = P.at("vevac")
        for ts in range(2):
            se_ln = ln_chunk(A(R2T + ts * D, D), A(XRB + ts * D, D), ts + 2,
                             ldwait=[("vevac", vv2)] if ts == 0 else ())
            P.op("vector", lambda v, ts=ts, r1o=r1o: v.tensor_add(
                A(XRB + ts * D, D), A(XRB + ts * D, D), A(r1o + ts * D, D)),
                wait=[("se", se_ln)], inc="ve")
        s2_vfree[par] = P.at("ve")
        st = P.dma("gpsimd", r2_scr.rearrange("(r p) d -> p r d", p=128)[
            :, it * 2:it * 2 + 2, :],
            A(XRB, 2 * D).rearrange("p (r d) -> p r d", r=2),
            "stc", wait=[("ve", P.at("ve"))])
        s2_stgate[0] = st
    P.barrier()

    if stop_stage < 3:
        P.emit(block, sems)
        es.close()
        return nc

    WQS, WKS3, WVS3 = 0, 8192, 16384
    R2I0, R2I1 = 24576, 26624
    LNT30, LNT31 = 28672, 30720
    LNF3 = 32768
    QFO, KFO, VTO = 34816, 36864, 38912
    ldwv = P.dma("sync", A(WQS, 8 * D).rearrange("p (c e) -> p c e", c=NCH),
                 wqTa.rearrange("(c p) e -> p c e", p=128), "ldw")
    ldwv = P.dma("sync", A(WKS3, 8 * D).rearrange("p (c e) -> p c e", c=NCH),
                 wkTa.rearrange("(c p) e -> p c e", p=128), "ldw")
    ldwv = P.dma("sync", A(WVS3, 8 * D).rearrange("p (c e) -> p c e", c=NCH),
                 wvTa.rearrange("(c p) e -> p c e", p=128), "ldw")
    vws = wsplit_inplace(WQS, NCH, 40960, wait0=[("ldw", ldwv)])
    vws = wsplit_inplace(WKS3, NCH, 40960)
    vws = wsplit_inplace(WVS3, NCH, 40960)
    LNF3H, LNF3L = 40960, 41984
    s3_xfree = [0, 0]
    s3_sfree = [0]
    for it in range(NT2):
        par = it % 2
        r2o = R2I0 if par == 0 else R2I1
        lno = LNT30 if par == 0 else LNT31
        ldx = "ldxa" if par == 0 else "ldxb"
        P.dma("sync", A(r2o, 2 * D).rearrange("p (r d) -> p r d", r=2),
              r2_scr.rearrange("(r p) d -> p r d", p=128)[
                  :, it * 2:it * 2 + 2, :],
              ldx, wait=[("se", s3_xfree[par])])
        lw = [(ldx, P.at(ldx))]
        se_ln = 0
        for r in range(2):
            se_ln = ln_chunk(A(r2o + r * D, D), A(lno + r * D, D), r,
                             ldwait=lw if r == 0 else ())
        s3_xfree[par] = se_ln
        tr_tile(lno, LNF3, 2, T2, wait=[("se", se_ln), ("sta", s3_sfree[0])])
        vtr = P.at("vevac")
        vls = bfsplit(LNF3, NCH * T2, LNF3H, LNF3L, wait=[("vevac", vtr)])
        for dst, ws in ((QFO, WQS), (KFO, WKS3)):
            for m in range(NCH):
                bank = (m % 2) + (0 if ws == WQS else 2)
                mms = []
                for k in range(NCH):
                    wh = bfap(ws + k * D, 1024)[:, m * 128:m * 128 + 128]
                    wl = bfap(ws + k * D + 512, 1024)[:, m * 128:m * 128 + 128]
                    ah = bfap(LNF3H, NCH * T2)[:, k * T2:(k + 1) * T2]
                    al = bfap(LNF3L, NCH * T2)[:, k * T2:(k + 1) * T2]
                    mms += [(wh, ah), (wh, al), (wl, ah)]
                te = P.mm_group(bank, PS[bank][:, 0:T2], mms,
                                wait=[("ve", max(vls, vws))])
                ev = P.op("scalar", lambda s, dst=dst, m=m, bank=bank:
                          s.copy(A(dst + m * T2, T2), PS[bank][:, 0:T2]),
                          wait=[("te", te)], inc="sevac")
                P.release(bank, "sevac", ev)
        for ts in range(2):
            for b2i in range(2):
                bank = 4 + ((ts * 2 + b2i) % 2)
                mms = []
                for k in range(NCH):
                    ah = bfap(LNF3H, NCH * T2)[:, k * T2 + ts * 128:
                                               k * T2 + ts * 128 + 128]
                    al = bfap(LNF3L, NCH * T2)[:, k * T2 + ts * 128:
                                               k * T2 + ts * 128 + 128]
                    wh = bfap(WVS3 + k * D, 1024)[:, b2i * 512:b2i * 512 + 512]
                    wl = bfap(WVS3 + k * D + 512, 1024)[:, b2i * 512:
                                                        b2i * 512 + 512]
                    mms += [(ah, wh), (ah, wl), (al, wh)]
                te = P.mm_group(bank, PS[bank][:, :], mms,
                                wait=[("ve", max(vls, vws))])
                ev = P.op("scalar", lambda s, ts=ts, b2i=b2i, bank=bank:
                          s.copy(A(VTO + ts * D + b2i * 512, 512),
                                 PS[bank][:, :]),
                          wait=[("te", te)], inc="sevac")
                P.release(bank, "sevac", ev)
        sv = P.at("sevac")
        st = P.dma("gpsimd", q_scr.rearrange("(c p) t -> p c t", p=128)[
            :, :, it * T2:(it + 1) * T2],
            A(QFO, NCH * T2).rearrange("p (c t) -> p c t", c=NCH),
            "sta", wait=[("sevac", sv)])
        st = P.dma("gpsimd", k_scr.rearrange("(c p) t -> p c t", p=128)[
            :, :, it * T2:(it + 1) * T2],
            A(KFO, NCH * T2).rearrange("p (c t) -> p c t", c=NCH),
            "sta", wait=[])
        st = P.dma("gpsimd", v_scr.rearrange("(r p) d -> p r d", p=128)[
            :, it * 2:it * 2 + 2, :],
            A(VTO, 2 * D).rearrange("p (r d) -> p r d", r=2),
            "sta", wait=[])
        s3_sfree[0] = st
    P.barrier()

    if stop_stage < 4:
        P.emit(block, sems)
        es.close()
        return nc

    SS = 0
    QF3, KF3, KTM, VT3, QT3 = 8192, 12288, 16384, 20480, 24576
    SC3, WI3, G1B = 28672, 30720, 32768
    RT3 = 33280
    NC3 = tloc // 512
    P.dma("sync", A(WI3, 4 * 512).rearrange("p (s t) -> p s t", s=4),
          expo512.rearrange("(s p) t -> p s t", p=128), "ldz")
    P.dma("sync", A(G1B, 512), arange_rep[:, tloc - 512:tloc], "ldz")
    lz = P.at("ldz")
    sv = P.op("scalar", lambda s: s.activation(
        A(WI3, 4 * 512), A(WI3, 4 * 512), ACTF.Exp, bias=MLN32,
        scale=lng_c), wait=[("ldz", lz)], inc="se")
    sv = P.op("scalar", lambda s: s.activation(
        A(G1B, 512), A(G1B, 512), ACTF.Exp, bias=MLN32,
        scale=lng_c), inc="se")
    vv = P.op("vector", lambda v: v.memset(A(SS, 8 * D), 0.0), inc="ve")
    s3b_stgate = [0]
    if dbg:
        P.dma("gpsimd", dbg_ss[:, :], A(SS, 8 * D), "stb",
              wait=[("ve", vv)])
    s3b_free = [0, 0, 0]
    for ci in reversed(range(NC3)):
        tprev, vprev, sprev = s3b_free
        P.dma("sync", A(QF3, 8 * 512).rearrange("p (c t) -> p c t", c=NCH),
              q_scr.rearrange("(c p) t -> p c t", p=128)[
                  :, :, ci * 512:(ci + 1) * 512],
              "ldxa", wait=[("te", tprev), ("ve", vprev)])
        P.dma("sync", A(KF3, 8 * 512).rearrange("p (c t) -> p c t", c=NCH),
              k_scr.rearrange("(c p) t -> p c t", p=128)[
                  :, :, ci * 512:(ci + 1) * 512],
              "ldxb", wait=[("te", tprev), ("sevac", sprev)])
        P.dma("sync", A(VT3, 4 * D).rearrange("p (r d) -> p r d", r=4),
              v_scr.rearrange("(r p) d -> p r d", p=128)[
                  :, ci * 4:ci * 4 + 4, :],
              "ldya", wait=[("te", tprev)])
        lq, lk, lv = P.at("ldxa"), P.at("ldxb"), P.at("ldya")
        for tk in range(4):
            bank = tk % 2
            mms = [(A(KF3 + c * 512 + tk * 128, 128), A(QF3 + c * 512, 512))
                   for c in range(NCH)]
            te = P.mm_group(bank, PS[bank][:, :], mms,
                            wait=[("ldxa", lq), ("ldxb", lk), ("se", sv)])
            ev = P.op("vector", lambda v, tk=tk, bank=bank: v.tensor_mul(
                A(SC3 + tk * 512, 512), PS[bank][:, :], A(WI3 + tk * 512, 512)),
                wait=[("te", te)], inc="vevac")
            P.release(bank, "vevac", ev)
        for c in range(NCH):
            P.op("vector", lambda v, c=c: v.tensor_mul(
                A(QT3 + c * 512, 512), A(QF3 + c * 512, 512), A(G1B, 512)),
                wait=[("ldxa", lq)] if c == 0 else (), inc="ve")
        vq = P.at("ve")
        tr_last = None
        for c in range(NCH):
            for r in range(4):
                b = 2 + ((c * 4 + r) % 2)
                te = P.op("tensor", lambda t, c=c, r=r, b=b: t.transpose(
                    PS[b][:, 0:128], A(KF3 + c * 512 + r * 128, 128), IDT),
                    wait=([("ldxb", lk)] if (c == 0 and r == 0) else []) +
                    P.guard(b), inc="te")
                tr_last = P.op("scalar", lambda s, c=c, r=r, b=b: s.activation(
                    A(KTM + r * D + c * 128, 128), PS[b][:, 0:128],
                    ACTF.Identity, scale=kscale[:, r:r + 1]),
                    wait=[("te", te)], inc="sevac")
                P.release(b, "sevac", tr_last)
        ktm_done = P.at("sevac")
        scev = P.at("vevac")
        for tq in range(4):
            for b2i in range(2):
                bank = 4 + ((tq * 2 + b2i) % 2)
                mms = [(A(SC3 + tk * 512 + tq * 128, 128),
                        A(VT3 + tk * D + b2i * 512, 512)) for tk in range(4)]
                mms += [(A(QT3 + c * 512 + tq * 128, 128),
                         A(SS + c * D + b2i * 512, 512)) for c in range(NCH)]
                te = P.mm_group(bank, PS[bank][:, :], mms,
                                wait=[("vevac", scev), ("ve", vq),
                                      ("ldya", lv),
                                      ("sevac", P.at("sevac"))])
                ev = P.op("scalar", lambda s, tq=tq, b2i=b2i, bank=bank:
                          s.copy(A(RT3 + tq * D + b2i * 512, 512),
                                 PS[bank][:, :]),
                          wait=[("te", te)] +
                               ([("sta", s3b_stgate[0])]
                                if (tq == 0 and b2i == 0) else []),
                          inc="sevac")
                P.release(bank, "sevac", ev)
        st = P.dma("gpsimd", ret_scr.rearrange("(r p) d -> p r d", p=128)[
            :, ci * 4:ci * 4 + 4, :],
            A(RT3, 4 * D).rearrange("p (r d) -> p r d", r=4),
            "sta", wait=[("sevac", P.at("sevac"))])
        for m in range(NCH):
            for b2i in range(2):
                bank = 6 + ((m * 2 + b2i) % 2)
                mms = [(A(KTM + r * D + m * 128, 128),
                        A(VT3 + r * D + b2i * 512, 512)) for r in range(4)]
                te = P.mm_group(bank, PS[bank][:, :], mms,
                                wait=[("sevac", ktm_done)])
                ev = P.op("vector", lambda v, m=m, b2i=b2i, bank=bank:
                          v.scalar_tensor_tensor(
                              A(SS + m * D + b2i * 512, 512),
                              A(SS + m * D + b2i * 512, 512),
                              g512_c[:, 0:1], PS[bank][:, :],
                              ALU.mult, ALU.add),
                          wait=[("te", te)], inc="vevac")
                P.release(bank, "vevac", ev)
        s3b_free = [P.at("te"), P.at("ve"), P.at("sevac")]
        s3b_stgate[0] = st
    if dbg:
        P.dma("gpsimd", dbg_sc[:, :], A(SC3, 2048), "stb",
              wait=[("vevac", P.at("vevac")), ("te", P.at("te"))])
        P.dma("gpsimd", dbg_qt[:, :], A(QT3, 4096), "stb",
              wait=[("ve", P.at("ve"))])
        P.dma("gpsimd", dbg_wi[:, :], A(WI3, 2048), "stb",
              wait=[("se", P.at("se"))])
    st = P.dma("gpsimd", sag_in.rearrange("(c p) e -> p c e", p=128),
               A(SS, 8 * D).rearrange("p (c e) -> p c e", c=NCH),
               "stb", wait=[("vevac", P.at("vevac"))])
    ccv = P.op("gpsimd", lambda g: g.collective_compute(
        "AllGather", ALU.bypass, replica_groups=rg_pair,
        ins=[sag_in[:, :]], outs=[sag_out[:, :]]),
        wait=[("stb", st)], inc="cc")
    P.barrier()

    if stop_stage < 5:
        P.emit(block, sems)
        es.close()
        return nc

    SFO, QF4, QT4, GB, RT4 = 0, 8192, 12288, 16384, 16384 + tloc
    lz = P.dma("sync", A(SFO, 8 * D).rearrange("p (c e) -> p c e", c=NCH),
               sag_out[D:2 * D, :].rearrange("(c p) e -> p c e", p=128),
               "ldz", wait=[("cc", P.at("cc"))])
    vv = P.op("vector", lambda v: v.tensor_scalar(
        A(SFO, 8 * D), A(SFO, 8 * D), hfi_c[:, 0:1], None, ALU.mult),
        wait=[("ldz", lz)], inc="ve")
    P.dma("sync", A(GB, tloc), arange_rep[:, :], "ldz")
    sv = P.op("scalar", lambda s: s.activation(
        A(GB, tloc), A(GB, tloc), ACTF.Exp, bias=MLN32, scale=lng_c),
        wait=[("ldz", P.at("ldz"))], inc="se")
    p2free = [0, 0, 0]
    p2_stgate = [0]
    for ci in range(NC3):
        P.dma("sync", A(QF4, 8 * 512).rearrange("p (c t) -> p c t", c=NCH),
              q_scr.rearrange("(c p) t -> p c t", p=128)[
                  :, :, ci * 512:(ci + 1) * 512],
              "ldxa", wait=[("ve", p2free[0]), ("te", p2free[2])])
        lq = P.at("ldxa")
        P.dma("sync", A(RT4, 4 * D).rearrange("p (r d) -> p r d", r=4),
              ret_scr.rearrange("(r p) d -> p r d", p=128)[
                  :, ci * 4:ci * 4 + 4, :],
              "ldya", wait=[("vevac", p2free[1]), ("sta", p2_stgate[0])])
        lr = P.at("ldya")
        for c in range(NCH):
            P.op("vector", lambda v, c=c, ci=ci: v.tensor_mul(
                A(QT4 + c * 512, 512), A(QF4 + c * 512, 512),
                A(GB + ci * 512, 512)),
                wait=[("ldxa", lq), ("se", sv), ("ve", vv)] if c == 0 else (),
                inc="ve")
        vq = P.at("ve")
        p2free[0] = vq
        vlast = 0
        for tq in range(4):
            for b2i in range(2):
                bank = (tq * 2 + b2i) % 4
                mms = [(A(QT4 + c * 512 + tq * 128, 128),
                        A(SFO + c * D + b2i * 512, 512)) for c in range(NCH)]
                te = P.mm_group(bank, PS[bank][:, :], mms,
                                wait=[("ve", vq)])
                vlast = P.op("vector", lambda v, tq=tq, b2i=b2i, bank=bank:
                             v.tensor_add(A(RT4 + tq * D + b2i * 512, 512),
                                          A(RT4 + tq * D + b2i * 512, 512),
                                          PS[bank][:, :]),
                             wait=[("te", te), ("ldya", lr)] +
                                  ([("sta", p2_stgate[0])]
                                   if (tq == 0 and b2i == 0) else []),
                             inc="vevac")
                P.release(bank, "vevac", vlast)
        st = P.dma("gpsimd", ret_scr.rearrange("(r p) d -> p r d", p=128)[
            :, ci * 4:ci * 4 + 4, :],
            A(RT4, 4 * D).rearrange("p (r d) -> p r d", r=4),
            "sta", wait=[("vevac", vlast)])
        p2_stgate[0] = st
        p2free[1] = P.at("vevac")
        p2free[2] = P.at("te")
    P.barrier()

    if stop_stage < 6:
        P.emit(block, sems)
        es.close()
        return nc

    WOS = 0
    REI0, REI1 = 8192, 12288
    R2I0b, R2I1b = 16384, 20480
    AOT3, R3T = 24576, 28672
    ldwv = P.dma("sync", A(WOS, 8 * D).rearrange("p (c e) -> p c e", c=NCH),
                 woTa.rearrange("(c p) e -> p c e", p=128), "ldw")
    vwo = wsplit_inplace(WOS, NCH, 32768, wait0=[("ldw", ldwv)])
    AOTH, AOTL = 32768, 34816
    s3c_tfree = [0, 0]
    s3c_vfree = [0, 0]
    s3c_stgate = [0]
    for it in range(NT):
        par = it % 2
        reo = REI0 if par == 0 else REI1
        r2o = R2I0b if par == 0 else R2I1b
        ldx = "ldxa" if par == 0 else "ldxb"
        ldy = "ldya" if par == 0 else "ldyb"
        P.dma("sync", A(reo, 4 * D).rearrange("p (r d) -> p r d", r=4),
              ret_scr.rearrange("(r p) d -> p r d", p=128)[
                  :, it * 4:it * 4 + 4, :],
              ldx, wait=[("se", s3c_tfree[par])])
        P.dma("sync", A(r2o, 4 * D).rearrange("p (r d) -> p r d", r=4),
              r2_scr.rearrange("(r p) d -> p r d", p=128)[
                  :, it * 4:it * 4 + 4, :],
              ldy, wait=[("ve", s3c_vfree[par])])
        lr, l2 = P.at(ldx), P.at(ldy)
        tr_tile(reo, AOT3, 4, 512, wait=[(ldx, lr)])
        vtr = P.at("vevac")
        vas = bfsplit(AOT3, NCH * 512, AOTH, AOTL, wait=[("vevac", vtr)])
        for ts in range(4):
            for b2i in range(2):
                bank = 4 + ((ts * 2 + b2i) % 2)
                mms = []
                for k in range(NCH):
                    ah = bfap(AOTH, NCH * 512)[:, k * 512 + ts * 128:
                                               k * 512 + ts * 128 + 128]
                    al = bfap(AOTL, NCH * 512)[:, k * 512 + ts * 128:
                                               k * 512 + ts * 128 + 128]
                    wh = bfap(WOS + k * D, 1024)[:, b2i * 512:b2i * 512 + 512]
                    wl = bfap(WOS + k * D + 512, 1024)[:, b2i * 512:
                                                       b2i * 512 + 512]
                    mms += [(ah, wh), (ah, wl), (al, wh)]
                te = P.mm_group(bank, PS[bank][:, :], mms,
                                wait=[("ve", max(vas, vwo))])
                ev = P.op("scalar", lambda s, ts=ts, b2i=b2i, bank=bank,
                          reo=reo:
                          s.activation(A(reo + ts * D + b2i * 512, 512),
                                       PS[bank][:, :], ACTF.Identity,
                                       scale=osc_c[:, 0:1]),
                          wait=[("te", te)], inc="sevac")
                P.release(bank, "sevac", ev)
        sv = P.at("sevac")
        for ts in range(4):
            se_ln = ln_chunk(A(reo + ts * D, D), A(R3T + ts * D, D), ts,
                             ldwait=[("sevac", sv)] if ts == 0 else (),
                             wait2=[("stc", s3c_stgate[0])] if ts == 0 else ())
            P.op("vector", lambda v, ts=ts, r2o=r2o: v.tensor_add(
                A(R3T + ts * D, D), A(R3T + ts * D, D), A(r2o + ts * D, D)),
                wait=[("se", se_ln), (ldy, l2)], inc="ve")
        st = P.dma("gpsimd", r3_scr.rearrange("(r p) d -> p r d", p=128)[
            :, it * 4:it * 4 + 4, :],
            A(R3T, 4 * D).rearrange("p (r d) -> p r d", r=4),
            "stc", wait=[("ve", P.at("ve"))])
        s3c_stgate[0] = st
        s3c_tfree[par] = P.at("se")
        s3c_vfree[par] = P.at("ve")
    P.barrier()

    if stop_stage < 7:
        P.emit(block, sems)
        es.close()
        return nc

    PHS = 0
    PHH, PHL = 16384, 24576
    R3I0, R3I1 = 0, 2048
    LNT40, LNT41 = 4096, 6144
    LNF4 = 8192
    LNFH, LNFL = 10240, 11264
    DBF = 12288
    LGB, ELG, CBF = 32768, 36864, 40960
    CTB = 36864
    ldwv = P.dma("sync", A(PHS, NCH * NPTOT).rearrange(
        "p (c e) -> p c e", c=NCH),
        phiT.rearrange("(c p) e -> p c e", p=128), "ldw")
    vphi = 0
    for cc_ in range(NCH):
        vphi = bfsplit(PHS + cc_ * NPTOT, NPTOT, PHH + cc_ * D,
                       PHL + cc_ * D, wait=[("ldw", ldwv)] if cc_ == 0 else ())
    s4_xfree = [0, 0]
    s4_pfree = [0]
    for it in range(NT2):
        par = it % 2
        r3o = R3I0 if par == 0 else R3I1
        lno = LNT40 if par == 0 else LNT41
        ldx = "ldxa" if par == 0 else "ldxb"
        P.dma("sync", A(r3o, 2 * D).rearrange("p (r d) -> p r d", r=2),
              r3_scr.rearrange("(r p) d -> p r d", p=128)[
                  :, it * 2:it * 2 + 2, :],
              ldx, wait=[("se", s4_xfree[par]), ("ve", vphi)])
        lw = [(ldx, P.at(ldx))]
        se_ln = 0
        for r in range(2):
            se_ln = ln_chunk(A(r3o + r * D, D), A(lno + r * D, D), r,
                             ldwait=lw if r == 0 else ())
        s4_xfree[par] = se_ln
        tr_tile(lno, LNF4, 2, T2, wait=[("se", se_ln)])
        vtr = P.at("vevac")
        vls = bfsplit(LNF4, NCH * T2, LNFH, LNFL, wait=[("vevac", vtr)])
        for ts in range(2):
            for bq in range(4):
                bank = (ts * 4 + bq) % 4
                mms = []
                for k in range(NCH):
                    lh = bfap(LNFH, NCH * T2)[:, k * T2 + ts * 128:
                                              k * T2 + ts * 128 + 128]
                    ll = bfap(LNFL, NCH * T2)[:, k * T2 + ts * 128:
                                              k * T2 + ts * 128 + 128]
                    ph = bfap(PHH + k * D, NPTOT)[:, bq * 512:bq * 512 + 512]
                    pl = bfap(PHL + k * D, NPTOT)[:, bq * 512:bq * 512 + 512]
                    mms += [(lh, ph), (lh, pl), (ll, ph)]
                mms.append((onesrow[0:1, 0:128],
                            phibrow[0:1, bq * 512:bq * 512 + 512]))
                te = P.mm_group(bank, PS[bank][:, :], mms,
                                wait=[("ve", vls), ("sta", s4_pfree[0])])
                ev = P.op("scalar", lambda s, ts=ts, bq=bq, bank=bank:
                          s.copy(A(LGB + ts * NPTOT + bq * 512, 512),
                                 PS[bank][:, :]),
                          wait=[("te", te)], inc="sevac")
                P.release(bank, "sevac", ev)
        sv = P.at("sevac")
        for r in range(2):
            lg = A(LGB + r * NPTOT, NPTOT)
            el = A(ELG + r * NPTOT, NPTOT)
            db = A(DBF + r * NPTOT, NPTOT)
            cb = A(CBF + r * NPTOT, NPTOT)
            sv = P.op("scalar", lambda s, lg=lg, el=el: s.activation(
                el, lg, ACTF.Exp),
                wait=([("sevac", sv), ("sta", s4_pfree[0])]
                      if r == 0 else ()), inc="se")
            vv = P.op("vector", lambda v, el=el: v.reduce_sum(
                SD64, el.rearrange("p (g s) -> p g s", s=32), axis=AX.X),
                wait=[("se", sv)], inc="ve")
            vv = P.op("vector", lambda v: v.reciprocal(SD64, SD64),
                      wait=[("ve", P.at("ve"))], inc="ve")
            vv = P.op("vector", lambda v, el=el, db=db: v.tensor_tensor(
                db.rearrange("p (g s) -> p g s", s=32),
                el.rearrange("p (g s) -> p g s", s=32),
                SD64.unsqueeze(2).broadcast_to([128, 64, 32]), ALU.mult),
                wait=[("ve", P.at("ve"))] +
                     ([("sta", s4_pfree[0])] if r == 0 else []), inc="ve")
            vv = P.op("vector", lambda v, el=el: v.reduce_sum(
                SD32, el.rearrange("p (g s) -> p s g", s=32), axis=AX.X),
                inc="ve")
            vv = P.op("vector", lambda v: v.reciprocal(SD32, SD32),
                      wait=[("ve", P.at("ve"))], inc="ve")
            vv = P.op("vector", lambda v, el=el, cb=cb: v.tensor_tensor(
                cb.rearrange("p (g s) -> p s g", s=32),
                el.rearrange("p (g s) -> p s g", s=32),
                SD32.unsqueeze(2).broadcast_to([128, 32, 64]), ALU.mult),
                wait=[("ve", P.at("ve"))], inc="ve")
        vdc = P.at("ve")
        st = P.dma("gpsimd", d_scr.rearrange("(r p) e -> p r e", p=128)[
            :, it * 2:it * 2 + 2, :],
            A(DBF, 2 * NPTOT).rearrange("p (r e) -> p r e", r=2),
            "sta", wait=[("ve", vdc)])
        for npc in range(NPC):
            for r in range(2):
                b = 4 + ((npc * 2 + r) % 2)
                te = P.op("tensor", lambda t, npc=npc, r=r, b=b: t.transpose(
                    PS[b][:, 0:128], A(CBF + r * NPTOT + npc * 128, 128), IDT),
                    wait=([("ve", vdc)] if (npc == 0 and r == 0) else []) +
                    P.guard(b), inc="te")
                ev = P.op("vector", lambda v, npc=npc, r=r, b=b: v.tensor_copy(
                    A(CTB + npc * T2 + r * 128, 128), PS[b][:, 0:128]),
                    wait=[("te", te)], inc="vevac")
                P.release(b, "vevac", ev)
        st = P.dma("gpsimd", c_scr.rearrange("(c p) t -> p c t", p=128)[
            :, :, it * T2:(it + 1) * T2],
            A(CTB, NPC * T2).rearrange("p (c t) -> p c t", c=NPC),
            "sta", wait=[("vevac", P.at("vevac"))])
        s4_pfree[0] = st
    P.barrier()

    if stop_stage < 8:
        P.emit(block, sems)
        es.close()
        return nc

    XP = 0
    DI0, DI1 = 16384, 20480
    DH0, DH1, DL0, DL1 = 24576, 26624, 28672, 30720
    R4I0, R4I1 = 32768, 34816
    LNT50, LNT51 = 36864, 38912
    LNH0, LNH1, LNL0, LNL1 = 40960, 41984, 43008, 44032
    vxp = P.op("vector", lambda v: v.memset(A(XP, 16384), 0.0), inc="ve")
    s4b_tfree = [0, 0]
    s4b_sfree = [0, 0]
    for it in range(NT2):
        par = it % 2
        dio = DI0 if par == 0 else DI1
        r3o = R4I0 if par == 0 else R4I1
        lno = LNT50 if par == 0 else LNT51
        ldx = "ldxa" if par == 0 else "ldxb"
        ldy = "ldya" if par == 0 else "ldyb"
        P.dma("sync", A(dio, 2 * NPTOT).rearrange("p (r e) -> p r e", r=2),
              d_scr.rearrange("(r p) e -> p r e", p=128)[
                  :, it * 2:it * 2 + 2, :],
              ldx, wait=[("te", s4b_tfree[par])])
        P.dma("sync", A(r3o, 2 * D).rearrange("p (r d) -> p r d", r=2),
              r3_scr.rearrange("(r p) d -> p r d", p=128)[
                  :, it * 2:it * 2 + 2, :],
              ldy, wait=[("se", s4b_sfree[par])])
        ld_, lr_ = P.at(ldx), P.at(ldy)
        se_ln = 0
        for r in range(2):
            se_ln = ln_chunk(A(r3o + r * D, D), A(lno + r * D, D), r,
                             ldwait=[(ldy, lr_)] if r == 0 else ())
        s4b_sfree[par] = se_ln
        dh = DH0 if par == 0 else DH1
        dl = DL0 if par == 0 else DL1
        lnh = LNH0 if par == 0 else LNH1
        lnl = LNL0 if par == 0 else LNL1
        vds = bfsplit(dio, 2 * NPTOT, dh, dl, wait=[(ldx, ld_)])
        vds = bfsplit(lno, 2 * D, lnh, lnl, wait=[("se", se_ln)])
        for npc in range(NPC):
            for b2i in range(2):
                bank = (npc * 2 + b2i) % 6
                mms = []
                for r in range(2):
                    ah = bfap(dh, 2 * NPTOT)[:, r * NPTOT + npc * 128:
                                             r * NPTOT + npc * 128 + 128]
                    al = bfap(dl, 2 * NPTOT)[:, r * NPTOT + npc * 128:
                                             r * NPTOT + npc * 128 + 128]
                    bh = bfap(lnh, 2 * D)[:, r * D + b2i * 512:
                                          r * D + b2i * 512 + 512]
                    bl = bfap(lnl, 2 * D)[:, r * D + b2i * 512:
                                          r * D + b2i * 512 + 512]
                    mms += [(ah, bh), (ah, bl), (al, bh)]
                te = P.mm_group(bank, PS[bank][:, :], mms,
                                wait=[("ve", vds), ("ve", vxp)])
                ev = P.op("vector", lambda v, npc=npc, b2i=b2i, bank=bank:
                          v.tensor_add(A(XP + npc * D + b2i * 512, 512),
                                       A(XP + npc * D + b2i * 512, 512),
                                       PS[bank][:, :]),
                          wait=[("te", te)], inc="vevac")
                P.release(bank, "vevac", ev)
        s4b_tfree[par] = P.at("te")
    st = P.dma("gpsimd", xt_in.rearrange("(c p) e -> p c e", p=128),
               A(XP, NPC * D).rearrange("p (c e) -> p c e", c=NPC),
               "stb", wait=[("vevac", P.at("vevac"))])
    ccv = P.op("gpsimd", lambda g: g.collective_compute(
        "ReduceScatter", ALU.add, replica_groups=rg_all,
        ins=[xt_in[:, :]], outs=[xt_out[:, :]]),
        wait=[("stb", st)], inc="cc")
    P.barrier()

    if stop_stage < 9:
        P.emit(block, sems)
        es.close()
        return nc

    XS = 0
    LNB = XS + NPLC * D
    XTF = LNB + NPLC * D
    EW0 = XTF + NCH * NPL
    EW1 = EW0 + 8192
    YTL = EW1 + 8192
    EBR = SM[0:1, 1024:2048]
    EBA = YTL + NPLC * D
    lz = P.dma("sync", A(XS, NPLC * D).rearrange("p (c d) -> p c d", c=NPLC),
               xt_out.rearrange("(c p) d -> p c d", p=128),
               "ldz", wait=[("cc", P.at("cc"))])
    P.dma("sync", A(EBA, NEL * D, 0, 1), eb[:, :], "ldz")
    lz = P.at("ldz")
    vv = P.op("vector", lambda v: v.tensor_scalar(
        A(XS, NPLC * D), A(XS, NPLC * D), 65000.0, -33000.0, ALU.min, ALU.max),
        wait=[("ldz", lz)], inc="ve")
    sv = 0
    for r in range(NPLC):
        sv = ln_chunk(A(XS + r * D, D), A(LNB + r * D, D), r,
                      ldwait=[("ve", vv)] if r == 0 else ())
    for r in range(NPLC):
        sv = P.op("scalar", lambda s, r=r: s.activation(
            A(XS + r * D, D), A(LNB + r * D, D), ACTF.Tanh), inc="se")
    tr_tile(XS, XTF, NPLC, NPL, wait=[("se", sv)])
    vtr = P.at("vevac")
    ew_sem = ["ldxa", "ldxb"]
    ew_free = [0, 0]
    for e in range(NEL):
        par = e % 2
        ewo = EW0 if par == 0 else EW1
        P.dma("sync", A(ewo, 8 * D).rearrange("p (c h) -> p c h", c=NCH),
              ew[e, :, :].rearrange("(c p) h -> p c h", p=128),
              ew_sem[par], wait=[("te", ew_free[par])])
        le = P.at(ew_sem[par])
        for b2i in range(2):
            bank = (e * 2 + b2i) % 4
            mms = [(A(XTF + k * NPL + e * 32, 32, 0, 128),
                    A(ewo + k * D + b2i * 512, 512)) for k in range(NCH)]
            mms.append((onesrow[0:1, 0:32],
                        A(EBA + e * D + b2i * 512, 512, 0, 1)))
            te = P.mm_group(bank, PS[bank][0:32, :], mms,
                            wait=[("vevac", vtr), (ew_sem[par], le),
                                  ("ldz", lz)])
            ev = P.op("vector", lambda v, e=e, b2i=b2i, bank=bank:
                      v.tensor_copy(A(YTL + e * D + b2i * 512, 512, 0, 32),
                                    PS[bank][0:32, :]),
                      wait=[("te", te)], inc="vevac")
            P.release(bank, "vevac", ev)
        ew_free[par] = P.at("te")
    st = P.dma("gpsimd", yt_in.rearrange("(n p) d -> p n d", p=32),
               A(YTL, NEL * D, 0, 32).rearrange("p (n d) -> p n d", n=NEL),
               "stb", wait=[("vevac", P.at("vevac"))])
    ccv = P.op("gpsimd", lambda g: g.collective_compute(
        "AllGather", ALU.bypass, replica_groups=rg_all,
        ins=[yt_in[:, :]], outs=[yt_out[:, :]]),
        wait=[("stb", st)], inc="cc")
    P.barrier()

    if stop_stage < 10:
        P.emit(block, sems)
        es.close()
        return nc

    YTF = 0
    YTH, YTL5 = 16384, 24576
    CFB = 32768
    CFH, CFL = 0, 4096
    R5I0, R5I1 = 12288, 12288
    YOB, OTB = 8192, 40960
    lz = P.dma("sync", A(YTF, NPC * D).rearrange("p (c d) -> p c d", c=NPC),
               yt_out.rearrange("(c p) d -> p c d", p=128),
               "ldz", wait=[("cc", P.at("cc"))])
    lyt = P.at("ldz")
    vsp = 0
    for cc_ in range(NPC):
        vsp = bfsplit(YTF + cc_ * D, D, YTH + cc_ * 512, YTL5 + cc_ * 512,
                      wait=[("ldz", lyt)] if cc_ == 0 else ())
    s5_free = [0, 0]
    s5_cfree = [0]
    s5_vfree = [0]
    s5_stgate = [0]
    for it in range(NT):
        par = it % 2
        r3o = R5I0 if par == 0 else R5I1
        ldx = "ldxa" if par == 0 else "ldxb"
        P.dma("sync", A(CFB, NPC * 512).rearrange("p (c t) -> p c t", c=NPC),
              c_scr.rearrange("(c p) t -> p c t", p=128)[
                  :, :, it * 512:(it + 1) * 512],
              "ldya", wait=[("ve", max(s5_vfree[0], vsp)),
                            ("te", s5_cfree[0])])
        lc = P.at("ldya")
        P.dma("sync", A(r3o, 4 * D).rearrange("p (r d) -> p r d", r=4),
              r3_scr.rearrange("(r p) d -> p r d", p=128)[
                  :, it * 4:it * 4 + 4, :],
              ldx, wait=[("ve", max(s5_free[0], vsp))])
        l3 = P.at(ldx)
        vps = 0
        for cc_ in range(NPC):
            vps = bfsplit(CFB + cc_ * 512, 512, CFH + cc_ * 256,
                          CFL + cc_ * 256,
                          wait=[("ldya", lc)] if cc_ == 0 else ())
        for ts in range(4):
            for b2i in range(2):
                bank = (ts * 2 + b2i) % 4
                mms = []
                for npc in range(NPC):
                    ch = bfap(CFH + npc * 256, 512)[:, ts * 128:ts * 128 + 128]
                    cl = bfap(CFL + npc * 256, 512)[:, ts * 128:ts * 128 + 128]
                    yh = bfap(YTH + npc * 512, D)[:, b2i * 512:b2i * 512 + 512]
                    yl = bfap(YTL5 + npc * 512, D)[:, b2i * 512:b2i * 512 + 512]
                    mms += [(ch, yh), (ch, yl), (cl, yh)]
                te = P.mm_group(bank, PS[bank][:, :], mms,
                                wait=[("ve", vps)])
                ev = P.op("scalar", lambda s, ts=ts, b2i=b2i, bank=bank:
                          s.copy(A(YOB + ts * D + b2i * 512, 512),
                                 PS[bank][:, :]),
                          wait=[("te", te)], inc="sevac")
                P.release(bank, "sevac", ev)
        s5_cfree[0] = P.at("te")
        s5_vfree[0] = P.at("ve")
        sv = P.at("sevac")
        for ts in range(4):
            se_ln = ln_chunk(A(YOB + ts * D, D), A(OTB + ts * D, D), ts,
                             ldwait=[("sevac", sv)] if ts == 0 else (),
                             wait2=[("sto", s5_stgate[0])] if ts == 0 else ())
            P.op("vector", lambda v, ts=ts, r3o=r3o: v.tensor_add(
                A(OTB + ts * D, D), A(OTB + ts * D, D), A(r3o + ts * D, D)),
                wait=[("se", se_ln), (ldx, l3)], inc="ve")
        st = P.dma("gpsimd", out.rearrange("(r p) d -> p r d", p=128)[
            :, it * 4:it * 4 + 4, :],
            A(OTB, 4 * D).rearrange("p (r d) -> p r d", r=4),
            "sto", wait=[("ve", P.at("ve"))])
        s5_free[0] = P.at("ve")
        s5_stgate[0] = st
    P.op("gpsimd", lambda g: g.nop(), wait=[("sto", P.at("sto"))])

    P.emit(block, sems)
    es.close()
    return nc


def _mk_inputs(core, ncores, tloc, ins):
    b, h = core // 2, core % 2
    nbat = 4 if ncores == 8 else 1
    x = np.asarray(ins["L0_attn_input_data"])
    T = x.shape[1]
    xs = x[b, h * tloc:(h + 1) * tloc]
    halo = (np.zeros((HALO, D), np.float32) if h == 0
            else x[b, tloc - HALO:tloc])
    NEL = 64 // ncores
    expo = np.arange(512)[:, None] - np.arange(512)[None, :] - 1.0
    expo = np.where(expo >= 0, expo, 1e38).astype(np.float32)
    ecol = (128.0 * np.arange(4)[None, :]
            + np.arange(128)[:, None]).astype(np.float32)
    sel_own = np.zeros((1, ncores), np.float32)
    sel_own[0, core] = 1.0
    sel_pair = np.zeros((1, ncores), np.float32)
    sel_pair[0, 2 * b] = 1.0
    sel_pair[0, 2 * b + 1] = 1.0
    return {
        "x": np.ascontiguousarray(
            np.concatenate([halo, xs], 0), np.float32),
        "w1": np.ascontiguousarray(ins["L0_attn_W1"], np.float32),
        "b1": np.ascontiguousarray(ins["L0_attn_bias1"], np.float32),
        "w2": np.ascontiguousarray(ins["L0_attn_W2"], np.float32),
        "b2": np.ascontiguousarray(
            ins["L0_attn_bias2"], np.float32).reshape(1, D),
        "mw": np.ascontiguousarray(ins["L0_attn_memory_weights"], np.float32),
        "wrT": np.ascontiguousarray(
            np.asarray(ins["L0_ffn_W_r"], np.float32).T),
        "wkT": np.ascontiguousarray(
            np.asarray(ins["L0_ffn_W_k"], np.float32).T),
        "wvT": np.ascontiguousarray(
            np.asarray(ins["L0_ffn_W_v"], np.float32).T),
        "mu_r": np.ascontiguousarray(ins["L0_ffn_mu_r"], np.float32),
        "mu_k": np.ascontiguousarray(ins["L0_ffn_mu_k"], np.float32),
        "wqTa": np.ascontiguousarray(
            np.asarray(ins["L1_attn_Wq"], np.float32).T),
        "wkTa": np.ascontiguousarray(
            np.asarray(ins["L1_attn_Wk"], np.float32).T),
        "wvTa": np.ascontiguousarray(
            np.asarray(ins["L1_attn_Wv"], np.float32).T),
        "woTa": np.ascontiguousarray(
            np.asarray(ins["L1_attn_Wo"], np.float32).T),
        "dlogit": np.asarray(ins["L1_attn_decay_logit"],
                             np.float32).reshape(1),
        "oscale": np.asarray(ins["L1_attn_out_scale"],
                             np.float32).reshape(1),
        "phiT": np.ascontiguousarray(
            np.asarray(ins["L1_ffn_phi_w"], np.float32).T),
        "phib": np.ascontiguousarray(
            ins["L1_ffn_phi_b"], np.float32).reshape(1, NPTOT),
        "ew": np.ascontiguousarray(
            np.asarray(ins["L1_ffn_expert_w"],
                       np.float32)[core * NEL:(core + 1) * NEL]),
        "eb": np.ascontiguousarray(
            np.asarray(ins["L1_ffn_expert_b"],
                       np.float32)[core * NEL:(core + 1) * NEL]
        ).reshape(1, NEL * D),
        "ident": np.eye(128, dtype=np.float32),
        "expo512": expo,
        "arange_rep": np.ascontiguousarray(
            np.broadcast_to(np.arange(tloc - 1, -1, -1, dtype=np.float32),
                            (128, tloc))),
        "ecol": ecol,
        "hmask": (np.ones((1, HALO), np.float32) if h == 1
                  else np.zeros((1, HALO), np.float32)),
        "hflag": np.asarray([1.0 if h == 1 else 0.0], np.float32),
        "selpair": sel_pair,
        "selown": sel_own,
    }


def kernel(**inputs):
    ncores = 8
    tloc = 4096
    nc = build_nc(ncores=ncores, tloc=tloc)
    in_maps = [_mk_inputs(c, ncores, tloc, inputs) for c in range(ncores)]
    res = run_bass_kernel_spmd(nc, in_maps, core_ids=list(range(ncores)))
    shards = [res.results[c]["out"] for c in range(ncores)]
    full = np.zeros((4, 2 * tloc, D), np.float32)
    for c in range(ncores):
        b, h = c // 2, c % 2
        full[b, h * tloc:(h + 1) * tloc] = shards[c]
    return full
